# version 03
# baseline (speedup 1.0000x reference)
import sys

import numpy as np
import ml_dtypes

sys.path.insert(0, "/opt/trn_rl_repo")
import concourse.bass as bass
import concourse.mybir as mybir
from concourse.bass_utils import run_bass_kernel_spmd

F32 = mybir.dt.float32
BF16 = mybir.dt.bfloat16
I32 = mybir.dt.int32
AF = mybir.ActivationFunctionType
ALU = mybir.AluOpType
BF = ml_dtypes.bfloat16

B, T, C = 4, 2048, 4096
DEBUG_TAPS = False
H, HD = 32, 128
OWN = T // 2
NB = T // 128
EPS = 1e-5
EXP_BIAS = -10.0
NEG = -1.0e9
RS128 = 1.0 / float(np.sqrt(128.0))


class Ctr:

    def __init__(self, sem):
        self.sem = sem
        self.n = 0


class Prog:

    def __init__(self):
        self.ops = {k: [] for k in ("sync", "gpsimd", "vector", "scalar", "tensor")}

    def add(self, eng, fn, waits=(), incs=()):
        for c, v in waits:
            assert v <= c.n, f"wait on future value {v} > {c.n} ({eng})"
        post = []
        for c, v in incs:
            c.n += v
            post.append((c.sem, v))
        self.ops[eng].append((list(waits), fn, post))

    def emit(self, block):
        def mk(name):
            def inner(eng):
                for waits, fn, incs in self.ops[name]:
                    for c, v in waits:
                        eng.wait_ge(c.sem, v)
                    ins = fn(eng)
                    for sem, v in incs:
                        ins.then_inc(sem, v)
            return inner

        block.sync(mk("sync"))
        block.gpsimd(mk("gpsimd"))
        block.vector(mk("vector"))
        block.scalar(mk("scalar"))
        block.tensor(mk("tensor"))


def build():
    import contextlib

    nc = bass.Bass("TRN2", target_bir_lowering=False)

    x_in = nc.dram_tensor("x", [T, C], F32, kind="ExternalInput")
    wqT = nc.dram_tensor("wqT", [C, C], F32, kind="ExternalInput")
    wkT = nc.dram_tensor("wkT", [C, C], F32, kind="ExternalInput")
    wvT = nc.dram_tensor("wvT", [C, C], F32, kind="ExternalInput")
    wpT = nc.dram_tensor("wpT", [C, C], F32, kind="ExternalInput")
    awT = nc.dram_tensor("awT", [C, 10], F32, kind="ExternalInput")
    gat = nc.dram_tensor("gat", [1, 1], F32, kind="ExternalInput")
    qw = nc.dram_tensor("qw", [C, C], I32, kind="ExternalInput")
    scs = nc.dram_tensor("scs", [128, 1024], F32, kind="ExternalInput")
    zrs = nc.dram_tensor("zrs", [128, 1024], F32, kind="ExternalInput")
    brow = nc.dram_tensor("brow", [1, C], F32, kind="ExternalInput")
    cosf = nc.dram_tensor("cosf", [HD, T], BF16, kind="ExternalInput")
    ssinf = nc.dram_tensor("ssinf", [HD, T], BF16, kind="ExternalInput")
    p64 = nc.dram_tensor("p64", [HD, HD], BF16, kind="ExternalInput")
    dmask = nc.dram_tensor("dmask", [128, 128], BF16, kind="ExternalInput")
    xmask = nc.dram_tensor("xmask", [128, 128], BF16, kind="ExternalInput")

    out = nc.dram_tensor("out", [OWN, C], F32, kind="ExternalOutput")

    KI = "ExternalOutput" if DEBUG_TAPS else "Internal"
    xn1_d = nc.dram_tensor("xn1_d", [T, C], BF16, kind=KI)
    kf_d = nc.dram_tensor("kf_d", [C, T], BF16, kind=KI)
    qf_d = nc.dram_tensor("qf_d", [C, OWN], BF16, kind=KI)
    v_d = nc.dram_tensor("v_d", [T, C], BF16, kind=KI)
    wnat_d = nc.dram_tensor("wnat_d", [C, C], BF16, kind=KI)
    attn_d = nc.dram_tensor("attn_d", [OWN, C], F32, kind=KI)
    x2_d = nc.dram_tensor("x2_d", [OWN, C], F32, kind=KI)
    xn2_d = nc.dram_tensor("xn2_d", [OWN, C], BF16, kind=KI)
    ffn_d = nc.dram_tensor("ffn_d", [OWN, C], F32, kind=KI)
    y_dbg = nc.dram_tensor("y_dbg", [C, OWN], BF16, kind=KI) if DEBUG_TAPS else None
    sums_dbg = nc.dram_tensor("sums_dbg", [H, 2048], F32, kind=KI) if DEBUG_TAPS else None
    av_dbg = nc.dram_tensor("av_dbg", [10, C], BF16, kind=KI) if DEBUG_TAPS else None
    ak_dbg = nc.dram_tensor("ak_dbg", [128, 32 * 10], BF16, kind=KI) if DEBUG_TAPS else None

    with contextlib.ExitStack() as ctx:
        def sbt(name, shape, dt):
            return ctx.enter_context(nc.sbuf_tensor(name, shape, dt))

        def pst(name, shape, dt=F32):
            return ctx.enter_context(nc.psum_tensor(name, shape, dt))

        BIGA = sbt("BIGA", [128, 32, 1024], BF16)
        B0 = sbt("B0", [128, 32, 512], BF16)
        B1 = sbt("B1", [128, 32, 512], BF16)
        CC = sbt("CC", [128, 8192], F32)
        DD = sbt("DD", [128, 4096], BF16)
        cos_sb = sbt("cos_sb", [HD, 1024], BF16)
        ssin_sb = sbt("ssin_sb", [HD, 1024], BF16)
        p64_sb = sbt("p64_sb", [HD, HD], BF16)
        sc_sb = sbt("sc_sb", [128, 1024], F32)
        zr_sb = sbt("zr_sb", [128, 1024], F32)
        dm_sb = sbt("dm_sb", [128, 128], BF16)
        xm_sb = sbt("xm_sb", [128, 128], BF16)
        ak_sb = sbt("ak_sb", [128, 32, 10], BF16)
        av_sb = sbt("av_sb", [10, C], BF16)
        pf_sb = sbt("pf_sb", [128, 32, 10], BF16)
        g_sb = sbt("g_sb", [1, 1], F32)
        ones128 = sbt("ones128", [128, 1], BF16)
        ones10 = sbt("ones10", [10, 1], BF16)
        o110 = sbt("o110", [1, 10], BF16)
        o1128 = sbt("o1128", [1, 128], BF16)
        srow = sbt("srow", [1, 1024], F32)
        sarow = sbt("sarow", [1, 1024], BF16)
        stat = sbt("stat", [128, 16], F32)
        cst_sb = sbt("cst_sb", [128, 2], F32)
        k0_sb = sbt("k0_sb", [128, 1024], BF16)
        t1_sb = sbt("t1_sb", [128, 1024], BF16)
        k1_sb = sbt("k1_sb", [128, 1024], BF16)
        ape_sb = sbt("ape_sb", [128, 2, 512], F32)
        ve_sb = ape_sb.ap().bitcast(BF16)[:, :, 0:512]

        b1f = B1.ap().rearrange("p a b -> p (a b)")
        kf2 = [b1f[:, 0:2048], b1f[:, 2048:4096]]
        qf2 = [b1f[:, 4096:5120], b1f[:, 5120:6144]]
        v2 = [b1f[:, 6144:8192].rearrange("p (kc d) -> p kc d", d=128),
              b1f[:, 8192:10240].rearrange("p (kc d) -> p kc d", d=128)]
        exps = [b1f[:, 10240:11264], b1f[:, 11264:12288]]
        invb = b1f[:, 12288:13312]
        pa_sb = b1f[0:10, 13312:14336]
        expa = b1f[0:10, 14336:15360]
        invrow = b1f[0:1, 15360:16384]

        ccb = CC.ap().bitcast(BF16)
        xt_sl = [ccb[:, 0:4096], ccb[:, 4096:8192]]
        nrm_sl = [ccb[:, 8192:12288], ccb[:, 12288:16384]]
        qw_sl = [ccb[:, 0:4096], ccb[:, 4096:8192]]
        wn_sl = [ccb[:, 8192:12288], ccb[:, 12288:16384]]
        at_ap = CC.ap()[:, 0:4096]
        xr_ap = ccb[:, 8192:12288]
        p4dum = ccb[:, 12288:16384]
        ft_ap = CC.ap()[:, 0:4096]
        x2r_ap = CC.ap()[:, 4096:8192]

        psA = pst("psA", [128, 1024])
        psB = pst("psB", [128, 1024])
        psC = pst("psC", [128, 1024])
        psD = pst("psD", [128, 1024])
        psAB = [psA, psB]

        block = ctx.enter_context(nc.Block())
        p = Prog()
        sems = [ctx.enter_context(nc.semaphore(f"S{i}")) for i in range(8)]
        CsyL = Ctr(sems[0])
        CgpL = Ctr(sems[1])
        Cst = Ctr(sems[2])
        Cpe = Ctr(sems[3])
        Cact = Ctr(sems[4])
        Cdve = Ctr(sems[5])
        Cpool = Ctr(sems[6])
        Cout = Ctr(sems[7])

        p.add("vector", lambda e: e.memset(ones128[:, :], 1.0))
        p.add("vector", lambda e: e.memset(ones10[:, :], 1.0))
        p.add("vector", lambda e: e.memset(o110[:, :], 1.0))
        p.add("vector", lambda e: e.memset(o1128[:, :], 1.0))
        p.add("vector", lambda e: e.memset(cst_sb[:, 0:1], EPS))
        p.add("vector", lambda e: e.memset(cst_sb[:, 1:2], EXP_BIAS), incs=[(Cdve, 1)])
        ones_ready = Cdve.n
        for dst, src in ((p64_sb, p64), (dm_sb, dmask), (xm_sb, xmask),
                         (g_sb, gat)):
            nd = len(dst.shape)
            p.add("sync", lambda e, d=dst, s=src, nd=nd: e.dma_start(
                out=d[(slice(None),) * nd], in_=s[(slice(None),) * nd]),
                incs=[(CsyL, 16)])
        consts_ready = CsyL.n
        p.add("gpsimd", lambda e: e.dma_start(
            out=pf_sb[:, :, :], in_=awT[:, :].rearrange("(c p) j -> p c j", p=128)),
            incs=[(CgpL, 16)])
        p.add("gpsimd", lambda e: e.dma_start(out=sc_sb[:, :], in_=scs[:, :]),
              incs=[(CgpL, 16)])
        p.add("gpsimd", lambda e: e.dma_start(out=zr_sb[:, :], in_=zrs[:, :]),
              incs=[(CgpL, 16)])
        sczr_ready = CgpL.n
        pf_ready = CgpL.n

        def emit_ln_stats(src_ap, sc0, dummy_ap, pre_waits):
            p.add("vector", lambda e: e.reduce_sum(
                stat[:, sc0 + 5:sc0 + 6], src_ap, mybir.AxisListType.X),
                waits=pre_waits)
            p.add("vector", lambda e: e.scalar_tensor_tensor(
                out=dummy_ap, in0=src_ap, scalar=1.0, in1=src_ap,
                op0=ALU.mult, op1=ALU.mult, accum_out=stat[:, sc0 + 1:sc0 + 2]))
            p.add("vector", lambda e: e.tensor_scalar(
                stat[:, sc0:sc0 + 1], stat[:, sc0 + 5:sc0 + 6], -1.0 / C, None, ALU.mult))
            p.add("vector", lambda e: e.tensor_scalar(
                stat[:, sc0 + 3:sc0 + 4], stat[:, sc0 + 1:sc0 + 2], 1.0 / C, None,
                ALU.mult))
            p.add("vector", lambda e: e.tensor_tensor(
                stat[:, sc0 + 2:sc0 + 3], stat[:, sc0:sc0 + 1], stat[:, sc0:sc0 + 1],
                ALU.mult), incs=[(Cdve, 1)])
            m2m = Cdve.n
            p.add("vector", lambda e: e.tensor_tensor(
                stat[:, sc0 + 3:sc0 + 4], stat[:, sc0 + 3:sc0 + 4],
                stat[:, sc0 + 2:sc0 + 3], ALU.subtract),
                waits=[(Cdve, m2m)], incs=[(Cdve, 1)])
            dm = Cdve.n
            p.add("scalar", lambda e: e.activation(
                stat[:, sc0 + 4:sc0 + 5], stat[:, sc0 + 3:sc0 + 4], AF.Ln,
                bias=cst_sb[:, 0:1]), waits=[(Cdve, dm)], incs=[(Cact, 1)])
            lnm = Cact.n
            p.add("scalar", lambda e: e.activation(
                stat[:, sc0 + 4:sc0 + 5], stat[:, sc0 + 4:sc0 + 5], AF.Exp, scale=-0.5),
                waits=[(Cact, lnm)], incs=[(Cact, 1)])
            return stat[:, sc0:sc0 + 1], stat[:, sc0 + 4:sc0 + 5], Cact.n

        p0_norm_marks = {}
        p0_store_marks = {}
        for i in range(16):
            sl = i % 2
            w = []
            if i >= 2:
                w.append((Cdve, p0_norm_marks[i - 2]))
            p.add("gpsimd", lambda e, i=i, sl=sl: e.dma_start(
                out=xt_sl[sl], in_=x_in[128 * i:128 * (i + 1), :]),
                waits=w, incs=[(CgpL, 16)])
            ld = CgpL.n
            nm, rs, am = emit_ln_stats(xt_sl[sl], 8 * sl, DD[:, 0:C], [(CgpL, ld)])
            w = [(Cact, am)]
            if i >= 2:
                w.append((Cst, p0_store_marks[i - 2]))
            p.add("vector", lambda e, sl=sl, nm=nm, rs=rs: e.tensor_scalar(
                nrm_sl[sl], xt_sl[sl], nm, rs, ALU.add, ALU.mult),
                waits=w, incs=[(Cdve, 1)])
            p0_norm_marks[i] = Cdve.n
            p.add("sync", lambda e, i=i, sl=sl: e.dma_start(
                out=xn1_d[128 * i:128 * (i + 1), :], in_=nrm_sl[sl]),
                waits=[(Cdve, Cdve.n)], incs=[(Cst, 16)])
            p0_store_marks[i] = Cst.n
        xn1_stored = Cst.n
        p0_dve_done = Cdve.n

        kq_evac_marks = {}
        rope_prev = {"t1": 0, "pl": 0, "st": 0}
        ve_store_marks = {}

        def emit_rope_finish(kind, h, half, evac_m, tab_m):
            rot_m = Cpe.n
            prev_t1 = rope_prev["t1"]
            prev_pl = rope_prev["pl"]
            prev_st = rope_prev["st"]
            p.add("vector", lambda e: e.tensor_tensor(
                t1_sb[:, :], psC[:, :], ssin_sb[:, :], ALU.mult),
                waits=[(Cpe, rot_m), (Cpool, prev_pl), (CsyL, tab_m)], incs=[(Cdve, 1)])
            rope_prev["t1"] = Cdve.n
            p.add("gpsimd", lambda e: e.tensor_tensor(
                k1_sb[:, :], k0_sb[:, :], cos_sb[:, :], ALU.mult),
                waits=[(Cact, evac_m), (Cst, prev_st), (CsyL, tab_m)],
                incs=[(Cpool, 1)])
            t0_m = Cpool.n
            p.add("gpsimd", lambda e: e.tensor_tensor(
                k1_sb[:, :], k1_sb[:, :], t1_sb[:, :], ALU.add),
                waits=[(Cdve, rope_prev["t1"]), (Cpool, t0_m)], incs=[(Cpool, 1)])
            rope_prev["pl"] = Cpool.n
            if kind == "k":
                p.add("sync", lambda e, h=h, half=half: e.dma_start(
                    out=kf_d[128 * h:128 * (h + 1), 1024 * half:1024 * (half + 1)],
                    in_=k1_sb[:, :]),
                    waits=[(Cpool, Cpool.n)], incs=[(Cst, 16)])
            else:
                p.add("sync", lambda e, h=h: e.dma_start(
                    out=qf_d[128 * h:128 * (h + 1), 0:OWN], in_=k1_sb[:, :]),
                    waits=[(Cpool, Cpool.n)], incs=[(Cst, 16)])
            rope_prev["st"] = Cst.n

        av_evac_mark = {"m": 0}
        ak_evac_mark = {"m": 0}
        for half in range(2):
            p.add("sync", lambda e, half=half: e.dma_start(
                out=cos_sb[:, :], in_=cosf[:, 1024 * half:1024 * (half + 1)]),
                waits=[(Cpool, rope_prev["pl"])], incs=[(CsyL, 16)])
            p.add("sync", lambda e, half=half: e.dma_start(
                out=ssin_sb[:, :], in_=ssinf[:, 1024 * half:1024 * (half + 1)]),
                waits=[(Cdve, rope_prev["t1"])], incs=[(CsyL, 16)])
            ropetab_ready = CsyL.n
            if half == 0:
                xw = [(Cst, xn1_stored), (CsyL, consts_ready)]
            else:
                xw = [(Cst, xn1_stored), (Cpe, Cpe.n)]
            for cch in range(32):
                p.add("sync", lambda e, cch=cch, half=half: e.dma_start(
                    out=BIGA[:, cch, :],
                    in_=xn1_d[1024 * half:1024 * (half + 1), 128 * cch:128 * (cch + 1)],
                    transpose=True),
                    waits=(xw if cch == 0 else []), incs=[(CsyL, 16)])
            xnf_ready = CsyL.n

            ev_marks = {}
            for e8 in range(8):
                wsl = B0 if e8 % 2 == 0 else B1
                wv_w = [(CgpL, pf_ready)]
                if e8 >= 2 or half == 1:
                    wv_w.append((Cpe, Cpe.n))
                p.add("gpsimd", lambda e, e8=e8, wsl=wsl: e.dma_start(
                    out=wsl[:, :, :],
                    in_=wvT[:, 512 * e8:512 * (e8 + 1)].rearrange("(c p) n -> p c n", p=128)),
                    waits=wv_w, incs=[(CgpL, 16)])
                wv_ld = CgpL.n
                for tc in range(8):
                    ps = psAB[tc % 2]
                    ii = e8 * 8 + tc
                    for cch in range(32):
                        w = []
                        if cch == 0:
                            w = [(CgpL, wv_ld), (CsyL, xnf_ready)]
                            prev = ev_marks.get(ii - 2)
                            if prev is not None:
                                w.append((Cact, prev))
                        p.add("tensor", lambda e, ps=ps, cch=cch, tc=tc, wsl=wsl: e.matmul(
                            ps[:, 0:512], BIGA[:, cch, 128 * tc:128 * (tc + 1)],
                            wsl[:, cch, :], start=(cch == 0), stop=(cch == 31),
                            skip_group_check=True),
                            waits=w, incs=([(Cpe, 1)] if cch == 31 else []))
                        if half == 0 and tc == 0:
                            p.add("tensor", lambda e, cch=cch, wsl=wsl: e.matmul(
                                psC[0:10, 0:512], pf_sb[:, cch, :], wsl[:, cch, :],
                                start=(cch == 0), stop=(cch == 31), skip_group_check=True),
                                waits=([(Cact, av_evac_mark["m"]), (CgpL, pf_ready)]
                                       if cch == 0 else []),
                                incs=([(Cpe, 1)] if cch == 31 else []))
                    mm_m = Cpe.n
                    w = [(Cpe, mm_m)]
                    prev = ve_store_marks.get(ii - 2 + 64 * half)
                    if prev is not None:
                        w.append((Cst, prev))
                    p.add("scalar", lambda e, tc=tc, ps=ps: e.copy(
                        ve_sb[:, tc % 2, :], ps[:, 0:512]),
                        waits=w, incs=[(Cact, 1)])
                    p.add("sync", lambda e, tc=tc, e8=e8, half=half: e.dma_start(
                        out=v_d[1024 * half + 128 * tc:1024 * half + 128 * (tc + 1),
                                512 * e8:512 * (e8 + 1)],
                        in_=ve_sb[:, tc % 2, :]),
                        waits=[(Cact, Cact.n)], incs=[(Cst, 16)])
                    ev_marks[ii] = Cact.n
                    ve_store_marks[ii + 64 * half] = Cst.n
                if half == 0:
                    p.add("scalar", lambda e, e8=e8: e.copy(
                        av_sb[0:10, 512 * e8:512 * (e8 + 1)], psC[0:10, 0:512]),
                        waits=[(Cpe, Cpe.n)], incs=[(Cact, 1)])
                    av_evac_mark["m"] = Cact.n

            for kind in (("k", "q") if half == 0 else ("k",)):
                wT_src = wkT if kind == "k" else wqT
                for h in range(32):
                    wsl = (B0 if h % 2 == 0 else B1)[:, :, 0:128]
                    p.add("gpsimd", lambda e, h=h, wsl=wsl, wT_src=wT_src: e.dma_start(
                        out=wsl, in_=wT_src[:, 128 * h:128 * (h + 1)].rearrange(
                            "(c p) n -> p c n", p=128)),
                        waits=[(Cpe, Cpe.n)], incs=[(CgpL, 16)])
                    wld = CgpL.n
                    ps = psAB[h % 2]
                    for cch in range(32):
                        w = []
                        if cch == 0:
                            w = [(CgpL, wld)]
                            prev = kq_evac_marks.get((kind, half, h - 2))
                            if prev is not None:
                                w.append((Cact, prev))
                        for sg in range(2):
                            p.add("tensor", lambda e, ps=ps, cch=cch, sg=sg, wsl=wsl: e.matmul(
                                ps[:, 512 * sg:512 * (sg + 1)], wsl[:, cch, :],
                                BIGA[:, cch, 512 * sg:512 * (sg + 1)],
                                start=(cch == 0), stop=(cch == 31), skip_group_check=True),
                                waits=(w if sg == 0 else []),
                                incs=([(Cpe, 1)] if (cch == 31 and sg == 1) else []))
                        if kind == "k" and half == 0:
                            p.add("tensor", lambda e, cch=cch, wsl=wsl: e.matmul(
                                psD[:, 0:10], wsl[:, cch, :], pf_sb[:, cch, :],
                                start=(cch == 0), stop=(cch == 31), skip_group_check=True),
                                waits=([(Cdve, ak_evac_mark["m"])] if cch == 0 else []),
                                incs=([(Cpe, 1)] if cch == 31 else []))
                    mm_m = Cpe.n
                    p.add("scalar", lambda e, ps=ps: e.copy(k0_sb[:, :], ps[:, :]),
                          waits=[(Cpe, mm_m), (Cpool, rope_prev["pl"])], incs=[(Cact, 1)])
                    kq_evac_marks[(kind, half, h)] = Cact.n
                    evac_m = Cact.n
                    if kind == "k" and half == 0:
                        p.add("vector", lambda e, h=h: e.tensor_copy(
                            ak_sb[:, h, :], psD[:, 0:10]),
                            waits=[(Cpe, mm_m)], incs=[(Cdve, 1)])
                        ak_evac_mark["m"] = Cdve.n
                    p.add("tensor", lambda e: e.matmul(
                        psC[:, 0:512], p64_sb[:, :], k0_sb[:, 0:512],
                        start=True, stop=True, skip_group_check=True),
                        waits=[(Cact, evac_m), (Cdve, rope_prev["t1"])])
                    p.add("tensor", lambda e: e.matmul(
                        psC[:, 512:1024], p64_sb[:, :], k0_sb[:, 512:1024],
                        start=True, stop=True, skip_group_check=True),
                        incs=[(Cpe, 1)])
                    emit_rope_finish(kind, h, half, evac_m, ropetab_ready)
        if DEBUG_TAPS:
            p.add("sync", lambda e: e.dma_start(
                out=av_dbg[:, :], in_=av_sb[0:10, :]),
                waits=[(Cact, Cact.n)], incs=[(Cst, 16)])
            p.add("sync", lambda e: e.dma_start(
                out=ak_dbg[:, :], in_=ak_sb.ap().rearrange("p a b -> p (a b)")),
                waits=[(Cdve, Cdve.n)], incs=[(Cst, 16)])
        p1_stores = Cst.n
        p1_pe_done = Cpe.n
        p1_dve_done = Cdve.n

        head_load_marks = {}
        head_end_pe = {}
        head_free = {}
        dq_wn_store = {}

        def emit_head_loads(h):
            sl = h % 2
            w = [(Cst, p1_stores)] if h < 2 else [(Cpe, head_end_pe[h - 2])]
            p.add("sync", lambda e, h=h, sl=sl: e.dma_start(
                out=kf2[sl], in_=kf_d[128 * h:128 * (h + 1), :]),
                waits=w, incs=[(CsyL, 16)])
            p.add("sync", lambda e, h=h, sl=sl: e.dma_start(
                out=qf2[sl], in_=qf_d[128 * h:128 * (h + 1), :]), incs=[(CsyL, 16)])
            p.add("sync", lambda e, h=h, sl=sl: e.dma_start(
                out=v2[sl], in_=v_d[:, 128 * h:128 * (h + 1)].rearrange(
                    "(kc tp) d -> tp kc d", tp=128)), incs=[(CsyL, 16)])
            head_load_marks[h] = CsyL.n

        emit_head_loads(0)
        emit_head_loads(1)

        for h in range(H):
            sl = h % 2
            if h == 0:
                p.add("gpsimd", lambda e: e.dma_start(
                    out=B0[:, :, :],
                    in_=wpT[:, 0:512].rearrange("(c p) n -> p c n", p=128)),
                    waits=[(Cpe, p1_pe_done)], incs=[(CgpL, 16)])
                wp0_ld = CgpL.n

            exp_marks = {}
            pv_marks = {}

            def bank_segs(qlo):
                return [(qlo, 512), (512, 1024)] if qlo < 512 else [(qlo, 1024)]

            def emit_sum_pv(cp):
                cqp = cp % 8
                qlop = 128 * cqp
                slp = cp % 2
                sgl = bank_segs(qlop)
                for sg, (s0, s1) in enumerate(sgl):
                    w = []
                    if sg == 0:
                        w = [(Cact, exp_marks[cp])]
                        if cp == 0 and h == 0:
                            w.append((Cdve, p1_dve_done))
                    p.add("tensor", lambda e, s0=s0, s1=s1, slp=slp, cp=cp: e.matmul(
                        psD[0:1, s0:s1], ones128[:, :], exps[slp][:, s0:s1],
                        start=(cp == 0), stop=(cp == 15), skip_group_check=True),
                        waits=w)
                for sg, (s0, s1) in enumerate(sgl):
                    w = []
                    if cp == 0 and sg == 0 and h >= 1:
                        w = [(Cdve, head_free[h - 1][2])]
                    p.add("tensor", lambda e, s0=s0, s1=s1, slp=slp, cp=cp, sl=sl: e.matmul(
                        psC[:, s0:s1], v2[sl][:, cp, :], exps[slp][:, s0:s1],
                        start=(cp == 0), stop=(cp == 15), skip_group_check=True),
                        waits=w,
                        incs=([(Cpe, 1)] if sg == len(sgl) - 1 else []))
                pv_marks[cp] = Cpe.n

            for c in range(16):
                cq = c % 8
                qlo = 128 * cq
                width = 1024 - qlo
                slot = c % 2
                pss = psAB[slot]
                w = []
                if c == 0:
                    w = [(CsyL, head_load_marks[h])]
                    if h == 0:
                        w.append((Cdve, p1_dve_done))
                    else:
                        w.append((Cdve, head_free[h - 1][0]))
                elif c == 1 and h >= 1:
                    w = [(Cact, head_free[h - 1][1])]
                elif c >= 2:
                    w = [(Cact, exp_marks[c - 2])]
                sgl = bank_segs(qlo)
                for sg, (s0, s1) in enumerate(sgl):
                    p.add("tensor", lambda e, pss=pss, c=c, s0=s0, s1=s1, sl=sl: e.matmul(
                        pss[:, s0:s1], kf2[sl][:, 128 * c:128 * (c + 1)],
                        qf2[sl][:, s0:s1], start=True, stop=True, skip_group_check=True),
                        waits=(w if sg == 0 else []),
                        incs=([(Cpe, 1)] if sg == len(sgl) - 1 else []))
                sc_m = Cpe.n
                msk = dm_sb if c < 8 else xm_sb
                p.add("vector", lambda e, pss=pss, qlo=qlo, msk=msk: e.tensor_tensor(
                    pss[:, qlo:qlo + 128], pss[:, qlo:qlo + 128], msk[:, :], ALU.add),
                    waits=[(Cpe, sc_m)], incs=[(Cdve, 1)])
                mk_m = Cdve.n
                ew = [(Cdve, mk_m)]
                if c >= 2:
                    ew.append((Cpe, pv_marks[c - 2]))
                p.add("scalar", lambda e, pss=pss, qlo=qlo, width=width, slot=slot: e.activation(
                    exps[slot][:, qlo:qlo + width], pss[:, qlo:qlo + width], AF.Exp,
                    scale=RS128, bias=cst_sb[:, 1:2]),
                    waits=ew, incs=[(Cact, 1)])
                exp_marks[c] = Cact.n
                if c >= 1:
                    emit_sum_pv(c - 1)
            emit_sum_pv(15)
            pv_done = Cpe.n

            p.add("scalar", lambda e: e.copy(srow[:, :], psD[0:1, :]),
                  waits=[(Cpe, pv_done)], incs=[(Cact, 1)])
            srow_m = Cact.n
            for sg in range(2):
                p.add("tensor", lambda e, sg=sg, sl=sl, h=h: e.matmul(
                    psA[0:10, 512 * sg:512 * (sg + 1)], ak_sb[:, h, :],
                    qf2[sl][:, 512 * sg:512 * (sg + 1)], start=True, stop=True,
                    skip_group_check=True),
                    waits=([(Cact, exp_marks[14])] if sg == 0 else []),
                    incs=([(Cpe, 1)] if sg == 1 else []))
            asc_m = Cpe.n
            p.add("scalar", lambda e: e.activation(
                expa[:, :], psA[0:10, :], AF.Exp, scale=1.0 / 64.0),
                waits=[(Cpe, asc_m)], incs=[(Cact, 1)])
            expa_m = Cact.n
            for sg in range(2):
                p.add("tensor", lambda e, sg=sg: e.matmul(
                    psD[0:1, 512 * sg:512 * (sg + 1)], ones10[:, :],
                    expa[:, 512 * sg:512 * (sg + 1)], start=True, stop=True,
                    skip_group_check=True),
                    waits=([(Cact, srow_m), (Cact, expa_m)] if sg == 0 else []),
                    incs=([(Cpe, 1)] if sg == 1 else []))
            asum_m = Cpe.n
            p.add("scalar", lambda e: e.copy(sarow[:, :], psD[0:1, :]),
                  waits=[(Cpe, asum_m)], incs=[(Cact, 1)])
            sae_m = Cact.n
            p.add("scalar", lambda e: e.activation(sarow[:, :], sarow[:, :], AF.Ln),
                  waits=[(Cact, sae_m)], incs=[(Cact, 1)])
            saln_m = Cact.n
            p.add("scalar", lambda e: e.activation(sarow[:, :], sarow[:, :], AF.Exp,
                                                   scale=-1.0),
                  waits=[(Cact, saln_m)], incs=[(Cact, 1)])
            sa_m = Cact.n
            p.add("vector", lambda e: e.tensor_scalar(
                sarow[:, :], sarow[:, :], g_sb[:, :], None, ALU.mult),
                waits=[(Cact, sa_m)], incs=[(Cdve, 1)])
            f_m = Cdve.n
            for sg in range(2):
                p.add("tensor", lambda e, sg=sg: e.matmul(
                    psB[0:10, 512 * sg:512 * (sg + 1)], o110[:, :],
                    sarow[:, 512 * sg:512 * (sg + 1)], start=True, stop=True,
                    skip_group_check=True),
                    waits=([(Cact, exp_marks[15]), (Cdve, f_m)] if sg == 0 else []),
                    incs=([(Cpe, 1)] if sg == 1 else []))
            fb_m = Cpe.n
            p.add("vector", lambda e: e.tensor_tensor(
                pa_sb[:, :], expa[:, :], psB[0:10, :], ALU.mult),
                waits=[(Cpe, fb_m)], incs=[(Cdve, 1)])
            pa_m = Cdve.n
            for sg in range(2):
                p.add("tensor", lambda e, sg=sg, h=h: e.matmul(
                    psA[:, 512 * sg:512 * (sg + 1)], av_sb[0:10, 128 * h:128 * (h + 1)],
                    pa_sb[:, 512 * sg:512 * (sg + 1)], start=True, stop=True,
                    skip_group_check=True),
                    waits=([(Cdve, pa_m)] if sg == 0 else []),
                    incs=([(Cpe, 1)] if sg == 1 else []))
            av_m = Cpe.n
            p.add("scalar", lambda e: e.activation(srow[:, :], srow[:, :], AF.Ln),
                  incs=[(Cact, 1)])
            srln_m = Cact.n
            p.add("scalar", lambda e: e.activation(invrow[:, :], srow[:, :], AF.Exp,
                                                   scale=-1.0),
                  waits=[(Cact, srln_m)], incs=[(Cact, 1)])
            inv_m = Cact.n
            for sg in range(2):
                p.add("tensor", lambda e, sg=sg: e.matmul(
                    psB[:, 512 * sg:512 * (sg + 1)], o1128[:, :],
                    invrow[:, 512 * sg:512 * (sg + 1)], start=True, stop=True,
                    skip_group_check=True),
                    waits=([(Cact, inv_m), (Cdve, pa_m)] if sg == 0 else []),
                    incs=([(Cpe, 1)] if sg == 1 else []))
            invb_pe = Cpe.n
            p.add("scalar", lambda e: e.copy(invb[:, :], psB[:, :]),
                  waits=[(Cpe, invb_pe)], incs=[(Cact, 1)])
            invb_m = Cact.n
            p.add("vector", lambda e, h=h: e.tensor_tensor(
                BIGA[:, h, :], psC[:, :], invb[:, :], ALU.mult),
                waits=[(Cact, invb_m), (Cpe, pv_done)], incs=[(Cdve, 1)])
            y1_m = Cdve.n
            p.add("vector", lambda e, h=h: e.tensor_tensor(
                BIGA[:, h, :], BIGA[:, h, :], psA[:, :], ALU.add),
                waits=[(Cpe, av_m), (Cdve, y1_m)], incs=[(Cdve, 1)])
            y2_m = Cdve.n
            if DEBUG_TAPS:
                p.add("sync", lambda e, h=h: e.dma_start(
                    out=y_dbg[128 * h:128 * (h + 1), :], in_=BIGA[:, h, :]),
                    waits=[(Cdve, y2_m)], incs=[(Cst, 16)])
                p.add("sync", lambda e, h=h: e.dma_start(
                    out=sums_dbg[h:h + 1, 0:1024], in_=srow[:, :]),
                    waits=[(Cact, inv_m)], incs=[(Cst, 16)])
                p.add("gpsimd", lambda e, h=h: e.dma_start(
                    out=sums_dbg[h:h + 1, 1024:2048], in_=sarow[:, :]),
                    waits=[(Cdve, f_m)], incs=[(CgpL, 16)])
            head_free[h] = (y2_m, invb_m, y1_m)
            head_end_pe[h] = Cpe.n
            if h + 2 < H:
                emit_head_loads(h + 2)

            oc = h
            dsl = oc % 2
            w = [(CgpL, sczr_ready)]
            if oc < 2:
                w.append((Cst, xn1_stored))
                w.append((Cdve, p0_dve_done))
            prev = dq_wn_store.get(oc - 2)
            if prev is not None:
                w.append((Cst, prev))
            p.add("gpsimd", lambda e, oc=oc, dsl=dsl: e.dma_start(
                out=qw_sl[dsl], in_=qw[128 * oc:128 * (oc + 1), :]),
                waits=w, incs=[(CgpL, 16)])
            qw_ld = CgpL.n
            for g in range(32):
                eng = "vector" if g % 2 == 0 else "gpsimd"
                p.add(eng, lambda e, g=g, oc=oc, dsl=dsl: e.tensor_scalar(
                    wn_sl[dsl][:, 128 * g:128 * (g + 1)],
                    qw_sl[dsl][:, 128 * g:128 * (g + 1)],
                    sc_sb[:, 32 * oc + g:32 * oc + g + 1],
                    zr_sb[:, 32 * oc + g:32 * oc + g + 1],
                    ALU.mult, ALU.add),
                    waits=([(CgpL, qw_ld), (CgpL, sczr_ready)] if g in (0, 1) else []),
                    incs=([(Cdve, 1)] if g == 30 else ([(Cpool, 1)] if g == 31 else [])))
            dq_d, dq_p = Cdve.n, Cpool.n
            p.add("sync", lambda e, oc=oc, dsl=dsl: e.dma_start(
                out=wnat_d[128 * oc:128 * (oc + 1), :], in_=wn_sl[dsl]),
                waits=[(Cdve, dq_d), (Cpool, dq_p)], incs=[(Cst, 16)])
            dq_wn_store[oc] = Cst.n
        wnat_all = Cst.n
        p2_done_pe = Cpe.n
        p2_done_act = Cact.n
        p2_done_dve = Cdve.n

        proj_mm_marks = {}
        ape_store_marks = {}
        ev_prev = {}
        for e8 in range(8):
            wsl = B0 if e8 % 2 == 0 else B1
            if e8 > 0:
                w = ([(Cpe, p2_done_pe), (Cact, p2_done_act), (Cdve, p2_done_dve)]
                     if e8 == 1 else [(Cpe, proj_mm_marks[e8 - 2])])
                p.add("gpsimd", lambda e, e8=e8, wsl=wsl: e.dma_start(
                    out=wsl[:, :, :],
                    in_=wpT[:, 512 * e8:512 * (e8 + 1)].rearrange("(c p) n -> p c n", p=128)),
                    waits=w, incs=[(CgpL, 16)])
                wld = CgpL.n
            else:
                wld = wp0_ld
            for tc in range(8):
                ps = psAB[tc % 2]
                ii = e8 * 8 + tc
                for cch in range(32):
                    w = []
                    if cch == 0:
                        w = [(CgpL, wld)]
                        if e8 == 0 and tc == 0:
                            w.append((Cdve, p2_done_dve))
                        prev = ev_prev.get(ii - 2)
                        if prev is not None:
                            w.append((Cact, prev))
                    p.add("tensor", lambda e, ps=ps, cch=cch, tc=tc, wsl=wsl: e.matmul(
                        ps[:, 0:512], BIGA[:, cch, 128 * tc:128 * (tc + 1)],
                        wsl[:, cch, :], start=(cch == 0), stop=(cch == 31),
                        skip_group_check=True),
                        waits=w, incs=([(Cpe, 1)] if cch == 31 else []))
                mm_m = Cpe.n
                w = [(Cpe, mm_m)]
                prev = ape_store_marks.get(ii - 2)
                if prev is not None:
                    w.append((Cst, prev))
                p.add("scalar", lambda e, tc=tc, ps=ps: e.copy(
                    ape_sb[:, tc % 2, :], ps[:, 0:512]),
                    waits=w, incs=[(Cact, 1)])
                ev_prev[ii] = Cact.n
                p.add("sync", lambda e, tc=tc, e8=e8: e.dma_start(
                    out=attn_d[128 * tc:128 * (tc + 1), 512 * e8:512 * (e8 + 1)],
                    in_=ape_sb[:, tc % 2, :]),
                    waits=[(Cact, Cact.n)], incs=[(Cst, 16)])
                ape_store_marks[ii] = Cst.n
            proj_mm_marks[e8] = Cpe.n
        attn_stored = Cst.n

        p4_marks = []
        p4_x2m = {}
        for tc in range(8):
            w = [(Cst, attn_stored), (Cst, wnat_all)] if tc == 0 else [(Cst, p4_marks[tc - 1])]
            p.add("sync", lambda e, tc=tc: e.dma_start(
                out=at_ap, in_=attn_d[128 * tc:128 * (tc + 1), :]),
                waits=w, incs=[(CsyL, 16)])
            wg = [(Cst, wnat_all)] if tc == 0 else [(Cdve, p4_x2m[tc - 1])]
            p.add("gpsimd", lambda e, tc=tc: e.dma_start(
                out=xr_ap, in_=x_in[128 * tc:128 * (tc + 1), :]),
                waits=wg, incs=[(CgpL, 16)])
            ld_s, ld_g = CsyL.n, CgpL.n
            nm, rs, am = emit_ln_stats(at_ap, 0, p4dum, [(CsyL, ld_s), (CgpL, ld_g)])
            p.add("vector", lambda e, nm=nm, rs=rs: e.tensor_scalar(
                at_ap, at_ap, nm, rs, ALU.add, ALU.mult),
                waits=[(Cact, am)], incs=[(Cdve, 1)])
            u_m = Cdve.n
            p.add("vector", lambda e: e.tensor_tensor(at_ap, at_ap, xr_ap, ALU.add),
                  waits=[(Cdve, u_m)], incs=[(Cdve, 1)])
            x2_m = Cdve.n
            p4_x2m[tc] = x2_m
            p.add("sync", lambda e, tc=tc: e.dma_start(
                out=x2_d[128 * tc:128 * (tc + 1), :], in_=at_ap),
                waits=[(Cdve, x2_m)], incs=[(Cst, 16)])
            nm2, rs2, am2 = emit_ln_stats(at_ap, 0, p4dum, [(Cdve, x2_m)])
            p.add("vector", lambda e, nm2=nm2, rs2=rs2: e.tensor_scalar(
                DD[:, 0:C], at_ap, nm2, rs2, ALU.add, ALU.mult),
                waits=[(Cact, am2)], incs=[(Cdve, 1)])
            p.add("sync", lambda e, tc=tc: e.dma_start(
                out=xn2_d[128 * tc:128 * (tc + 1), :], in_=DD[:, 0:C]),
                waits=[(Cdve, Cdve.n)], incs=[(Cst, 16)])
            p4_marks.append(Cst.n)
        p4_stored = Cst.n

        for cch in range(32):
            w = [(Cst, p4_stored), (Cpe, proj_mm_marks[7])] if cch == 0 else []
            p.add("sync", lambda e, cch=cch: e.dma_start(
                out=BIGA[:, cch, 0:OWN],
                in_=xn2_d[:, 128 * cch:128 * (cch + 1)], transpose=True),
                waits=w, incs=[(CsyL, 16)])
        xn2f_ready = CsyL.n
        p.add("gpsimd", lambda e: e.dma_start(out=DD[0:1, 0:C], in_=brow[:, :]),
              waits=[(Cst, p4_stored)], incs=[(CgpL, 16)])
        brow_ld = CgpL.n
        ffn_mm_marks = {}
        ffe_store_marks = {}
        ev_prev = {}
        for e8 in range(8):
            wsl = B0 if e8 % 2 == 0 else B1
            w = ([(Cpe, proj_mm_marks[7]), (Cst, wnat_all)] if e8 < 2
                 else [(Cpe, ffn_mm_marks[e8 - 2]), (Cst, wnat_all)])
            for ic in range(32):
                p.add("sync", lambda e, e8=e8, ic=ic, wsl=wsl: e.dma_start(
                    out=wsl[:, ic, :],
                    in_=wnat_d[512 * e8:512 * (e8 + 1), 128 * ic:128 * (ic + 1)],
                    transpose=True),
                    waits=(w if ic == 0 else []), incs=[(CsyL, 16)])
            wld = CsyL.n
            for tc in range(8):
                ps = psAB[tc % 2]
                ii = e8 * 8 + tc
                w = [(CsyL, wld), (CsyL, xn2f_ready), (CgpL, brow_ld)]
                prev = ev_prev.get(ii - 2)
                if prev is not None:
                    w.append((Cdve, prev))
                p.add("tensor", lambda e, ps=ps, e8=e8: e.matmul(
                    ps[:, 0:512], o1128[:, :], DD[0:1, 512 * e8:512 * (e8 + 1)],
                    start=True, stop=False, skip_group_check=True), waits=w)
                for ic in range(32):
                    p.add("tensor", lambda e, ps=ps, ic=ic, tc=tc, wsl=wsl: e.matmul(
                        ps[:, 0:512], BIGA[:, ic, 128 * tc:128 * (tc + 1)],
                        wsl[:, ic, :], start=False, stop=(ic == 31),
                        skip_group_check=True),
                        incs=([(Cpe, 1)] if ic == 31 else []))
                mm_m = Cpe.n
                w = [(Cpe, mm_m)]
                prev = ffe_store_marks.get(ii - 2)
                if prev is not None:
                    w.append((Cst, prev))
                p.add("vector", lambda e, tc=tc, ps=ps: e.tensor_copy(
                    ape_sb[:, tc % 2, :], ps[:, 0:512]),
                    waits=w, incs=[(Cdve, 1)])
                ev_prev[ii] = Cdve.n
                p.add("sync", lambda e, tc=tc, e8=e8: e.dma_start(
                    out=ffn_d[128 * tc:128 * (tc + 1), 512 * e8:512 * (e8 + 1)],
                    in_=ape_sb[:, tc % 2, :]),
                    waits=[(Cdve, Cdve.n)], incs=[(Cst, 16)])
                ffe_store_marks[ii] = Cst.n
            ffn_mm_marks[e8] = Cpe.n
        ffn_stored = Cst.n

        p6_marks = []
        for tc in range(8):
            w = [(Cst, ffn_stored)] if tc == 0 else [(Cout, p6_marks[tc - 1])]
            p.add("sync", lambda e, tc=tc: e.dma_start(
                out=ft_ap, in_=ffn_d[128 * tc:128 * (tc + 1), :]),
                waits=w, incs=[(CsyL, 16)])
            p.add("sync", lambda e, tc=tc: e.dma_start(
                out=x2r_ap, in_=x2_d[128 * tc:128 * (tc + 1), :]), incs=[(CsyL, 16)])
            ld = CsyL.n
            nm, rs, am = emit_ln_stats(ft_ap, 0, DD[:, 0:C], [(CsyL, ld)])
            p.add("vector", lambda e, nm=nm, rs=rs: e.tensor_scalar(
                ft_ap, ft_ap, nm, rs, ALU.add, ALU.mult),
                waits=[(Cact, am)], incs=[(Cdve, 1)])
            u6_m = Cdve.n
            p.add("vector", lambda e: e.tensor_tensor(ft_ap, ft_ap, x2r_ap, ALU.add),
                  waits=[(Cdve, u6_m)], incs=[(Cdve, 1)])
            p.add("sync", lambda e, tc=tc: e.dma_start(
                out=out[128 * tc:128 * (tc + 1), :], in_=ft_ap),
                waits=[(Cdve, Cdve.n)], incs=[(Cout, 16)])
            p6_marks.append(Cout.n)
        p.add("sync", lambda e: e.nop(), waits=[(Cout, Cout.n)])

        p.emit(block)
    return nc


def _host_prep(L0_attn_x, L0_attn_w_attn, L0_attn_w_proj, L0_attn_adapter_wte,
               L0_attn_gating, L0_ffn_qweight, L0_ffn_scales, L0_ffn_scaled_zeros,
               L0_ffn_bias):
    x = np.asarray(L0_attn_x, dtype=np.float32)
    wa = np.asarray(L0_attn_w_attn, dtype=np.float32)
    wp = np.asarray(L0_attn_w_proj, dtype=np.float32)
    aw = np.asarray(L0_attn_adapter_wte, dtype=np.float32)
    g = np.asarray(L0_attn_gating, dtype=np.float32)
    qwv = np.ascontiguousarray(np.asarray(L0_ffn_qweight, dtype=np.int32).reshape(C, C))
    sc = np.asarray(L0_ffn_scales, dtype=np.float32).reshape(C, 32)
    zr = np.asarray(L0_ffn_scaled_zeros, dtype=np.float32).reshape(C, 32)
    bi = np.ascontiguousarray(np.asarray(L0_ffn_bias, dtype=np.float32).reshape(1, C))

    wqT = np.ascontiguousarray(wa[0:C].T)
    wkT = np.ascontiguousarray(wa[C:2 * C].T)
    wvT = np.ascontiguousarray(wa[2 * C:3 * C].T)
    wpT = np.ascontiguousarray(wp.T)
    awT = np.ascontiguousarray(aw.T)
    scs = np.ascontiguousarray(sc.reshape(32, 128, 32).transpose(1, 0, 2).reshape(128, 1024))
    zrs = np.ascontiguousarray(zr.reshape(32, 128, 32).transpose(1, 0, 2).reshape(128, 1024))

    theta = 1.0 / (10000.0 ** (np.arange(0, HD, 2, dtype=np.float64) / HD))
    idx = np.outer(np.arange(T, dtype=np.float64), theta)
    idx2 = np.concatenate([idx, idx], axis=-1)
    cos_t = np.cos(idx2).astype(np.float32)
    sin_t = np.sin(idx2).astype(np.float32)
    sgn = np.where(np.arange(HD) < HD // 2, -1.0, 1.0).astype(np.float32)
    ssin_t = sin_t * sgn[None, :]
    p64m = np.zeros((HD, HD), dtype=np.float32)
    for d in range(HD):
        p64m[(d + 64) % 128, d] = 1.0
    dmask = np.where(np.arange(128)[:, None] <= np.arange(128)[None, :], 0.0,
                     NEG).astype(np.float32)

    perms = []
    for s in range(2):
        order = list(range(s, NB, 2)) + list(range(1 - s, NB, 2))
        perm = np.concatenate([np.arange(128 * gi, 128 * gi + 128) for gi in order])
        perms.append(perm)

    in_maps = []
    for c in range(8):
        b, s = c // 2, c % 2
        perm = perms[s]
        in_maps.append({
            "x": np.ascontiguousarray(x[b][perm]),
            "wqT": wqT, "wkT": wkT, "wvT": wvT, "wpT": wpT, "awT": awT,
            "gat": np.ascontiguousarray(g.reshape(1, 1)),
            "qw": qwv, "scs": scs, "zrs": zrs, "brow": bi,
            "cosf": np.ascontiguousarray(cos_t[perm].T).astype(BF),
            "ssinf": np.ascontiguousarray(ssin_t[perm].T).astype(BF),
            "p64": p64m.astype(BF),
            "dmask": dmask.astype(BF),
            "xmask": np.full((128, 128), NEG if s == 0 else 0.0, dtype=np.float32).astype(BF),
        })
    return in_maps, perms


def kernel(**inputs):
    in_maps, perms = _host_prep(**inputs)
    nc = build()
    res = run_bass_kernel_spmd(nc, in_maps, core_ids=list(range(8)))
    outf = np.empty((B, T, C), dtype=np.float32)
    for c in range(8):
        b, s = c // 2, c % 2
        outf[b][perms[s][0:OWN]] = np.asarray(res.results[c]["out"], dtype=np.float32)
    return outf


# version 04
# speedup vs baseline: 1.0619x; 1.0619x over previous
import sys

import numpy as np
import ml_dtypes

sys.path.insert(0, "/opt/trn_rl_repo")
import concourse.bass as bass
import concourse.mybir as mybir
from concourse.bass_utils import run_bass_kernel_spmd

F32 = mybir.dt.float32
BF16 = mybir.dt.bfloat16
I32 = mybir.dt.int32
AF = mybir.ActivationFunctionType
ALU = mybir.AluOpType
BF = ml_dtypes.bfloat16

B, T, C = 4, 2048, 4096
DEBUG_TAPS = False
H, HD = 32, 128
OWN = T // 2
NB = T // 128
EPS = 1e-5
EXP_BIAS = -10.0
NEG = -1.0e9
RS128 = 1.0 / float(np.sqrt(128.0))


class Ctr:

    def __init__(self, sem):
        self.sem = sem
        self.n = 0


class Prog:

    def __init__(self):
        self.ops = {k: [] for k in ("sync", "gpsimd", "vector", "scalar", "tensor")}

    def add(self, eng, fn, waits=(), incs=()):
        for c, v in waits:
            assert v <= c.n, f"wait on future value {v} > {c.n} ({eng})"
        post = []
        for c, v in incs:
            c.n += v
            post.append((c.sem, v))
        self.ops[eng].append((list(waits), fn, post))

    def emit(self, block):
        def mk(name):
            def inner(eng):
                for waits, fn, incs in self.ops[name]:
                    for c, v in waits:
                        eng.wait_ge(c.sem, v)
                    ins = fn(eng)
                    for sem, v in incs:
                        ins.then_inc(sem, v)
            return inner

        block.sync(mk("sync"))
        block.gpsimd(mk("gpsimd"))
        block.vector(mk("vector"))
        block.scalar(mk("scalar"))
        block.tensor(mk("tensor"))


def build():
    import contextlib

    nc = bass.Bass("TRN2", target_bir_lowering=False)

    x_in = nc.dram_tensor("x", [T, C], F32, kind="ExternalInput")
    wqT = nc.dram_tensor("wqT", [C, C], F32, kind="ExternalInput")
    wkT = nc.dram_tensor("wkT", [C, C], F32, kind="ExternalInput")
    wvT = nc.dram_tensor("wvT", [C, C], F32, kind="ExternalInput")
    wpT = nc.dram_tensor("wpT", [C, C], F32, kind="ExternalInput")
    awT = nc.dram_tensor("awT", [C, 10], F32, kind="ExternalInput")
    gat = nc.dram_tensor("gat", [1, 1], F32, kind="ExternalInput")
    qw = nc.dram_tensor("qw", [C, C], I32, kind="ExternalInput")
    scs = nc.dram_tensor("scs", [128, 1024], F32, kind="ExternalInput")
    zrs = nc.dram_tensor("zrs", [128, 1024], F32, kind="ExternalInput")
    brow = nc.dram_tensor("brow", [1, C], F32, kind="ExternalInput")
    cosf = nc.dram_tensor("cosf", [HD, T], BF16, kind="ExternalInput")
    ssinf = nc.dram_tensor("ssinf", [HD, T], BF16, kind="ExternalInput")
    p64 = nc.dram_tensor("p64", [HD, HD], BF16, kind="ExternalInput")
    dmask = nc.dram_tensor("dmask", [128, 128], BF16, kind="ExternalInput")
    xmask = nc.dram_tensor("xmask", [128, 128], BF16, kind="ExternalInput")

    out = nc.dram_tensor("out", [OWN, C], F32, kind="ExternalOutput")

    KI = "ExternalOutput" if DEBUG_TAPS else "Internal"
    xn1_d = nc.dram_tensor("xn1_d", [T, C], BF16, kind=KI)
    kf_d = nc.dram_tensor("kf_d", [C, T], BF16, kind=KI)
    qf_d = nc.dram_tensor("qf_d", [C, OWN], BF16, kind=KI)
    v_d = nc.dram_tensor("v_d", [T, C], BF16, kind=KI)
    wnat_d = nc.dram_tensor("wnat_d", [C, C], BF16, kind=KI)
    attn_d = nc.dram_tensor("attn_d", [OWN, C], F32, kind=KI)
    x2_d = nc.dram_tensor("x2_d", [OWN, C], F32, kind=KI)
    xn2_d = nc.dram_tensor("xn2_d", [OWN, C], BF16, kind=KI)
    ffn_d = nc.dram_tensor("ffn_d", [OWN, C], F32, kind=KI)
    y_dbg = nc.dram_tensor("y_dbg", [C, OWN], BF16, kind=KI) if DEBUG_TAPS else None
    sums_dbg = nc.dram_tensor("sums_dbg", [H, 2048], F32, kind=KI) if DEBUG_TAPS else None
    av_dbg = nc.dram_tensor("av_dbg", [10, C], BF16, kind=KI) if DEBUG_TAPS else None
    ak_dbg = nc.dram_tensor("ak_dbg", [128, 32 * 10], BF16, kind=KI) if DEBUG_TAPS else None

    with contextlib.ExitStack() as ctx:
        def sbt(name, shape, dt):
            return ctx.enter_context(nc.sbuf_tensor(name, shape, dt))

        def pst(name, shape, dt=F32):
            return ctx.enter_context(nc.psum_tensor(name, shape, dt))

        BIGA = sbt("BIGA", [128, 32, 1024], BF16)
        B0 = sbt("B0", [128, 32, 512], BF16)
        B1 = sbt("B1", [128, 32, 512], BF16)
        CC = sbt("CC", [128, 8192], F32)
        DD = sbt("DD", [128, 4096], BF16)
        cos_sb = sbt("cos_sb", [HD, 1024], BF16)
        ssin_sb = sbt("ssin_sb", [HD, 1024], BF16)
        p64_sb = sbt("p64_sb", [HD, HD], BF16)
        sc_sb = sbt("sc_sb", [128, 1024], F32)
        zr_sb = sbt("zr_sb", [128, 1024], F32)
        dm_sb = sbt("dm_sb", [128, 128], BF16)
        xm_sb = sbt("xm_sb", [128, 128], BF16)
        ak_sb = sbt("ak_sb", [128, 32, 10], BF16)
        av_sb = sbt("av_sb", [10, C], BF16)
        pf_sb = sbt("pf_sb", [128, 32, 10], BF16)
        g_sb = sbt("g_sb", [1, 1], F32)
        ones128 = sbt("ones128", [128, 1], BF16)
        ones10 = sbt("ones10", [10, 1], BF16)
        o110 = sbt("o110", [1, 10], BF16)
        o1128 = sbt("o1128", [1, 128], BF16)
        srow = sbt("srow", [1, 1024], F32)
        sarow = sbt("sarow", [1, 1024], BF16)
        stat = sbt("stat", [128, 16], F32)
        cst_sb = sbt("cst_sb", [128, 2], F32)
        k0_sb = sbt("k0_sb", [128, 1024], BF16)
        t1_sb = sbt("t1_sb", [128, 1024], BF16)
        k1_sb = sbt("k1_sb", [128, 1024], BF16)
        ape_sb = sbt("ape_sb", [128, 2, 512], F32)
        ve_sb = ape_sb.ap().bitcast(BF16)[:, :, 0:512]

        b1f = B1.ap().rearrange("p a b -> p (a b)")
        kf2 = [b1f[:, 0:2048], b1f[:, 2048:4096]]
        qf2 = [b1f[:, 4096:5120], b1f[:, 5120:6144]]
        v2 = [b1f[:, 6144:8192].rearrange("p (kc d) -> p kc d", d=128),
              b1f[:, 8192:10240].rearrange("p (kc d) -> p kc d", d=128)]
        exps = [b1f[:, 10240:11264], b1f[:, 11264:12288]]
        invb = b1f[:, 12288:13312]
        pa_sb = b1f[0:10, 13312:14336]
        expa = b1f[0:10, 14336:15360]
        invrow = b1f[0:1, 15360:16384]

        ccb = CC.ap().bitcast(BF16)
        xt_sl = [ccb[:, 0:4096], ccb[:, 4096:8192]]
        nrm_sl = [ccb[:, 8192:12288], ccb[:, 12288:16384]]
        qw_sl = [ccb[:, 0:4096], ccb[:, 4096:8192]]
        wn_sl = [ccb[:, 8192:12288], ccb[:, 12288:16384]]
        at_ap = CC.ap()[:, 0:4096]
        xr_ap = ccb[:, 8192:12288]
        p4dum = ccb[:, 12288:16384]
        ft_ap = CC.ap()[:, 0:4096]
        x2r_ap = CC.ap()[:, 4096:8192]

        psA = pst("psA", [128, 1024])
        psB = pst("psB", [128, 1024])
        psC = pst("psC", [128, 1024])
        psD = pst("psD", [128, 1024])
        psAB = [psA, psB]

        block = ctx.enter_context(nc.Block())
        p = Prog()
        sems = [ctx.enter_context(nc.semaphore(f"S{i}")) for i in range(8)]
        CsyL = Ctr(sems[0])
        CgpL = Ctr(sems[1])
        Cst = Ctr(sems[2])
        Cpe = Ctr(sems[3])
        Cact = Ctr(sems[4])
        Cdve = Ctr(sems[5])
        Cpool = Ctr(sems[6])
        Cout = Ctr(sems[7])

        p.add("vector", lambda e: e.memset(ones128[:, :], 1.0))
        p.add("vector", lambda e: e.memset(ones10[:, :], 1.0))
        p.add("vector", lambda e: e.memset(o110[:, :], 1.0))
        p.add("vector", lambda e: e.memset(o1128[:, :], 1.0))
        p.add("vector", lambda e: e.memset(cst_sb[:, 0:1], EPS))
        p.add("vector", lambda e: e.memset(cst_sb[:, 1:2], EXP_BIAS), incs=[(Cdve, 1)])
        ones_ready = Cdve.n
        for dst, src in ((p64_sb, p64), (dm_sb, dmask), (xm_sb, xmask),
                         (g_sb, gat)):
            nd = len(dst.shape)
            p.add("sync", lambda e, d=dst, s=src, nd=nd: e.dma_start(
                out=d[(slice(None),) * nd], in_=s[(slice(None),) * nd]),
                incs=[(CsyL, 16)])
        consts_ready = CsyL.n
        p.add("gpsimd", lambda e: e.dma_start(
            out=pf_sb[:, :, :], in_=awT[:, :].rearrange("(c p) j -> p c j", p=128)),
            incs=[(CgpL, 16)])
        p.add("gpsimd", lambda e: e.dma_start(out=sc_sb[:, :], in_=scs[:, :]),
              incs=[(CgpL, 16)])
        p.add("gpsimd", lambda e: e.dma_start(out=zr_sb[:, :], in_=zrs[:, :]),
              incs=[(CgpL, 16)])
        sczr_ready = CgpL.n
        pf_ready = CgpL.n

        def emit_ln_stats(src_ap, sc0, dummy_ap, pre_waits):
            p.add("vector", lambda e: e.reduce_sum(
                stat[:, sc0 + 5:sc0 + 6], src_ap, mybir.AxisListType.X),
                waits=pre_waits)
            p.add("vector", lambda e: e.scalar_tensor_tensor(
                out=dummy_ap, in0=src_ap, scalar=1.0, in1=src_ap,
                op0=ALU.mult, op1=ALU.mult, accum_out=stat[:, sc0 + 1:sc0 + 2]))
            p.add("vector", lambda e: e.tensor_scalar(
                stat[:, sc0:sc0 + 1], stat[:, sc0 + 5:sc0 + 6], -1.0 / C, None, ALU.mult))
            p.add("vector", lambda e: e.tensor_scalar(
                stat[:, sc0 + 3:sc0 + 4], stat[:, sc0 + 1:sc0 + 2], 1.0 / C, None,
                ALU.mult))
            p.add("vector", lambda e: e.tensor_tensor(
                stat[:, sc0 + 2:sc0 + 3], stat[:, sc0:sc0 + 1], stat[:, sc0:sc0 + 1],
                ALU.mult), incs=[(Cdve, 1)])
            m2m = Cdve.n
            p.add("vector", lambda e: e.tensor_tensor(
                stat[:, sc0 + 3:sc0 + 4], stat[:, sc0 + 3:sc0 + 4],
                stat[:, sc0 + 2:sc0 + 3], ALU.subtract),
                waits=[(Cdve, m2m)], incs=[(Cdve, 1)])
            dm = Cdve.n
            p.add("scalar", lambda e: e.activation(
                stat[:, sc0 + 4:sc0 + 5], stat[:, sc0 + 3:sc0 + 4], AF.Ln,
                bias=cst_sb[:, 0:1]), waits=[(Cdve, dm)], incs=[(Cact, 1)])
            lnm = Cact.n
            p.add("scalar", lambda e: e.activation(
                stat[:, sc0 + 4:sc0 + 5], stat[:, sc0 + 4:sc0 + 5], AF.Exp, scale=-0.5),
                waits=[(Cact, lnm)], incs=[(Cact, 1)])
            return stat[:, sc0:sc0 + 1], stat[:, sc0 + 4:sc0 + 5], Cact.n

        p0_norm_marks = {}
        p0_store_marks = {}
        for i in range(16):
            sl = i % 2
            w = []
            if i >= 2:
                w.append((Cdve, p0_norm_marks[i - 2]))
            p.add("gpsimd", lambda e, i=i, sl=sl: e.dma_start(
                out=xt_sl[sl], in_=x_in[128 * i:128 * (i + 1), :]),
                waits=w, incs=[(CgpL, 16)])
            ld = CgpL.n
            nm, rs, am = emit_ln_stats(xt_sl[sl], 8 * sl, DD[:, 0:C], [(CgpL, ld)])
            w = [(Cact, am)]
            if i >= 2:
                w.append((Cst, p0_store_marks[i - 2]))
            p.add("vector", lambda e, sl=sl, nm=nm, rs=rs: e.tensor_scalar(
                nrm_sl[sl], xt_sl[sl], nm, rs, ALU.add, ALU.mult),
                waits=w, incs=[(Cdve, 1)])
            p0_norm_marks[i] = Cdve.n
            p.add("sync", lambda e, i=i, sl=sl: e.dma_start(
                out=xn1_d[128 * i:128 * (i + 1), :], in_=nrm_sl[sl]),
                waits=[(Cdve, Cdve.n)], incs=[(Cst, 16)])
            p0_store_marks[i] = Cst.n
        xn1_stored = Cst.n
        p0_dve_done = Cdve.n

        kq_evac_marks = {}
        rope_prev = {"t1": 0, "pl": 0, "st": 0}
        ve_store_marks = {}

        def emit_rope_finish(kind, h, half, evac_m, tab_m):
            rot_m = Cpe.n
            prev_t1 = rope_prev["t1"]
            prev_pl = rope_prev["pl"]
            prev_st = rope_prev["st"]
            p.add("vector", lambda e: e.tensor_tensor(
                t1_sb[:, :], psC[:, :], ssin_sb[:, :], ALU.mult),
                waits=[(Cpe, rot_m), (Cpool, prev_pl), (CsyL, tab_m)], incs=[(Cdve, 1)])
            rope_prev["t1"] = Cdve.n
            p.add("gpsimd", lambda e: e.tensor_tensor(
                k1_sb[:, :], k0_sb[:, :], cos_sb[:, :], ALU.mult),
                waits=[(Cact, evac_m), (Cst, prev_st), (CsyL, tab_m)],
                incs=[(Cpool, 1)])
            t0_m = Cpool.n
            p.add("gpsimd", lambda e: e.tensor_tensor(
                k1_sb[:, :], k1_sb[:, :], t1_sb[:, :], ALU.add),
                waits=[(Cdve, rope_prev["t1"]), (Cpool, t0_m)], incs=[(Cpool, 1)])
            rope_prev["pl"] = Cpool.n
            if kind == "k":
                p.add("sync", lambda e, h=h, half=half: e.dma_start(
                    out=kf_d[128 * h:128 * (h + 1), 1024 * half:1024 * (half + 1)],
                    in_=k1_sb[:, :]),
                    waits=[(Cpool, Cpool.n)], incs=[(Cst, 16)])
            else:
                p.add("sync", lambda e, h=h: e.dma_start(
                    out=qf_d[128 * h:128 * (h + 1), 0:OWN], in_=k1_sb[:, :]),
                    waits=[(Cpool, Cpool.n)], incs=[(Cst, 16)])
            rope_prev["st"] = Cst.n

        av_evac_mark = {"m": 0}
        ak_evac_mark = {"m": 0}
        bslot_pe = {0: 0, 1: 0}
        for half in range(2):
            p.add("sync", lambda e, half=half: e.dma_start(
                out=cos_sb[:, :], in_=cosf[:, 1024 * half:1024 * (half + 1)]),
                waits=[(Cpool, rope_prev["pl"])], incs=[(CsyL, 16)])
            p.add("sync", lambda e, half=half: e.dma_start(
                out=ssin_sb[:, :], in_=ssinf[:, 1024 * half:1024 * (half + 1)]),
                waits=[(Cdve, rope_prev["t1"])], incs=[(CsyL, 16)])
            ropetab_ready = CsyL.n
            if half == 0:
                xw = [(Cst, xn1_stored), (CsyL, consts_ready)]
            else:
                xw = [(Cst, xn1_stored), (Cpe, Cpe.n)]
            for cch in range(32):
                p.add("sync", lambda e, cch=cch, half=half: e.dma_start(
                    out=BIGA[:, cch, :],
                    in_=xn1_d[1024 * half:1024 * (half + 1), 128 * cch:128 * (cch + 1)],
                    transpose=True),
                    waits=(xw if cch == 0 else []), incs=[(CsyL, 16)])
            xnf_ready = CsyL.n

            ev_marks = {}
            for e8 in range(8):
                wsl = B0 if e8 % 2 == 0 else B1
                wv_w = [(CgpL, pf_ready), (Cpe, bslot_pe[e8 % 2])]
                p.add("gpsimd", lambda e, e8=e8, wsl=wsl: e.dma_start(
                    out=wsl[:, :, :],
                    in_=wvT[:, 512 * e8:512 * (e8 + 1)].rearrange("(c p) n -> p c n", p=128)),
                    waits=wv_w, incs=[(CgpL, 16)])
                wv_ld = CgpL.n
                for tc in range(8):
                    ps = psAB[tc % 2]
                    ii = e8 * 8 + tc
                    for cch in range(32):
                        w = []
                        if cch == 0:
                            w = [(CgpL, wv_ld), (CsyL, xnf_ready)]
                            prev = ev_marks.get(ii - 2)
                            if prev is not None:
                                w.append((Cact, prev))
                        p.add("tensor", lambda e, ps=ps, cch=cch, tc=tc, wsl=wsl: e.matmul(
                            ps[:, 0:512], BIGA[:, cch, 128 * tc:128 * (tc + 1)],
                            wsl[:, cch, :], start=(cch == 0), stop=(cch == 31),
                            skip_group_check=True),
                            waits=w, incs=([(Cpe, 1)] if cch == 31 else []))
                        if half == 0 and tc == 0:
                            p.add("tensor", lambda e, cch=cch, wsl=wsl: e.matmul(
                                psC[0:10, 0:512], pf_sb[:, cch, :], wsl[:, cch, :],
                                start=(cch == 0), stop=(cch == 31), skip_group_check=True),
                                waits=([(Cact, av_evac_mark["m"]), (CgpL, pf_ready)]
                                       if cch == 0 else []),
                                incs=([(Cpe, 1)] if cch == 31 else []))
                    mm_m = Cpe.n
                    w = [(Cpe, mm_m)]
                    prev = ve_store_marks.get(ii - 2 + 64 * half)
                    if prev is not None:
                        w.append((Cst, prev))
                    p.add("scalar", lambda e, tc=tc, ps=ps: e.copy(
                        ve_sb[:, tc % 2, :], ps[:, 0:512]),
                        waits=w, incs=[(Cact, 1)])
                    p.add("sync", lambda e, tc=tc, e8=e8, half=half: e.dma_start(
                        out=v_d[1024 * half + 128 * tc:1024 * half + 128 * (tc + 1),
                                512 * e8:512 * (e8 + 1)],
                        in_=ve_sb[:, tc % 2, :]),
                        waits=[(Cact, Cact.n)], incs=[(Cst, 16)])
                    ev_marks[ii] = Cact.n
                    ve_store_marks[ii + 64 * half] = Cst.n
                bslot_pe[e8 % 2] = Cpe.n
                if half == 0:
                    p.add("scalar", lambda e, e8=e8: e.copy(
                        av_sb[0:10, 512 * e8:512 * (e8 + 1)], psC[0:10, 0:512]),
                        waits=[(Cpe, Cpe.n)], incs=[(Cact, 1)])
                    av_evac_mark["m"] = Cact.n

            for kind in (("k", "q") if half == 0 else ("k",)):
                wT_src = wkT if kind == "k" else wqT
                for h in range(32):
                    wsl = (B0 if h % 2 == 0 else B1)[:, :, 0:128]
                    p.add("gpsimd", lambda e, h=h, wsl=wsl, wT_src=wT_src: e.dma_start(
                        out=wsl, in_=wT_src[:, 128 * h:128 * (h + 1)].rearrange(
                            "(c p) n -> p c n", p=128)),
                        waits=[(Cpe, bslot_pe[h % 2])], incs=[(CgpL, 16)])
                    wld = CgpL.n
                    ps = psAB[h % 2]
                    for cch in range(32):
                        w = []
                        if cch == 0:
                            w = [(CgpL, wld)]
                            prev = kq_evac_marks.get((kind, half, h - 2))
                            if prev is not None:
                                w.append((Cact, prev))
                        for sg in range(2):
                            p.add("tensor", lambda e, ps=ps, cch=cch, sg=sg, wsl=wsl: e.matmul(
                                ps[:, 512 * sg:512 * (sg + 1)], wsl[:, cch, :],
                                BIGA[:, cch, 512 * sg:512 * (sg + 1)],
                                start=(cch == 0), stop=(cch == 31), skip_group_check=True),
                                waits=(w if sg == 0 else []),
                                incs=([(Cpe, 1)] if (cch == 31 and sg == 1) else []))
                        if kind == "k" and half == 0:
                            p.add("tensor", lambda e, cch=cch, wsl=wsl: e.matmul(
                                psD[:, 0:10], wsl[:, cch, :], pf_sb[:, cch, :],
                                start=(cch == 0), stop=(cch == 31), skip_group_check=True),
                                waits=([(Cdve, ak_evac_mark["m"])] if cch == 0 else []),
                                incs=([(Cpe, 1)] if cch == 31 else []))
                    mm_m = Cpe.n
                    p.add("scalar", lambda e, ps=ps: e.copy(k0_sb[:, :], ps[:, :]),
                          waits=[(Cpe, mm_m), (Cpool, rope_prev["pl"])], incs=[(Cact, 1)])
                    kq_evac_marks[(kind, half, h)] = Cact.n
                    evac_m = Cact.n
                    if kind == "k" and half == 0:
                        p.add("vector", lambda e, h=h: e.tensor_copy(
                            ak_sb[:, h, :], psD[:, 0:10]),
                            waits=[(Cpe, mm_m)], incs=[(Cdve, 1)])
                        ak_evac_mark["m"] = Cdve.n
                    p.add("tensor", lambda e: e.matmul(
                        psC[:, 0:512], p64_sb[:, :], k0_sb[:, 0:512],
                        start=True, stop=True, skip_group_check=True),
                        waits=[(Cact, evac_m), (Cdve, rope_prev["t1"])])
                    p.add("tensor", lambda e: e.matmul(
                        psC[:, 512:1024], p64_sb[:, :], k0_sb[:, 512:1024],
                        start=True, stop=True, skip_group_check=True),
                        incs=[(Cpe, 1)])
                    emit_rope_finish(kind, h, half, evac_m, ropetab_ready)
        if DEBUG_TAPS:
            p.add("sync", lambda e: e.dma_start(
                out=av_dbg[:, :], in_=av_sb[0:10, :]),
                waits=[(Cact, Cact.n)], incs=[(Cst, 16)])
            p.add("sync", lambda e: e.dma_start(
                out=ak_dbg[:, :], in_=ak_sb.ap().rearrange("p a b -> p (a b)")),
                waits=[(Cdve, Cdve.n)], incs=[(Cst, 16)])
        p1_stores = Cst.n
        p1_pe_done = Cpe.n
        p1_dve_done = Cdve.n

        head_load_marks = {}
        head_end_pe = {}
        head_free = {}
        dq_wn_store = {}

        def emit_head_loads(h):
            sl = h % 2
            w = [(Cst, p1_stores)] if h < 2 else [(Cpe, head_end_pe[h - 2])]
            p.add("sync", lambda e, h=h, sl=sl: e.dma_start(
                out=kf2[sl], in_=kf_d[128 * h:128 * (h + 1), :]),
                waits=w, incs=[(CsyL, 16)])
            p.add("sync", lambda e, h=h, sl=sl: e.dma_start(
                out=qf2[sl], in_=qf_d[128 * h:128 * (h + 1), :]), incs=[(CsyL, 16)])
            p.add("sync", lambda e, h=h, sl=sl: e.dma_start(
                out=v2[sl], in_=v_d[:, 128 * h:128 * (h + 1)].rearrange(
                    "(kc tp) d -> tp kc d", tp=128)), incs=[(CsyL, 16)])
            head_load_marks[h] = CsyL.n

        emit_head_loads(0)
        emit_head_loads(1)

        for h in range(H):
            sl = h % 2
            if h == 0:
                p.add("gpsimd", lambda e: e.dma_start(
                    out=B0[:, :, :],
                    in_=wpT[:, 0:512].rearrange("(c p) n -> p c n", p=128)),
                    waits=[(Cpe, p1_pe_done)], incs=[(CgpL, 16)])
                wp0_ld = CgpL.n

            exp_marks = {}
            pv_marks = {}

            def bank_segs(qlo):
                return [(qlo, 512), (512, 1024)] if qlo < 512 else [(qlo, 1024)]

            def emit_sum_pv(cp):
                cqp = cp % 8
                qlop = 128 * cqp
                slp = cp % 2
                sgl = bank_segs(qlop)
                for sg, (s0, s1) in enumerate(sgl):
                    w = []
                    if sg == 0:
                        w = [(Cact, exp_marks[cp])]
                        if cp == 0 and h == 0:
                            w.append((Cdve, p1_dve_done))
                    p.add("tensor", lambda e, s0=s0, s1=s1, slp=slp, cp=cp: e.matmul(
                        psD[0:1, s0:s1], ones128[:, :], exps[slp][:, s0:s1],
                        start=(cp == 0), stop=(cp == 15), skip_group_check=True),
                        waits=w)
                for sg, (s0, s1) in enumerate(sgl):
                    w = []
                    if cp == 0 and sg == 0 and h >= 1:
                        w = [(Cdve, head_free[h - 1][2])]
                    p.add("tensor", lambda e, s0=s0, s1=s1, slp=slp, cp=cp, sl=sl: e.matmul(
                        psC[:, s0:s1], v2[sl][:, cp, :], exps[slp][:, s0:s1],
                        start=(cp == 0), stop=(cp == 15), skip_group_check=True),
                        waits=w,
                        incs=([(Cpe, 1)] if sg == len(sgl) - 1 else []))
                pv_marks[cp] = Cpe.n

            for c in range(16):
                cq = c % 8
                qlo = 128 * cq
                width = 1024 - qlo
                slot = c % 2
                pss = psAB[slot]
                w = []
                if c == 0:
                    w = [(CsyL, head_load_marks[h])]
                    if h == 0:
                        w.append((Cdve, p1_dve_done))
                    else:
                        w.append((Cdve, head_free[h - 1][0]))
                elif c == 1 and h >= 1:
                    w = [(Cact, head_free[h - 1][1])]
                elif c >= 2:
                    w = [(Cact, exp_marks[c - 2])]
                sgl = bank_segs(qlo)
                for sg, (s0, s1) in enumerate(sgl):
                    p.add("tensor", lambda e, pss=pss, c=c, s0=s0, s1=s1, sl=sl: e.matmul(
                        pss[:, s0:s1], kf2[sl][:, 128 * c:128 * (c + 1)],
                        qf2[sl][:, s0:s1], start=True, stop=True, skip_group_check=True),
                        waits=(w if sg == 0 else []),
                        incs=([(Cpe, 1)] if sg == len(sgl) - 1 else []))
                sc_m = Cpe.n
                msk = dm_sb if c < 8 else xm_sb
                p.add("vector", lambda e, pss=pss, qlo=qlo, msk=msk: e.tensor_tensor(
                    pss[:, qlo:qlo + 128], pss[:, qlo:qlo + 128], msk[:, :], ALU.add),
                    waits=[(Cpe, sc_m)], incs=[(Cdve, 1)])
                mk_m = Cdve.n
                ew = [(Cdve, mk_m)]
                if c >= 2:
                    ew.append((Cpe, pv_marks[c - 2]))
                p.add("scalar", lambda e, pss=pss, qlo=qlo, width=width, slot=slot: e.activation(
                    exps[slot][:, qlo:qlo + width], pss[:, qlo:qlo + width], AF.Exp,
                    scale=RS128, bias=cst_sb[:, 1:2]),
                    waits=ew, incs=[(Cact, 1)])
                exp_marks[c] = Cact.n
                if c >= 1:
                    emit_sum_pv(c - 1)
            emit_sum_pv(15)
            pv_done = Cpe.n

            p.add("scalar", lambda e: e.copy(srow[:, :], psD[0:1, :]),
                  waits=[(Cpe, pv_done)], incs=[(Cact, 1)])
            srow_m = Cact.n
            for sg in range(2):
                p.add("tensor", lambda e, sg=sg, sl=sl, h=h: e.matmul(
                    psA[0:10, 512 * sg:512 * (sg + 1)], ak_sb[:, h, :],
                    qf2[sl][:, 512 * sg:512 * (sg + 1)], start=True, stop=True,
                    skip_group_check=True),
                    waits=([(Cact, exp_marks[14])] if sg == 0 else []),
                    incs=([(Cpe, 1)] if sg == 1 else []))
            asc_m = Cpe.n
            p.add("scalar", lambda e: e.activation(
                expa[:, :], psA[0:10, :], AF.Exp, scale=1.0 / 64.0),
                waits=[(Cpe, asc_m)], incs=[(Cact, 1)])
            expa_m = Cact.n
            for sg in range(2):
                p.add("tensor", lambda e, sg=sg: e.matmul(
                    psD[0:1, 512 * sg:512 * (sg + 1)], ones10[:, :],
                    expa[:, 512 * sg:512 * (sg + 1)], start=True, stop=True,
                    skip_group_check=True),
                    waits=([(Cact, srow_m), (Cact, expa_m)] if sg == 0 else []),
                    incs=([(Cpe, 1)] if sg == 1 else []))
            asum_m = Cpe.n
            p.add("scalar", lambda e: e.copy(sarow[:, :], psD[0:1, :]),
                  waits=[(Cpe, asum_m)], incs=[(Cact, 1)])
            sae_m = Cact.n
            p.add("scalar", lambda e: e.activation(sarow[:, :], sarow[:, :], AF.Ln),
                  waits=[(Cact, sae_m)], incs=[(Cact, 1)])
            saln_m = Cact.n
            p.add("scalar", lambda e: e.activation(sarow[:, :], sarow[:, :], AF.Exp,
                                                   scale=-1.0),
                  waits=[(Cact, saln_m)], incs=[(Cact, 1)])
            sa_m = Cact.n
            p.add("vector", lambda e: e.tensor_scalar(
                sarow[:, :], sarow[:, :], g_sb[:, :], None, ALU.mult),
                waits=[(Cact, sa_m)], incs=[(Cdve, 1)])
            f_m = Cdve.n
            for sg in range(2):
                p.add("tensor", lambda e, sg=sg: e.matmul(
                    psB[0:10, 512 * sg:512 * (sg + 1)], o110[:, :],
                    sarow[:, 512 * sg:512 * (sg + 1)], start=True, stop=True,
                    skip_group_check=True),
                    waits=([(Cact, exp_marks[15]), (Cdve, f_m)] if sg == 0 else []),
                    incs=([(Cpe, 1)] if sg == 1 else []))
            fb_m = Cpe.n
            p.add("vector", lambda e: e.tensor_tensor(
                pa_sb[:, :], expa[:, :], psB[0:10, :], ALU.mult),
                waits=[(Cpe, fb_m)], incs=[(Cdve, 1)])
            pa_m = Cdve.n
            for sg in range(2):
                p.add("tensor", lambda e, sg=sg, h=h: e.matmul(
                    psA[:, 512 * sg:512 * (sg + 1)], av_sb[0:10, 128 * h:128 * (h + 1)],
                    pa_sb[:, 512 * sg:512 * (sg + 1)], start=True, stop=True,
                    skip_group_check=True),
                    waits=([(Cdve, pa_m)] if sg == 0 else []),
                    incs=([(Cpe, 1)] if sg == 1 else []))
            av_m = Cpe.n
            p.add("scalar", lambda e: e.activation(srow[:, :], srow[:, :], AF.Ln),
                  incs=[(Cact, 1)])
            srln_m = Cact.n
            p.add("scalar", lambda e: e.activation(invrow[:, :], srow[:, :], AF.Exp,
                                                   scale=-1.0),
                  waits=[(Cact, srln_m)], incs=[(Cact, 1)])
            inv_m = Cact.n
            for sg in range(2):
                p.add("tensor", lambda e, sg=sg: e.matmul(
                    psB[:, 512 * sg:512 * (sg + 1)], o1128[:, :],
                    invrow[:, 512 * sg:512 * (sg + 1)], start=True, stop=True,
                    skip_group_check=True),
                    waits=([(Cact, inv_m), (Cdve, pa_m)] if sg == 0 else []),
                    incs=([(Cpe, 1)] if sg == 1 else []))
            invb_pe = Cpe.n
            p.add("scalar", lambda e: e.copy(invb[:, :], psB[:, :]),
                  waits=[(Cpe, invb_pe)], incs=[(Cact, 1)])
            invb_m = Cact.n
            p.add("vector", lambda e, h=h: e.tensor_tensor(
                BIGA[:, h, :], psC[:, :], invb[:, :], ALU.mult),
                waits=[(Cact, invb_m), (Cpe, pv_done)], incs=[(Cdve, 1)])
            y1_m = Cdve.n
            p.add("vector", lambda e, h=h: e.tensor_tensor(
                BIGA[:, h, :], BIGA[:, h, :], psA[:, :], ALU.add),
                waits=[(Cpe, av_m), (Cdve, y1_m)], incs=[(Cdve, 1)])
            y2_m = Cdve.n
            if DEBUG_TAPS:
                p.add("sync", lambda e, h=h: e.dma_start(
                    out=y_dbg[128 * h:128 * (h + 1), :], in_=BIGA[:, h, :]),
                    waits=[(Cdve, y2_m)], incs=[(Cst, 16)])
                p.add("sync", lambda e, h=h: e.dma_start(
                    out=sums_dbg[h:h + 1, 0:1024], in_=srow[:, :]),
                    waits=[(Cact, inv_m)], incs=[(Cst, 16)])
                p.add("gpsimd", lambda e, h=h: e.dma_start(
                    out=sums_dbg[h:h + 1, 1024:2048], in_=sarow[:, :]),
                    waits=[(Cdve, f_m)], incs=[(CgpL, 16)])
            head_free[h] = (y2_m, invb_m, y1_m)
            head_end_pe[h] = Cpe.n
            if h + 2 < H:
                emit_head_loads(h + 2)

            oc = h
            dsl = oc % 2
            w = [(CgpL, sczr_ready)]
            if oc < 2:
                w.append((Cst, xn1_stored))
                w.append((Cdve, p0_dve_done))
            prev = dq_wn_store.get(oc - 2)
            if prev is not None:
                w.append((Cst, prev))
            p.add("gpsimd", lambda e, oc=oc, dsl=dsl: e.dma_start(
                out=qw_sl[dsl], in_=qw[128 * oc:128 * (oc + 1), :]),
                waits=w, incs=[(CgpL, 16)])
            qw_ld = CgpL.n
            for g in range(32):
                eng = "vector" if g % 2 == 0 else "gpsimd"
                p.add(eng, lambda e, g=g, oc=oc, dsl=dsl: e.tensor_scalar(
                    wn_sl[dsl][:, 128 * g:128 * (g + 1)],
                    qw_sl[dsl][:, 128 * g:128 * (g + 1)],
                    sc_sb[:, 32 * oc + g:32 * oc + g + 1],
                    zr_sb[:, 32 * oc + g:32 * oc + g + 1],
                    ALU.mult, ALU.add),
                    waits=([(CgpL, qw_ld), (CgpL, sczr_ready)] if g in (0, 1) else []),
                    incs=([(Cdve, 1)] if g == 30 else ([(Cpool, 1)] if g == 31 else [])))
            dq_d, dq_p = Cdve.n, Cpool.n
            p.add("sync", lambda e, oc=oc, dsl=dsl: e.dma_start(
                out=wnat_d[128 * oc:128 * (oc + 1), :], in_=wn_sl[dsl]),
                waits=[(Cdve, dq_d), (Cpool, dq_p)], incs=[(Cst, 16)])
            dq_wn_store[oc] = Cst.n
        wnat_all = Cst.n
        p2_done_pe = Cpe.n
        p2_done_act = Cact.n
        p2_done_dve = Cdve.n

        proj_mm_marks = {}
        ape_store_marks = {}
        ev_prev = {}
        for e8 in range(8):
            wsl = B0 if e8 % 2 == 0 else B1
            if e8 > 0:
                w = ([(Cpe, p2_done_pe), (Cact, p2_done_act), (Cdve, p2_done_dve)]
                     if e8 == 1 else [(Cpe, proj_mm_marks[e8 - 2])])
                p.add("gpsimd", lambda e, e8=e8, wsl=wsl: e.dma_start(
                    out=wsl[:, :, :],
                    in_=wpT[:, 512 * e8:512 * (e8 + 1)].rearrange("(c p) n -> p c n", p=128)),
                    waits=w, incs=[(CgpL, 16)])
                wld = CgpL.n
            else:
                wld = wp0_ld
            for tc in range(8):
                ps = psAB[tc % 2]
                ii = e8 * 8 + tc
                for cch in range(32):
                    w = []
                    if cch == 0:
                        w = [(CgpL, wld)]
                        if e8 == 0 and tc == 0:
                            w.append((Cdve, p2_done_dve))
                        prev = ev_prev.get(ii - 2)
                        if prev is not None:
                            w.append((Cact, prev))
                    p.add("tensor", lambda e, ps=ps, cch=cch, tc=tc, wsl=wsl: e.matmul(
                        ps[:, 0:512], BIGA[:, cch, 128 * tc:128 * (tc + 1)],
                        wsl[:, cch, :], start=(cch == 0), stop=(cch == 31),
                        skip_group_check=True),
                        waits=w, incs=([(Cpe, 1)] if cch == 31 else []))
                mm_m = Cpe.n
                w = [(Cpe, mm_m)]
                prev = ape_store_marks.get(ii - 2)
                if prev is not None:
                    w.append((Cst, prev))
                p.add("scalar", lambda e, tc=tc, ps=ps: e.copy(
                    ape_sb[:, tc % 2, :], ps[:, 0:512]),
                    waits=w, incs=[(Cact, 1)])
                ev_prev[ii] = Cact.n
                p.add("sync", lambda e, tc=tc, e8=e8: e.dma_start(
                    out=attn_d[128 * tc:128 * (tc + 1), 512 * e8:512 * (e8 + 1)],
                    in_=ape_sb[:, tc % 2, :]),
                    waits=[(Cact, Cact.n)], incs=[(Cst, 16)])
                ape_store_marks[ii] = Cst.n
            proj_mm_marks[e8] = Cpe.n
        attn_stored = Cst.n

        p4_marks = []
        p4_x2m = {}
        for tc in range(8):
            w = [(Cst, attn_stored), (Cst, wnat_all)] if tc == 0 else [(Cst, p4_marks[tc - 1])]
            p.add("sync", lambda e, tc=tc: e.dma_start(
                out=at_ap, in_=attn_d[128 * tc:128 * (tc + 1), :]),
                waits=w, incs=[(CsyL, 16)])
            wg = [(Cst, wnat_all)] if tc == 0 else [(Cdve, p4_x2m[tc - 1])]
            p.add("gpsimd", lambda e, tc=tc: e.dma_start(
                out=xr_ap, in_=x_in[128 * tc:128 * (tc + 1), :]),
                waits=wg, incs=[(CgpL, 16)])
            ld_s, ld_g = CsyL.n, CgpL.n
            nm, rs, am = emit_ln_stats(at_ap, 0, p4dum, [(CsyL, ld_s), (CgpL, ld_g)])
            p.add("vector", lambda e, nm=nm, rs=rs: e.tensor_scalar(
                at_ap, at_ap, nm, rs, ALU.add, ALU.mult),
                waits=[(Cact, am)], incs=[(Cdve, 1)])
            u_m = Cdve.n
            p.add("vector", lambda e: e.tensor_tensor(at_ap, at_ap, xr_ap, ALU.add),
                  waits=[(Cdve, u_m)], incs=[(Cdve, 1)])
            x2_m = Cdve.n
            p4_x2m[tc] = x2_m
            p.add("sync", lambda e, tc=tc: e.dma_start(
                out=x2_d[128 * tc:128 * (tc + 1), :], in_=at_ap),
                waits=[(Cdve, x2_m)], incs=[(Cst, 16)])
            nm2, rs2, am2 = emit_ln_stats(at_ap, 0, p4dum, [(Cdve, x2_m)])
            p.add("vector", lambda e, nm2=nm2, rs2=rs2: e.tensor_scalar(
                DD[:, 0:C], at_ap, nm2, rs2, ALU.add, ALU.mult),
                waits=[(Cact, am2)], incs=[(Cdve, 1)])
            p.add("sync", lambda e, tc=tc: e.dma_start(
                out=xn2_d[128 * tc:128 * (tc + 1), :], in_=DD[:, 0:C]),
                waits=[(Cdve, Cdve.n)], incs=[(Cst, 16)])
            p4_marks.append(Cst.n)
        p4_stored = Cst.n

        for cch in range(32):
            w = [(Cst, p4_stored), (Cpe, proj_mm_marks[7])] if cch == 0 else []
            p.add("sync", lambda e, cch=cch: e.dma_start(
                out=BIGA[:, cch, 0:OWN],
                in_=xn2_d[:, 128 * cch:128 * (cch + 1)], transpose=True),
                waits=w, incs=[(CsyL, 16)])
        xn2f_ready = CsyL.n
        p.add("gpsimd", lambda e: e.dma_start(out=DD[0:1, 0:C], in_=brow[:, :]),
              waits=[(Cst, p4_stored)], incs=[(CgpL, 16)])
        brow_ld = CgpL.n
        ffn_mm_marks = {}
        ffe_store_marks = {}
        ev_prev = {}
        for e8 in range(8):
            wsl = B0 if e8 % 2 == 0 else B1
            w = ([(Cpe, proj_mm_marks[7]), (Cst, wnat_all)] if e8 < 2
                 else [(Cpe, ffn_mm_marks[e8 - 2]), (Cst, wnat_all)])
            for ic in range(32):
                p.add("sync", lambda e, e8=e8, ic=ic, wsl=wsl: e.dma_start(
                    out=wsl[:, ic, :],
                    in_=wnat_d[512 * e8:512 * (e8 + 1), 128 * ic:128 * (ic + 1)],
                    transpose=True),
                    waits=(w if ic == 0 else []), incs=[(CsyL, 16)])
            wld = CsyL.n
            for tc in range(8):
                ps = psAB[tc % 2]
                ii = e8 * 8 + tc
                w = [(CsyL, wld), (CsyL, xn2f_ready), (CgpL, brow_ld)]
                prev = ev_prev.get(ii - 2)
                if prev is not None:
                    w.append((Cdve, prev))
                p.add("tensor", lambda e, ps=ps, e8=e8: e.matmul(
                    ps[:, 0:512], o1128[:, :], DD[0:1, 512 * e8:512 * (e8 + 1)],
                    start=True, stop=False, skip_group_check=True), waits=w)
                for ic in range(32):
                    p.add("tensor", lambda e, ps=ps, ic=ic, tc=tc, wsl=wsl: e.matmul(
                        ps[:, 0:512], BIGA[:, ic, 128 * tc:128 * (tc + 1)],
                        wsl[:, ic, :], start=False, stop=(ic == 31),
                        skip_group_check=True),
                        incs=([(Cpe, 1)] if ic == 31 else []))
                mm_m = Cpe.n
                w = [(Cpe, mm_m)]
                prev = ffe_store_marks.get(ii - 2)
                if prev is not None:
                    w.append((Cst, prev))
                p.add("vector", lambda e, tc=tc, ps=ps: e.tensor_copy(
                    ape_sb[:, tc % 2, :], ps[:, 0:512]),
                    waits=w, incs=[(Cdve, 1)])
                ev_prev[ii] = Cdve.n
                p.add("scalar", lambda e, tc=tc, e8=e8: e.dma_start(
                    out=ffn_d[128 * tc:128 * (tc + 1), 512 * e8:512 * (e8 + 1)],
                    in_=ape_sb[:, tc % 2, :]),
                    waits=[(Cdve, Cdve.n)], incs=[(Cst, 16)])
                ffe_store_marks[ii] = Cst.n
            ffn_mm_marks[e8] = Cpe.n
        ffn_stored = Cst.n

        p6_marks = []
        for tc in range(8):
            w = [(Cst, ffn_stored)] if tc == 0 else [(Cout, p6_marks[tc - 1])]
            p.add("sync", lambda e, tc=tc: e.dma_start(
                out=ft_ap, in_=ffn_d[128 * tc:128 * (tc + 1), :]),
                waits=w, incs=[(CsyL, 16)])
            p.add("sync", lambda e, tc=tc: e.dma_start(
                out=x2r_ap, in_=x2_d[128 * tc:128 * (tc + 1), :]), incs=[(CsyL, 16)])
            ld = CsyL.n
            nm, rs, am = emit_ln_stats(ft_ap, 0, DD[:, 0:C], [(CsyL, ld)])
            p.add("vector", lambda e, nm=nm, rs=rs: e.tensor_scalar(
                ft_ap, ft_ap, nm, rs, ALU.add, ALU.mult),
                waits=[(Cact, am)], incs=[(Cdve, 1)])
            u6_m = Cdve.n
            p.add("vector", lambda e: e.tensor_tensor(ft_ap, ft_ap, x2r_ap, ALU.add),
                  waits=[(Cdve, u6_m)], incs=[(Cdve, 1)])
            p.add("sync", lambda e, tc=tc: e.dma_start(
                out=out[128 * tc:128 * (tc + 1), :], in_=ft_ap),
                waits=[(Cdve, Cdve.n)], incs=[(Cout, 16)])
            p6_marks.append(Cout.n)
        p.add("sync", lambda e: e.nop(), waits=[(Cout, Cout.n)])

        p.emit(block)
    return nc


def _host_prep(L0_attn_x, L0_attn_w_attn, L0_attn_w_proj, L0_attn_adapter_wte,
               L0_attn_gating, L0_ffn_qweight, L0_ffn_scales, L0_ffn_scaled_zeros,
               L0_ffn_bias):
    x = np.asarray(L0_attn_x, dtype=np.float32)
    wa = np.asarray(L0_attn_w_attn, dtype=np.float32)
    wp = np.asarray(L0_attn_w_proj, dtype=np.float32)
    aw = np.asarray(L0_attn_adapter_wte, dtype=np.float32)
    g = np.asarray(L0_attn_gating, dtype=np.float32)
    qwv = np.ascontiguousarray(np.asarray(L0_ffn_qweight, dtype=np.int32).reshape(C, C))
    sc = np.asarray(L0_ffn_scales, dtype=np.float32).reshape(C, 32)
    zr = np.asarray(L0_ffn_scaled_zeros, dtype=np.float32).reshape(C, 32)
    bi = np.ascontiguousarray(np.asarray(L0_ffn_bias, dtype=np.float32).reshape(1, C))

    wqT = np.ascontiguousarray(wa[0:C].T)
    wkT = np.ascontiguousarray(wa[C:2 * C].T)
    wvT = np.ascontiguousarray(wa[2 * C:3 * C].T)
    wpT = np.ascontiguousarray(wp.T)
    awT = np.ascontiguousarray(aw.T)
    scs = np.ascontiguousarray(sc.reshape(32, 128, 32).transpose(1, 0, 2).reshape(128, 1024))
    zrs = np.ascontiguousarray(zr.reshape(32, 128, 32).transpose(1, 0, 2).reshape(128, 1024))

    theta = 1.0 / (10000.0 ** (np.arange(0, HD, 2, dtype=np.float64) / HD))
    idx = np.outer(np.arange(T, dtype=np.float64), theta)
    idx2 = np.concatenate([idx, idx], axis=-1)
    cos_t = np.cos(idx2).astype(np.float32)
    sin_t = np.sin(idx2).astype(np.float32)
    sgn = np.where(np.arange(HD) < HD // 2, -1.0, 1.0).astype(np.float32)
    ssin_t = sin_t * sgn[None, :]
    p64m = np.zeros((HD, HD), dtype=np.float32)
    for d in range(HD):
        p64m[(d + 64) % 128, d] = 1.0
    dmask = np.where(np.arange(128)[:, None] <= np.arange(128)[None, :], 0.0,
                     NEG).astype(np.float32)

    perms = []
    for s in range(2):
        order = list(range(s, NB, 2)) + list(range(1 - s, NB, 2))
        perm = np.concatenate([np.arange(128 * gi, 128 * gi + 128) for gi in order])
        perms.append(perm)

    in_maps = []
    for c in range(8):
        b, s = c // 2, c % 2
        perm = perms[s]
        in_maps.append({
            "x": np.ascontiguousarray(x[b][perm]),
            "wqT": wqT, "wkT": wkT, "wvT": wvT, "wpT": wpT, "awT": awT,
            "gat": np.ascontiguousarray(g.reshape(1, 1)),
            "qw": qwv, "scs": scs, "zrs": zrs, "brow": bi,
            "cosf": np.ascontiguousarray(cos_t[perm].T).astype(BF),
            "ssinf": np.ascontiguousarray(ssin_t[perm].T).astype(BF),
            "p64": p64m.astype(BF),
            "dmask": dmask.astype(BF),
            "xmask": np.full((128, 128), NEG if s == 0 else 0.0, dtype=np.float32).astype(BF),
        })
    return in_maps, perms


def kernel(**inputs):
    in_maps, perms = _host_prep(**inputs)
    nc = build()
    res = run_bass_kernel_spmd(nc, in_maps, core_ids=list(range(8)))
    outf = np.empty((B, T, C), dtype=np.float32)
    for c in range(8):
        b, s = c // 2, c % 2
        outf[b][perms[s][0:OWN]] = np.asarray(res.results[c]["out"], dtype=np.float32)
    return outf


# version 06
# speedup vs baseline: 1.3654x; 1.2859x over previous
import sys

import numpy as np
import ml_dtypes

sys.path.insert(0, "/opt/trn_rl_repo")
import concourse.bass as bass
import concourse.mybir as mybir
from concourse.bass_utils import run_bass_kernel_spmd

F32 = mybir.dt.float32
BF16 = mybir.dt.bfloat16
I32 = mybir.dt.int32
AF = mybir.ActivationFunctionType
ALU = mybir.AluOpType
BF = ml_dtypes.bfloat16

B, T, C = 4, 2048, 4096
DEBUG_TAPS = False
H, HD = 32, 128
OWN = T // 2
NB = T // 128
EPS = 1e-5
EXP_BIAS = -10.0
NEG = -1.0e9
RS128 = 1.0 / float(np.sqrt(128.0))


class Ctr:

    def __init__(self, sem):
        self.sem = sem
        self.n = 0


class Prog:

    def __init__(self):
        self.ops = {k: [] for k in ("sync", "gpsimd", "vector", "scalar", "tensor")}

    def add(self, eng, fn, waits=(), incs=()):
        for c, v in waits:
            assert v <= c.n, f"wait on future value {v} > {c.n} ({eng})"
        post = []
        for c, v in incs:
            c.n += v
            post.append((c.sem, v))
        self.ops[eng].append((list(waits), fn, post))

    def emit(self, block):
        def mk(name):
            def inner(eng):
                for waits, fn, incs in self.ops[name]:
                    for c, v in waits:
                        eng.wait_ge(c.sem, v)
                    ins = fn(eng)
                    for sem, v in incs:
                        ins.then_inc(sem, v)
            return inner

        block.sync(mk("sync"))
        block.gpsimd(mk("gpsimd"))
        block.vector(mk("vector"))
        block.scalar(mk("scalar"))
        block.tensor(mk("tensor"))


def build():
    import contextlib

    nc = bass.Bass("TRN2", target_bir_lowering=False)

    x_in = nc.dram_tensor("x", [T, C], F32, kind="ExternalInput")
    wqT = nc.dram_tensor("wqT", [C, C], F32, kind="ExternalInput")
    wkT = nc.dram_tensor("wkT", [C, C], F32, kind="ExternalInput")
    wvT = nc.dram_tensor("wvT", [C, C], F32, kind="ExternalInput")
    wpT = nc.dram_tensor("wpT", [C, C], F32, kind="ExternalInput")
    awT = nc.dram_tensor("awT", [C, 10], F32, kind="ExternalInput")
    gat = nc.dram_tensor("gat", [1, 1], F32, kind="ExternalInput")
    qw = nc.dram_tensor("qw", [C, C], I32, kind="ExternalInput")
    scs = nc.dram_tensor("scs", [128, 1024], F32, kind="ExternalInput")
    zrs = nc.dram_tensor("zrs", [128, 1024], F32, kind="ExternalInput")
    brow = nc.dram_tensor("brow", [1, C], F32, kind="ExternalInput")
    cosf = nc.dram_tensor("cosf", [HD, T], BF16, kind="ExternalInput")
    ssinf = nc.dram_tensor("ssinf", [HD, T], BF16, kind="ExternalInput")
    p64 = nc.dram_tensor("p64", [HD, HD], BF16, kind="ExternalInput")
    dmask = nc.dram_tensor("dmask", [128, 128], BF16, kind="ExternalInput")
    xmask = nc.dram_tensor("xmask", [128, 128], BF16, kind="ExternalInput")

    out = nc.dram_tensor("out", [OWN, C], F32, kind="ExternalOutput")

    KI = "ExternalOutput" if DEBUG_TAPS else "Internal"
    xn1_d = nc.dram_tensor("xn1_d", [T, C], BF16, kind=KI)
    kf_d = nc.dram_tensor("kf_d", [C, T], BF16, kind=KI)
    qf_d = nc.dram_tensor("qf_d", [C, OWN], BF16, kind=KI)
    v_d = nc.dram_tensor("v_d", [T, C], BF16, kind=KI)
    wnat_d = nc.dram_tensor("wnat_d", [C, C], BF16, kind=KI)
    attn_d = nc.dram_tensor("attn_d", [OWN, C], F32, kind=KI)
    x2_d = nc.dram_tensor("x2_d", [OWN, C], F32, kind=KI)
    xn2_d = nc.dram_tensor("xn2_d", [OWN, C], BF16, kind=KI)
    ffn_d = nc.dram_tensor("ffn_d", [OWN, C], F32, kind=KI)
    y_dbg = nc.dram_tensor("y_dbg", [C, OWN], BF16, kind=KI) if DEBUG_TAPS else None
    sums_dbg = nc.dram_tensor("sums_dbg", [H, 2048], F32, kind=KI) if DEBUG_TAPS else None
    av_dbg = nc.dram_tensor("av_dbg", [10, C], BF16, kind=KI) if DEBUG_TAPS else None
    ak_dbg = nc.dram_tensor("ak_dbg", [128, 32 * 10], BF16, kind=KI) if DEBUG_TAPS else None

    with contextlib.ExitStack() as ctx:
        def sbt(name, shape, dt):
            return ctx.enter_context(nc.sbuf_tensor(name, shape, dt))

        def pst(name, shape, dt=F32):
            return ctx.enter_context(nc.psum_tensor(name, shape, dt))

        BIGA = sbt("BIGA", [128, 32, 1024], BF16)
        B0 = sbt("B0", [128, 32, 512], BF16)
        B1 = sbt("B1", [128, 32, 512], BF16)
        CC = sbt("CC", [128, 8192], F32)
        DD = sbt("DD", [128, 4096], BF16)
        cos_sb = sbt("cos_sb", [HD, 1024], BF16)
        ssin_sb = sbt("ssin_sb", [HD, 1024], BF16)
        p64_sb = sbt("p64_sb", [HD, HD], BF16)
        sc_sb = sbt("sc_sb", [128, 1024], F32)
        zr_sb = sbt("zr_sb", [128, 1024], F32)
        dm_sb = sbt("dm_sb", [128, 128], BF16)
        xm_sb = sbt("xm_sb", [128, 128], BF16)
        ak_sb = sbt("ak_sb", [128, 32, 10], BF16)
        av_sb = sbt("av_sb", [10, C], BF16)
        pf_sb = sbt("pf_sb", [128, 32, 10], BF16)
        g_sb = sbt("g_sb", [1, 1], F32)
        ones128 = sbt("ones128", [128, 1], BF16)
        ones10 = sbt("ones10", [10, 1], BF16)
        o110 = sbt("o110", [1, 10], BF16)
        o1128 = sbt("o1128", [1, 128], BF16)
        srow = sbt("srow", [1, 1024], F32)
        sarow = sbt("sarow", [1, 1024], BF16)
        stat = sbt("stat", [128, 16], F32)
        cst_sb = sbt("cst_sb", [128, 2], F32)
        k0_sb = sbt("k0_sb", [128, 1024], BF16)
        t1_sb = sbt("t1_sb", [128, 1024], BF16)
        k1_sb = sbt("k1_sb", [128, 1024], BF16)
        ape_sb = sbt("ape_sb", [128, 2, 512], F32)
        ve_sb = ape_sb.ap().bitcast(BF16)[:, :, 0:512]

        b1f = B1.ap().rearrange("p a b -> p (a b)")
        kf2 = [b1f[:, 0:2048], b1f[:, 2048:4096]]
        qf2 = [b1f[:, 4096:5120], b1f[:, 5120:6144]]
        v2 = [b1f[:, 6144:8192].rearrange("p (kc d) -> p kc d", d=128),
              b1f[:, 8192:10240].rearrange("p (kc d) -> p kc d", d=128)]
        exps = [b1f[:, 10240:11264], b1f[:, 11264:12288]]
        invb = b1f[:, 12288:13312]
        pa_sb = b1f[0:10, 13312:14336]
        expa = b1f[0:10, 14336:15360]
        invrow = b1f[0:1, 15360:16384]

        ccb = CC.ap().bitcast(BF16)
        xt_sl = [ccb[:, 0:4096], ccb[:, 4096:8192]]
        nrm_sl = [ccb[:, 8192:12288], ccb[:, 12288:16384]]
        qw_sl = [ccb[:, 0:4096], ccb[:, 4096:8192]]
        wn_sl = [ccb[:, 8192:12288], ccb[:, 12288:16384]]
        at_ap = CC.ap()[:, 0:4096]
        xr_ap = ccb[:, 8192:12288]
        p4dum = ccb[:, 12288:16384]
        ft_ap = CC.ap()[:, 0:4096]
        x2r_ap = CC.ap()[:, 4096:8192]

        psA = pst("psA", [128, 1024])
        psB = pst("psB", [128, 1024])
        psC = pst("psC", [128, 1024])
        psD = pst("psD", [128, 1024])
        psAB = [psA, psB]

        block = ctx.enter_context(nc.Block())
        p = Prog()
        sems = [ctx.enter_context(nc.semaphore(f"S{i}")) for i in range(8)]
        CsyL = Ctr(sems[0])
        CgpL = Ctr(sems[1])
        Cst = Ctr(sems[2])
        Cpe = Ctr(sems[3])
        Cact = Ctr(sems[4])
        Cdve = Ctr(sems[5])
        Cpool = Ctr(sems[6])
        Cout = Ctr(sems[7])

        p.add("vector", lambda e: e.memset(ones128[:, :], 1.0))
        p.add("vector", lambda e: e.memset(ones10[:, :], 1.0))
        p.add("vector", lambda e: e.memset(o110[:, :], 1.0))
        p.add("vector", lambda e: e.memset(o1128[:, :], 1.0))
        p.add("vector", lambda e: e.memset(cst_sb[:, 0:1], EPS))
        p.add("vector", lambda e: e.memset(cst_sb[:, 1:2], EXP_BIAS), incs=[(Cdve, 1)])
        ones_ready = Cdve.n
        for dst, src in ((p64_sb, p64), (dm_sb, dmask), (xm_sb, xmask),
                         (g_sb, gat)):
            nd = len(dst.shape)
            p.add("sync", lambda e, d=dst, s=src, nd=nd: e.dma_start(
                out=d[(slice(None),) * nd], in_=s[(slice(None),) * nd]),
                incs=[(CsyL, 16)])
        consts_ready = CsyL.n
        p.add("gpsimd", lambda e: e.dma_start(
            out=pf_sb[:, :, :], in_=awT[:, :].rearrange("(c p) j -> p c j", p=128)),
            incs=[(CgpL, 16)])
        p.add("gpsimd", lambda e: e.dma_start(out=sc_sb[:, :], in_=scs[:, :]),
              incs=[(CgpL, 16)])
        p.add("gpsimd", lambda e: e.dma_start(out=zr_sb[:, :], in_=zrs[:, :]),
              incs=[(CgpL, 16)])
        sczr_ready = CgpL.n
        pf_ready = CgpL.n

        def emit_ln_stats(src_ap, sc0, dummy_ap, pre_waits):
            p.add("vector", lambda e: e.reduce_sum(
                stat[:, sc0 + 5:sc0 + 6], src_ap, mybir.AxisListType.X),
                waits=pre_waits)
            p.add("vector", lambda e: e.scalar_tensor_tensor(
                out=dummy_ap, in0=src_ap, scalar=1.0, in1=src_ap,
                op0=ALU.mult, op1=ALU.mult, accum_out=stat[:, sc0 + 1:sc0 + 2]))
            p.add("vector", lambda e: e.tensor_scalar(
                stat[:, sc0:sc0 + 1], stat[:, sc0 + 5:sc0 + 6], -1.0 / C, None, ALU.mult))
            p.add("vector", lambda e: e.tensor_scalar(
                stat[:, sc0 + 3:sc0 + 4], stat[:, sc0 + 1:sc0 + 2], 1.0 / C, None,
                ALU.mult))
            p.add("vector", lambda e: e.tensor_tensor(
                stat[:, sc0 + 2:sc0 + 3], stat[:, sc0:sc0 + 1], stat[:, sc0:sc0 + 1],
                ALU.mult), incs=[(Cdve, 1)])
            m2m = Cdve.n
            p.add("vector", lambda e: e.tensor_tensor(
                stat[:, sc0 + 3:sc0 + 4], stat[:, sc0 + 3:sc0 + 4],
                stat[:, sc0 + 2:sc0 + 3], ALU.subtract),
                waits=[(Cdve, m2m)], incs=[(Cdve, 1)])
            dm = Cdve.n
            p.add("scalar", lambda e: e.activation(
                stat[:, sc0 + 4:sc0 + 5], stat[:, sc0 + 3:sc0 + 4], AF.Ln,
                bias=cst_sb[:, 0:1]), waits=[(Cdve, dm)], incs=[(Cact, 1)])
            lnm = Cact.n
            p.add("scalar", lambda e: e.activation(
                stat[:, sc0 + 4:sc0 + 5], stat[:, sc0 + 4:sc0 + 5], AF.Exp, scale=-0.5),
                waits=[(Cact, lnm)], incs=[(Cact, 1)])
            return stat[:, sc0:sc0 + 1], stat[:, sc0 + 4:sc0 + 5], Cact.n

        p0_norm_marks = {}
        p0_store_marks = {}
        for i in range(16):
            sl = i % 2
            w = []
            if i >= 2:
                w.append((Cdve, p0_norm_marks[i - 2]))
            p.add("gpsimd", lambda e, i=i, sl=sl: e.dma_start(
                out=xt_sl[sl], in_=x_in[128 * i:128 * (i + 1), :]),
                waits=w, incs=[(CgpL, 16)])
            ld = CgpL.n
            nm, rs, am = emit_ln_stats(xt_sl[sl], 8 * sl, DD[:, 0:C], [(CgpL, ld)])
            w = [(Cact, am)]
            if i >= 2:
                w.append((Cst, p0_store_marks[i - 2]))
            p.add("vector", lambda e, sl=sl, nm=nm, rs=rs: e.tensor_scalar(
                nrm_sl[sl], xt_sl[sl], nm, rs, ALU.add, ALU.mult),
                waits=w, incs=[(Cdve, 1)])
            p0_norm_marks[i] = Cdve.n
            p.add("sync", lambda e, i=i, sl=sl: e.dma_start(
                out=xn1_d[128 * i:128 * (i + 1), :], in_=nrm_sl[sl]),
                waits=[(Cdve, Cdve.n)], incs=[(Cst, 16)])
            p0_store_marks[i] = Cst.n
        xn1_stored = Cst.n
        p0_dve_done = Cdve.n

        kq_evac_marks = {}
        rope_prev = {"t1": 0, "pl": 0, "st": 0}
        ve_store_marks = {}

        def emit_rope_finish(kind, h, half, evac_m, tab_m):
            rot_m = Cpe.n
            prev_t1 = rope_prev["t1"]
            prev_pl = rope_prev["pl"]
            prev_st = rope_prev["st"]
            p.add("vector", lambda e: e.tensor_tensor(
                t1_sb[:, :], psC[:, :], ssin_sb[:, :], ALU.mult),
                waits=[(Cpe, rot_m), (Cpool, prev_pl), (CsyL, tab_m)], incs=[(Cdve, 1)])
            rope_prev["t1"] = Cdve.n
            p.add("gpsimd", lambda e: e.tensor_tensor(
                k1_sb[:, :], k0_sb[:, :], cos_sb[:, :], ALU.mult),
                waits=[(Cact, evac_m), (Cst, prev_st), (CsyL, tab_m)],
                incs=[(Cpool, 1)])
            t0_m = Cpool.n
            p.add("gpsimd", lambda e: e.tensor_tensor(
                k1_sb[:, :], k1_sb[:, :], t1_sb[:, :], ALU.add),
                waits=[(Cdve, rope_prev["t1"]), (Cpool, t0_m)], incs=[(Cpool, 1)])
            rope_prev["pl"] = Cpool.n
            if kind == "k":
                p.add("sync", lambda e, h=h, half=half: e.dma_start(
                    out=kf_d[128 * h:128 * (h + 1), 1024 * half:1024 * (half + 1)],
                    in_=k1_sb[:, :]),
                    waits=[(Cpool, Cpool.n)], incs=[(Cst, 16)])
            else:
                p.add("sync", lambda e, h=h: e.dma_start(
                    out=qf_d[128 * h:128 * (h + 1), 0:OWN], in_=k1_sb[:, :]),
                    waits=[(Cpool, Cpool.n)], incs=[(Cst, 16)])
            rope_prev["st"] = Cst.n

        av_evac_mark = {"m": 0}
        ak_evac_mark = {"m": 0}
        bslot_pe = {0: 0, 1: 0}
        for half in range(2):
            p.add("sync", lambda e, half=half: e.dma_start(
                out=cos_sb[:, :], in_=cosf[:, 1024 * half:1024 * (half + 1)]),
                waits=[(Cpool, rope_prev["pl"])], incs=[(CsyL, 16)])
            p.add("sync", lambda e, half=half: e.dma_start(
                out=ssin_sb[:, :], in_=ssinf[:, 1024 * half:1024 * (half + 1)]),
                waits=[(Cdve, rope_prev["t1"])], incs=[(CsyL, 16)])
            ropetab_ready = CsyL.n
            if half == 0:
                xw = [(Cst, xn1_stored), (CsyL, consts_ready)]
            else:
                xw = [(Cst, xn1_stored), (Cpe, Cpe.n)]
            for cch in range(32):
                p.add("sync", lambda e, cch=cch, half=half: e.dma_start(
                    out=BIGA[:, cch, :],
                    in_=xn1_d[1024 * half:1024 * (half + 1), 128 * cch:128 * (cch + 1)],
                    transpose=True),
                    waits=(xw if cch == 0 else []), incs=[(CsyL, 16)])
            xnf_ready = CsyL.n

            ev_marks = {}
            for e8 in range(8):
                wsl = B0 if e8 % 2 == 0 else B1
                wv_w = [(CgpL, pf_ready), (Cpe, bslot_pe[e8 % 2])]
                p.add("gpsimd", lambda e, e8=e8, wsl=wsl: e.dma_start(
                    out=wsl[:, :, :],
                    in_=wvT[:, 512 * e8:512 * (e8 + 1)].rearrange("(c p) n -> p c n", p=128)),
                    waits=wv_w, incs=[(CgpL, 16)])
                wv_ld = CgpL.n
                for tc in range(8):
                    ps = psAB[tc % 2]
                    ii = e8 * 8 + tc
                    for cch in range(32):
                        w = []
                        if cch == 0:
                            w = [(CgpL, wv_ld), (CsyL, xnf_ready)]
                            prev = ev_marks.get(ii - 2)
                            if prev is not None:
                                w.append((Cact, prev))
                        p.add("tensor", lambda e, ps=ps, cch=cch, tc=tc, wsl=wsl: e.matmul(
                            ps[:, 0:512], BIGA[:, cch, 128 * tc:128 * (tc + 1)],
                            wsl[:, cch, :], start=(cch == 0), stop=(cch == 31),
                            skip_group_check=True),
                            waits=w, incs=([(Cpe, 1)] if cch == 31 else []))
                        if half == 0 and tc == 0:
                            p.add("tensor", lambda e, cch=cch, wsl=wsl: e.matmul(
                                psC[0:10, 0:512], pf_sb[:, cch, :], wsl[:, cch, :],
                                start=(cch == 0), stop=(cch == 31), skip_group_check=True),
                                waits=([(Cact, av_evac_mark["m"]), (CgpL, pf_ready)]
                                       if cch == 0 else []),
                                incs=([(Cpe, 1)] if cch == 31 else []))
                    mm_m = Cpe.n
                    w = [(Cpe, mm_m)]
                    prev = ve_store_marks.get(ii - 2 + 64 * half)
                    if prev is not None:
                        w.append((Cst, prev))
                    p.add("scalar", lambda e, tc=tc, ps=ps: e.copy(
                        ve_sb[:, tc % 2, :], ps[:, 0:512]),
                        waits=w, incs=[(Cact, 1)])
                    p.add("sync", lambda e, tc=tc, e8=e8, half=half: e.dma_start(
                        out=v_d[1024 * half + 128 * tc:1024 * half + 128 * (tc + 1),
                                512 * e8:512 * (e8 + 1)],
                        in_=ve_sb[:, tc % 2, :]),
                        waits=[(Cact, Cact.n)], incs=[(Cst, 16)])
                    ev_marks[ii] = Cact.n
                    ve_store_marks[ii + 64 * half] = Cst.n
                bslot_pe[e8 % 2] = Cpe.n
                if half == 0:
                    p.add("scalar", lambda e, e8=e8: e.copy(
                        av_sb[0:10, 512 * e8:512 * (e8 + 1)], psC[0:10, 0:512]),
                        waits=[(Cpe, Cpe.n)], incs=[(Cact, 1)])
                    av_evac_mark["m"] = Cact.n

            for kind in (("k", "q") if half == 0 else ("k",)):
                wT_src = wkT if kind == "k" else wqT
                wld_marks = {}

                def emit_kq_load(h):
                    wsl = (B0 if h % 2 == 0 else B1)[:, :, 0:128]
                    p.add("gpsimd", lambda e, h=h, wsl=wsl, wT_src=wT_src: e.dma_start(
                        out=wsl, in_=wT_src[:, 128 * h:128 * (h + 1)].rearrange(
                            "(c p) n -> p c n", p=128)),
                        waits=[(Cpe, bslot_pe[h % 2])], incs=[(CgpL, 16)])
                    wld_marks[h] = CgpL.n

                emit_kq_load(0)
                for h in range(32):
                    wsl = (B0 if h % 2 == 0 else B1)[:, :, 0:128]
                    wld = wld_marks[h]
                    ps = psAB[h % 2]
                    for cch in range(32):
                        w = []
                        if cch == 0:
                            w = [(CgpL, wld)]
                            prev = kq_evac_marks.get((kind, half, h - 2))
                            if prev is not None:
                                w.append((Cact, prev))
                        for sg in range(2):
                            p.add("tensor", lambda e, ps=ps, cch=cch, sg=sg, wsl=wsl: e.matmul(
                                ps[:, 512 * sg:512 * (sg + 1)], wsl[:, cch, :],
                                BIGA[:, cch, 512 * sg:512 * (sg + 1)],
                                start=(cch == 0), stop=(cch == 31), skip_group_check=True),
                                waits=(w if sg == 0 else []),
                                incs=([(Cpe, 1)] if (cch == 31 and sg == 1) else []))
                        if kind == "k" and half == 0:
                            p.add("tensor", lambda e, cch=cch, wsl=wsl: e.matmul(
                                psD[:, 0:10], wsl[:, cch, :], pf_sb[:, cch, :],
                                start=(cch == 0), stop=(cch == 31), skip_group_check=True),
                                waits=([(Cdve, ak_evac_mark["m"])] if cch == 0 else []),
                                incs=([(Cpe, 1)] if cch == 31 else []))
                    mm_m = Cpe.n
                    bslot_pe[h % 2] = mm_m
                    if h + 1 < 32:
                        emit_kq_load(h + 1)
                    p.add("scalar", lambda e, ps=ps: e.copy(k0_sb[:, :], ps[:, :]),
                          waits=[(Cpe, mm_m), (Cpool, rope_prev["pl"])], incs=[(Cact, 1)])
                    kq_evac_marks[(kind, half, h)] = Cact.n
                    evac_m = Cact.n
                    if kind == "k" and half == 0:
                        p.add("vector", lambda e, h=h: e.tensor_copy(
                            ak_sb[:, h, :], psD[:, 0:10]),
                            waits=[(Cpe, mm_m)], incs=[(Cdve, 1)])
                        ak_evac_mark["m"] = Cdve.n
                    p.add("tensor", lambda e: e.matmul(
                        psC[:, 0:512], p64_sb[:, :], k0_sb[:, 0:512],
                        start=True, stop=True, skip_group_check=True),
                        waits=[(Cact, evac_m), (Cdve, rope_prev["t1"])])
                    p.add("tensor", lambda e: e.matmul(
                        psC[:, 512:1024], p64_sb[:, :], k0_sb[:, 512:1024],
                        start=True, stop=True, skip_group_check=True),
                        incs=[(Cpe, 1)])
                    emit_rope_finish(kind, h, half, evac_m, ropetab_ready)
        if DEBUG_TAPS:
            p.add("sync", lambda e: e.dma_start(
                out=av_dbg[:, :], in_=av_sb[0:10, :]),
                waits=[(Cact, Cact.n)], incs=[(Cst, 16)])
            p.add("sync", lambda e: e.dma_start(
                out=ak_dbg[:, :], in_=ak_sb.ap().rearrange("p a b -> p (a b)")),
                waits=[(Cdve, Cdve.n)], incs=[(Cst, 16)])
        p1_stores = Cst.n
        p1_pe_done = Cpe.n
        p1_dve_done = Cdve.n

        head_load_marks = {}
        head_end_pe = {}
        head_free = {}
        dq_wn_store = {}

        def emit_head_loads(h):
            sl = h % 2
            w = [(Cst, p1_stores)] if h < 2 else [(Cpe, head_end_pe[h - 2])]
            p.add("sync", lambda e, h=h, sl=sl: e.dma_start(
                out=kf2[sl], in_=kf_d[128 * h:128 * (h + 1), :]),
                waits=w, incs=[(CsyL, 16)])
            p.add("sync", lambda e, h=h, sl=sl: e.dma_start(
                out=qf2[sl], in_=qf_d[128 * h:128 * (h + 1), :]), incs=[(CsyL, 16)])
            p.add("sync", lambda e, h=h, sl=sl: e.dma_start(
                out=v2[sl], in_=v_d[:, 128 * h:128 * (h + 1)].rearrange(
                    "(kc tp) d -> tp kc d", tp=128)), incs=[(CsyL, 16)])
            head_load_marks[h] = CsyL.n

        emit_head_loads(0)
        emit_head_loads(1)

        for h in range(H):
            sl = h % 2
            if h == 0:
                p.add("gpsimd", lambda e: e.dma_start(
                    out=B0[:, :, :],
                    in_=wpT[:, 0:512].rearrange("(c p) n -> p c n", p=128)),
                    waits=[(Cpe, p1_pe_done)], incs=[(CgpL, 16)])
                wp0_ld = CgpL.n

            exp_marks = {}
            pv_marks = {}

            def bank_segs(qlo):
                return [(qlo, 512), (512, 1024)] if qlo < 512 else [(qlo, 1024)]

            def emit_sum_pv(cp):
                cqp = cp % 8
                qlop = 128 * cqp
                slp = cp % 2
                sgl = bank_segs(qlop)
                for sg, (s0, s1) in enumerate(sgl):
                    w = []
                    if sg == 0:
                        w = [(Cact, exp_marks[cp])]
                        if cp == 0 and h == 0:
                            w.append((Cdve, p1_dve_done))
                    p.add("tensor", lambda e, s0=s0, s1=s1, slp=slp, cp=cp: e.matmul(
                        psD[0:1, s0:s1], ones128[:, :], exps[slp][:, s0:s1],
                        start=(cp == 0), stop=(cp == 15), skip_group_check=True),
                        waits=w)
                for sg, (s0, s1) in enumerate(sgl):
                    w = []
                    if cp == 0 and sg == 0 and h >= 1:
                        w = [(Cdve, head_free[h - 1][2])]
                    p.add("tensor", lambda e, s0=s0, s1=s1, slp=slp, cp=cp, sl=sl: e.matmul(
                        psC[:, s0:s1], v2[sl][:, cp, :], exps[slp][:, s0:s1],
                        start=(cp == 0), stop=(cp == 15), skip_group_check=True),
                        waits=w,
                        incs=([(Cpe, 1)] if sg == len(sgl) - 1 else []))
                pv_marks[cp] = Cpe.n

            for c in range(16):
                cq = c % 8
                qlo = 128 * cq
                width = 1024 - qlo
                slot = c % 2
                pss = psAB[slot]
                w = []
                if c == 0:
                    w = [(CsyL, head_load_marks[h])]
                    if h == 0:
                        w.append((Cdve, p1_dve_done))
                    else:
                        w.append((Cdve, head_free[h - 1][0]))
                elif c == 1 and h >= 1:
                    w = [(Cact, head_free[h - 1][1])]
                elif c >= 2:
                    w = [(Cact, exp_marks[c - 2])]
                sgl = bank_segs(qlo)
                for sg, (s0, s1) in enumerate(sgl):
                    p.add("tensor", lambda e, pss=pss, c=c, s0=s0, s1=s1, sl=sl: e.matmul(
                        pss[:, s0:s1], kf2[sl][:, 128 * c:128 * (c + 1)],
                        qf2[sl][:, s0:s1], start=True, stop=True, skip_group_check=True),
                        waits=(w if sg == 0 else []),
                        incs=([(Cpe, 1)] if sg == len(sgl) - 1 else []))
                sc_m = Cpe.n
                msk = dm_sb if c < 8 else xm_sb
                p.add("vector", lambda e, pss=pss, qlo=qlo, msk=msk: e.tensor_tensor(
                    pss[:, qlo:qlo + 128], pss[:, qlo:qlo + 128], msk[:, :], ALU.add),
                    waits=[(Cpe, sc_m)], incs=[(Cdve, 1)])
                mk_m = Cdve.n
                ew = [(Cdve, mk_m)]
                if c >= 2:
                    ew.append((Cpe, pv_marks[c - 2]))
                p.add("scalar", lambda e, pss=pss, qlo=qlo, width=width, slot=slot: e.activation(
                    exps[slot][:, qlo:qlo + width], pss[:, qlo:qlo + width], AF.Exp,
                    scale=RS128, bias=cst_sb[:, 1:2]),
                    waits=ew, incs=[(Cact, 1)])
                exp_marks[c] = Cact.n
                if c >= 1:
                    emit_sum_pv(c - 1)
            emit_sum_pv(15)
            pv_done = Cpe.n

            p.add("scalar", lambda e: e.copy(srow[:, :], psD[0:1, :]),
                  waits=[(Cpe, pv_done)], incs=[(Cact, 1)])
            srow_m = Cact.n
            for sg in range(2):
                p.add("tensor", lambda e, sg=sg, sl=sl, h=h: e.matmul(
                    psA[0:10, 512 * sg:512 * (sg + 1)], ak_sb[:, h, :],
                    qf2[sl][:, 512 * sg:512 * (sg + 1)], start=True, stop=True,
                    skip_group_check=True),
                    waits=([(Cact, exp_marks[14])] if sg == 0 else []),
                    incs=([(Cpe, 1)] if sg == 1 else []))
            asc_m = Cpe.n
            p.add("scalar", lambda e: e.activation(
                expa[:, :], psA[0:10, :], AF.Exp, scale=1.0 / 64.0),
                waits=[(Cpe, asc_m)], incs=[(Cact, 1)])
            expa_m = Cact.n
            for sg in range(2):
                p.add("tensor", lambda e, sg=sg: e.matmul(
                    psD[0:1, 512 * sg:512 * (sg + 1)], ones10[:, :],
                    expa[:, 512 * sg:512 * (sg + 1)], start=True, stop=True,
                    skip_group_check=True),
                    waits=([(Cact, srow_m), (Cact, expa_m)] if sg == 0 else []),
                    incs=([(Cpe, 1)] if sg == 1 else []))
            asum_m = Cpe.n
            p.add("scalar", lambda e: e.copy(sarow[:, :], psD[0:1, :]),
                  waits=[(Cpe, asum_m)], incs=[(Cact, 1)])
            sae_m = Cact.n
            p.add("scalar", lambda e: e.activation(sarow[:, :], sarow[:, :], AF.Ln),
                  waits=[(Cact, sae_m)], incs=[(Cact, 1)])
            saln_m = Cact.n
            p.add("scalar", lambda e: e.activation(sarow[:, :], sarow[:, :], AF.Exp,
                                                   scale=-1.0),
                  waits=[(Cact, saln_m)], incs=[(Cact, 1)])
            sa_m = Cact.n
            p.add("vector", lambda e: e.tensor_scalar(
                sarow[:, :], sarow[:, :], g_sb[:, :], None, ALU.mult),
                waits=[(Cact, sa_m)], incs=[(Cdve, 1)])
            f_m = Cdve.n
            for sg in range(2):
                p.add("tensor", lambda e, sg=sg: e.matmul(
                    psB[0:10, 512 * sg:512 * (sg + 1)], o110[:, :],
                    sarow[:, 512 * sg:512 * (sg + 1)], start=True, stop=True,
                    skip_group_check=True),
                    waits=([(Cact, exp_marks[15]), (Cdve, f_m)] if sg == 0 else []),
                    incs=([(Cpe, 1)] if sg == 1 else []))
            fb_m = Cpe.n
            p.add("vector", lambda e: e.tensor_tensor(
                pa_sb[:, :], expa[:, :], psB[0:10, :], ALU.mult),
                waits=[(Cpe, fb_m)], incs=[(Cdve, 1)])
            pa_m = Cdve.n
            for sg in range(2):
                p.add("tensor", lambda e, sg=sg, h=h: e.matmul(
                    psA[:, 512 * sg:512 * (sg + 1)], av_sb[0:10, 128 * h:128 * (h + 1)],
                    pa_sb[:, 512 * sg:512 * (sg + 1)], start=True, stop=True,
                    skip_group_check=True),
                    waits=([(Cdve, pa_m)] if sg == 0 else []),
                    incs=([(Cpe, 1)] if sg == 1 else []))
            av_m = Cpe.n
            p.add("scalar", lambda e: e.activation(srow[:, :], srow[:, :], AF.Ln),
                  incs=[(Cact, 1)])
            srln_m = Cact.n
            p.add("scalar", lambda e: e.activation(invrow[:, :], srow[:, :], AF.Exp,
                                                   scale=-1.0),
                  waits=[(Cact, srln_m)], incs=[(Cact, 1)])
            inv_m = Cact.n
            for sg in range(2):
                p.add("tensor", lambda e, sg=sg: e.matmul(
                    psB[:, 512 * sg:512 * (sg + 1)], o1128[:, :],
                    invrow[:, 512 * sg:512 * (sg + 1)], start=True, stop=True,
                    skip_group_check=True),
                    waits=([(Cact, inv_m), (Cdve, pa_m)] if sg == 0 else []),
                    incs=([(Cpe, 1)] if sg == 1 else []))
            invb_pe = Cpe.n
            p.add("scalar", lambda e: e.copy(invb[:, :], psB[:, :]),
                  waits=[(Cpe, invb_pe)], incs=[(Cact, 1)])
            invb_m = Cact.n
            p.add("vector", lambda e, h=h: e.tensor_tensor(
                BIGA[:, h, :], psC[:, :], invb[:, :], ALU.mult),
                waits=[(Cact, invb_m), (Cpe, pv_done)], incs=[(Cdve, 1)])
            y1_m = Cdve.n
            p.add("vector", lambda e, h=h: e.tensor_tensor(
                BIGA[:, h, :], BIGA[:, h, :], psA[:, :], ALU.add),
                waits=[(Cpe, av_m), (Cdve, y1_m)], incs=[(Cdve, 1)])
            y2_m = Cdve.n
            if DEBUG_TAPS:
                p.add("sync", lambda e, h=h: e.dma_start(
                    out=y_dbg[128 * h:128 * (h + 1), :], in_=BIGA[:, h, :]),
                    waits=[(Cdve, y2_m)], incs=[(Cst, 16)])
                p.add("sync", lambda e, h=h: e.dma_start(
                    out=sums_dbg[h:h + 1, 0:1024], in_=srow[:, :]),
                    waits=[(Cact, inv_m)], incs=[(Cst, 16)])
                p.add("gpsimd", lambda e, h=h: e.dma_start(
                    out=sums_dbg[h:h + 1, 1024:2048], in_=sarow[:, :]),
                    waits=[(Cdve, f_m)], incs=[(CgpL, 16)])
            head_free[h] = (y2_m, invb_m, y1_m)
            head_end_pe[h] = Cpe.n
            if h + 2 < H:
                emit_head_loads(h + 2)

            oc = h
            dsl = oc % 2
            w = [(CgpL, sczr_ready)]
            if oc < 2:
                w.append((Cst, xn1_stored))
                w.append((Cdve, p0_dve_done))
            prev = dq_wn_store.get(oc - 2)
            if prev is not None:
                w.append((Cst, prev))
            p.add("gpsimd", lambda e, oc=oc, dsl=dsl: e.dma_start(
                out=qw_sl[dsl], in_=qw[128 * oc:128 * (oc + 1), :]),
                waits=w, incs=[(CgpL, 16)])
            qw_ld = CgpL.n
            for g in range(32):
                eng = "vector" if g % 2 == 0 else "gpsimd"
                p.add(eng, lambda e, g=g, oc=oc, dsl=dsl: e.tensor_scalar(
                    wn_sl[dsl][:, 128 * g:128 * (g + 1)],
                    qw_sl[dsl][:, 128 * g:128 * (g + 1)],
                    sc_sb[:, 32 * oc + g:32 * oc + g + 1],
                    zr_sb[:, 32 * oc + g:32 * oc + g + 1],
                    ALU.mult, ALU.add),
                    waits=([(CgpL, qw_ld), (CgpL, sczr_ready)] if g in (0, 1) else []),
                    incs=([(Cdve, 1)] if g == 30 else ([(Cpool, 1)] if g == 31 else [])))
            dq_d, dq_p = Cdve.n, Cpool.n
            p.add("sync", lambda e, oc=oc, dsl=dsl: e.dma_start(
                out=wnat_d[128 * oc:128 * (oc + 1), :], in_=wn_sl[dsl]),
                waits=[(Cdve, dq_d), (Cpool, dq_p)], incs=[(Cst, 16)])
            dq_wn_store[oc] = Cst.n
        wnat_all = Cst.n
        p2_done_pe = Cpe.n
        p2_done_act = Cact.n
        p2_done_dve = Cdve.n

        proj_mm_marks = {}
        ape_store_marks = {}
        ev_prev = {}
        for e8 in range(8):
            wsl = B0 if e8 % 2 == 0 else B1
            if e8 > 0:
                w = ([(Cpe, p2_done_pe), (Cact, p2_done_act), (Cdve, p2_done_dve)]
                     if e8 == 1 else [(Cpe, proj_mm_marks[e8 - 2])])
                p.add("gpsimd", lambda e, e8=e8, wsl=wsl: e.dma_start(
                    out=wsl[:, :, :],
                    in_=wpT[:, 512 * e8:512 * (e8 + 1)].rearrange("(c p) n -> p c n", p=128)),
                    waits=w, incs=[(CgpL, 16)])
                wld = CgpL.n
            else:
                wld = wp0_ld
            for tc in range(8):
                ps = psAB[tc % 2]
                ii = e8 * 8 + tc
                for cch in range(32):
                    w = []
                    if cch == 0:
                        w = [(CgpL, wld)]
                        if e8 == 0 and tc == 0:
                            w.append((Cdve, p2_done_dve))
                        prev = ev_prev.get(ii - 2)
                        if prev is not None:
                            w.append((Cact, prev))
                    p.add("tensor", lambda e, ps=ps, cch=cch, tc=tc, wsl=wsl: e.matmul(
                        ps[:, 0:512], BIGA[:, cch, 128 * tc:128 * (tc + 1)],
                        wsl[:, cch, :], start=(cch == 0), stop=(cch == 31),
                        skip_group_check=True),
                        waits=w, incs=([(Cpe, 1)] if cch == 31 else []))
                mm_m = Cpe.n
                w = [(Cpe, mm_m)]
                prev = ape_store_marks.get(ii - 2)
                if prev is not None:
                    w.append((Cst, prev))
                p.add("scalar", lambda e, tc=tc, ps=ps: e.copy(
                    ape_sb[:, tc % 2, :], ps[:, 0:512]),
                    waits=w, incs=[(Cact, 1)])
                ev_prev[ii] = Cact.n
                p.add("sync", lambda e, tc=tc, e8=e8: e.dma_start(
                    out=attn_d[128 * tc:128 * (tc + 1), 512 * e8:512 * (e8 + 1)],
                    in_=ape_sb[:, tc % 2, :]),
                    waits=[(Cact, Cact.n)], incs=[(Cst, 16)])
                ape_store_marks[ii] = Cst.n
            proj_mm_marks[e8] = Cpe.n
        attn_stored = Cst.n

        p4_marks = []
        p4_x2m = {}
        for tc in range(8):
            w = [(Cst, attn_stored), (Cst, wnat_all)] if tc == 0 else [(Cst, p4_marks[tc - 1])]
            p.add("sync", lambda e, tc=tc: e.dma_start(
                out=at_ap, in_=attn_d[128 * tc:128 * (tc + 1), :]),
                waits=w, incs=[(CsyL, 16)])
            wg = [(Cst, wnat_all)] if tc == 0 else [(Cdve, p4_x2m[tc - 1])]
            p.add("gpsimd", lambda e, tc=tc: e.dma_start(
                out=xr_ap, in_=x_in[128 * tc:128 * (tc + 1), :]),
                waits=wg, incs=[(CgpL, 16)])
            ld_s, ld_g = CsyL.n, CgpL.n
            nm, rs, am = emit_ln_stats(at_ap, 0, p4dum, [(CsyL, ld_s), (CgpL, ld_g)])
            p.add("vector", lambda e, nm=nm, rs=rs: e.tensor_scalar(
                at_ap, at_ap, nm, rs, ALU.add, ALU.mult),
                waits=[(Cact, am)], incs=[(Cdve, 1)])
            u_m = Cdve.n
            p.add("vector", lambda e: e.tensor_tensor(at_ap, at_ap, xr_ap, ALU.add),
                  waits=[(Cdve, u_m)], incs=[(Cdve, 1)])
            x2_m = Cdve.n
            p4_x2m[tc] = x2_m
            p.add("sync", lambda e, tc=tc: e.dma_start(
                out=x2_d[128 * tc:128 * (tc + 1), :], in_=at_ap),
                waits=[(Cdve, x2_m)], incs=[(Cst, 16)])
            nm2, rs2, am2 = emit_ln_stats(at_ap, 0, p4dum, [(Cdve, x2_m)])
            p.add("vector", lambda e, nm2=nm2, rs2=rs2: e.tensor_scalar(
                DD[:, 0:C], at_ap, nm2, rs2, ALU.add, ALU.mult),
                waits=[(Cact, am2)], incs=[(Cdve, 1)])
            p.add("sync", lambda e, tc=tc: e.dma_start(
                out=xn2_d[128 * tc:128 * (tc + 1), :], in_=DD[:, 0:C]),
                waits=[(Cdve, Cdve.n)], incs=[(Cst, 16)])
            p4_marks.append(Cst.n)
        p4_stored = Cst.n

        for cch in range(32):
            w = [(Cst, p4_stored), (Cpe, proj_mm_marks[7])] if cch == 0 else []
            p.add("sync", lambda e, cch=cch: e.dma_start(
                out=BIGA[:, cch, 0:OWN],
                in_=xn2_d[:, 128 * cch:128 * (cch + 1)], transpose=True),
                waits=w, incs=[(CsyL, 16)])
        xn2f_ready = CsyL.n
        p.add("gpsimd", lambda e: e.dma_start(out=DD[0:1, 0:C], in_=brow[:, :]),
              waits=[(Cst, p4_stored)], incs=[(CgpL, 16)])
        brow_ld = CgpL.n
        ffn_mm_marks = {}
        ffe_store_marks = {}
        ev_prev = {}
        for e8 in range(8):
            wsl = B0 if e8 % 2 == 0 else B1
            w = ([(Cpe, proj_mm_marks[7]), (Cst, wnat_all)] if e8 < 2
                 else [(Cpe, ffn_mm_marks[e8 - 2]), (Cst, wnat_all)])
            for ic in range(32):
                p.add("sync", lambda e, e8=e8, ic=ic, wsl=wsl: e.dma_start(
                    out=wsl[:, ic, :],
                    in_=wnat_d[512 * e8:512 * (e8 + 1), 128 * ic:128 * (ic + 1)],
                    transpose=True),
                    waits=(w if ic == 0 else []), incs=[(CsyL, 16)])
            wld = CsyL.n
            for tc in range(8):
                ps = psAB[tc % 2]
                ii = e8 * 8 + tc
                w = [(CsyL, wld), (CsyL, xn2f_ready), (CgpL, brow_ld)]
                prev = ev_prev.get(ii - 2)
                if prev is not None:
                    w.append((Cdve, prev))
                p.add("tensor", lambda e, ps=ps, e8=e8: e.matmul(
                    ps[:, 0:512], o1128[:, :], DD[0:1, 512 * e8:512 * (e8 + 1)],
                    start=True, stop=False, skip_group_check=True), waits=w)
                for ic in range(32):
                    p.add("tensor", lambda e, ps=ps, ic=ic, tc=tc, wsl=wsl: e.matmul(
                        ps[:, 0:512], BIGA[:, ic, 128 * tc:128 * (tc + 1)],
                        wsl[:, ic, :], start=False, stop=(ic == 31),
                        skip_group_check=True),
                        incs=([(Cpe, 1)] if ic == 31 else []))
                mm_m = Cpe.n
                w = [(Cpe, mm_m)]
                prev = ffe_store_marks.get(ii - 2)
                if prev is not None:
                    w.append((Cst, prev))
                p.add("vector", lambda e, tc=tc, ps=ps: e.tensor_copy(
                    ape_sb[:, tc % 2, :], ps[:, 0:512]),
                    waits=w, incs=[(Cdve, 1)])
                ev_prev[ii] = Cdve.n
                p.add("scalar", lambda e, tc=tc, e8=e8: e.dma_start(
                    out=ffn_d[128 * tc:128 * (tc + 1), 512 * e8:512 * (e8 + 1)],
                    in_=ape_sb[:, tc % 2, :]),
                    waits=[(Cdve, Cdve.n)], incs=[(Cst, 16)])
                ffe_store_marks[ii] = Cst.n
            ffn_mm_marks[e8] = Cpe.n
        ffn_stored = Cst.n

        p6_marks = []
        for tc in range(8):
            w = [(Cst, ffn_stored)] if tc == 0 else [(Cout, p6_marks[tc - 1])]
            p.add("sync", lambda e, tc=tc: e.dma_start(
                out=ft_ap, in_=ffn_d[128 * tc:128 * (tc + 1), :]),
                waits=w, incs=[(CsyL, 16)])
            p.add("sync", lambda e, tc=tc: e.dma_start(
                out=x2r_ap, in_=x2_d[128 * tc:128 * (tc + 1), :]), incs=[(CsyL, 16)])
            ld = CsyL.n
            nm, rs, am = emit_ln_stats(ft_ap, 0, DD[:, 0:C], [(CsyL, ld)])
            p.add("vector", lambda e, nm=nm, rs=rs: e.tensor_scalar(
                ft_ap, ft_ap, nm, rs, ALU.add, ALU.mult),
                waits=[(Cact, am)], incs=[(Cdve, 1)])
            u6_m = Cdve.n
            p.add("vector", lambda e: e.tensor_tensor(ft_ap, ft_ap, x2r_ap, ALU.add),
                  waits=[(Cdve, u6_m)], incs=[(Cdve, 1)])
            p.add("sync", lambda e, tc=tc: e.dma_start(
                out=out[128 * tc:128 * (tc + 1), :], in_=ft_ap),
                waits=[(Cdve, Cdve.n)], incs=[(Cout, 16)])
            p6_marks.append(Cout.n)
        p.add("sync", lambda e: e.nop(), waits=[(Cout, Cout.n)])

        p.emit(block)
    return nc


def _host_prep(L0_attn_x, L0_attn_w_attn, L0_attn_w_proj, L0_attn_adapter_wte,
               L0_attn_gating, L0_ffn_qweight, L0_ffn_scales, L0_ffn_scaled_zeros,
               L0_ffn_bias):
    x = np.asarray(L0_attn_x, dtype=np.float32)
    wa = np.asarray(L0_attn_w_attn, dtype=np.float32)
    wp = np.asarray(L0_attn_w_proj, dtype=np.float32)
    aw = np.asarray(L0_attn_adapter_wte, dtype=np.float32)
    g = np.asarray(L0_attn_gating, dtype=np.float32)
    qwv = np.ascontiguousarray(np.asarray(L0_ffn_qweight, dtype=np.int32).reshape(C, C))
    sc = np.asarray(L0_ffn_scales, dtype=np.float32).reshape(C, 32)
    zr = np.asarray(L0_ffn_scaled_zeros, dtype=np.float32).reshape(C, 32)
    bi = np.ascontiguousarray(np.asarray(L0_ffn_bias, dtype=np.float32).reshape(1, C))

    wqT = np.ascontiguousarray(wa[0:C].T)
    wkT = np.ascontiguousarray(wa[C:2 * C].T)
    wvT = np.ascontiguousarray(wa[2 * C:3 * C].T)
    wpT = np.ascontiguousarray(wp.T)
    awT = np.ascontiguousarray(aw.T)
    scs = np.ascontiguousarray(sc.reshape(32, 128, 32).transpose(1, 0, 2).reshape(128, 1024))
    zrs = np.ascontiguousarray(zr.reshape(32, 128, 32).transpose(1, 0, 2).reshape(128, 1024))

    theta = 1.0 / (10000.0 ** (np.arange(0, HD, 2, dtype=np.float64) / HD))
    idx = np.outer(np.arange(T, dtype=np.float64), theta)
    idx2 = np.concatenate([idx, idx], axis=-1)
    cos_t = np.cos(idx2).astype(np.float32)
    sin_t = np.sin(idx2).astype(np.float32)
    sgn = np.where(np.arange(HD) < HD // 2, -1.0, 1.0).astype(np.float32)
    ssin_t = sin_t * sgn[None, :]
    p64m = np.zeros((HD, HD), dtype=np.float32)
    for d in range(HD):
        p64m[(d + 64) % 128, d] = 1.0
    dmask = np.where(np.arange(128)[:, None] <= np.arange(128)[None, :], 0.0,
                     NEG).astype(np.float32)

    perms = []
    for s in range(2):
        order = list(range(s, NB, 2)) + list(range(1 - s, NB, 2))
        perm = np.concatenate([np.arange(128 * gi, 128 * gi + 128) for gi in order])
        perms.append(perm)

    in_maps = []
    for c in range(8):
        b, s = c // 2, c % 2
        perm = perms[s]
        in_maps.append({
            "x": np.ascontiguousarray(x[b][perm]),
            "wqT": wqT, "wkT": wkT, "wvT": wvT, "wpT": wpT, "awT": awT,
            "gat": np.ascontiguousarray(g.reshape(1, 1)),
            "qw": qwv, "scs": scs, "zrs": zrs, "brow": bi,
            "cosf": np.ascontiguousarray(cos_t[perm].T).astype(BF),
            "ssinf": np.ascontiguousarray(ssin_t[perm].T).astype(BF),
            "p64": p64m.astype(BF),
            "dmask": dmask.astype(BF),
            "xmask": np.full((128, 128), NEG if s == 0 else 0.0, dtype=np.float32).astype(BF),
        })
    return in_maps, perms


def kernel(**inputs):
    in_maps, perms = _host_prep(**inputs)
    nc = build()
    res = run_bass_kernel_spmd(nc, in_maps, core_ids=list(range(8)))
    outf = np.empty((B, T, C), dtype=np.float32)
    for c in range(8):
        b, s = c // 2, c % 2
        outf[b][perms[s][0:OWN]] = np.asarray(res.results[c]["out"], dtype=np.float32)
    return outf


# version 07
# speedup vs baseline: 1.3926x; 1.0199x over previous
import sys

import numpy as np
import ml_dtypes

sys.path.insert(0, "/opt/trn_rl_repo")
import concourse.bass as bass
import concourse.mybir as mybir
from concourse.bass_utils import run_bass_kernel_spmd

F32 = mybir.dt.float32
BF16 = mybir.dt.bfloat16
I32 = mybir.dt.int32
AF = mybir.ActivationFunctionType
ALU = mybir.AluOpType
BF = ml_dtypes.bfloat16

B, T, C = 4, 2048, 4096
DEBUG_TAPS = False
H, HD = 32, 128
OWN = T // 2
NB = T // 128
EPS = 1e-5
EXP_BIAS = -10.0
NEG = -1.0e9
RS128 = 1.0 / float(np.sqrt(128.0))


class Ctr:

    def __init__(self, sem):
        self.sem = sem
        self.n = 0


class Prog:

    def __init__(self):
        self.ops = {k: [] for k in ("sync", "gpsimd", "vector", "scalar", "tensor")}

    def add(self, eng, fn, waits=(), incs=()):
        for c, v in waits:
            assert v <= c.n, f"wait on future value {v} > {c.n} ({eng})"
        post = []
        for c, v in incs:
            c.n += v
            post.append((c.sem, v))
        self.ops[eng].append((list(waits), fn, post))

    def emit(self, block):
        def mk(name):
            def inner(eng):
                for waits, fn, incs in self.ops[name]:
                    for c, v in waits:
                        eng.wait_ge(c.sem, v)
                    ins = fn(eng)
                    for sem, v in incs:
                        ins.then_inc(sem, v)
            return inner

        block.sync(mk("sync"))
        block.gpsimd(mk("gpsimd"))
        block.vector(mk("vector"))
        block.scalar(mk("scalar"))
        block.tensor(mk("tensor"))


def build():
    import contextlib

    nc = bass.Bass("TRN2", target_bir_lowering=False)

    x_in = nc.dram_tensor("x", [T, C], F32, kind="ExternalInput")
    wqT = nc.dram_tensor("wqT", [C, C], F32, kind="ExternalInput")
    wkT = nc.dram_tensor("wkT", [C, C], F32, kind="ExternalInput")
    wvT = nc.dram_tensor("wvT", [C, C], F32, kind="ExternalInput")
    wpT = nc.dram_tensor("wpT", [C, C], F32, kind="ExternalInput")
    awT = nc.dram_tensor("awT", [C, 10], F32, kind="ExternalInput")
    gat = nc.dram_tensor("gat", [1, 1], F32, kind="ExternalInput")
    qw = nc.dram_tensor("qw", [C, C], I32, kind="ExternalInput")
    scs = nc.dram_tensor("scs", [128, 1024], F32, kind="ExternalInput")
    zrs = nc.dram_tensor("zrs", [128, 1024], F32, kind="ExternalInput")
    brow = nc.dram_tensor("brow", [1, C], F32, kind="ExternalInput")
    cosf = nc.dram_tensor("cosf", [HD, T], BF16, kind="ExternalInput")
    ssinf = nc.dram_tensor("ssinf", [HD, T], BF16, kind="ExternalInput")
    p64 = nc.dram_tensor("p64", [HD, HD], BF16, kind="ExternalInput")
    dmask = nc.dram_tensor("dmask", [128, 128], BF16, kind="ExternalInput")
    xmask = nc.dram_tensor("xmask", [128, 128], BF16, kind="ExternalInput")

    out = nc.dram_tensor("out", [OWN, C], F32, kind="ExternalOutput")

    KI = "ExternalOutput" if DEBUG_TAPS else "Internal"
    xn1_d = nc.dram_tensor("xn1_d", [T, C], BF16, kind=KI)
    kf_d = nc.dram_tensor("kf_d", [C, T], BF16, kind=KI)
    qf_d = nc.dram_tensor("qf_d", [C, OWN], BF16, kind=KI)
    v_d = nc.dram_tensor("v_d", [T, C], BF16, kind=KI)
    wnat_d = nc.dram_tensor("wnat_d", [C, C], BF16, kind=KI)
    attn_d = nc.dram_tensor("attn_d", [OWN, C], F32, kind=KI)
    x2_d = nc.dram_tensor("x2_d", [OWN, C], F32, kind=KI)
    xn2_d = nc.dram_tensor("xn2_d", [OWN, C], BF16, kind=KI)
    ffn_d = nc.dram_tensor("ffn_d", [OWN, C], F32, kind=KI)
    y_dbg = nc.dram_tensor("y_dbg", [C, OWN], BF16, kind=KI) if DEBUG_TAPS else None
    sums_dbg = nc.dram_tensor("sums_dbg", [H, 2048], F32, kind=KI) if DEBUG_TAPS else None
    av_dbg = nc.dram_tensor("av_dbg", [10, C], BF16, kind=KI) if DEBUG_TAPS else None
    ak_dbg = nc.dram_tensor("ak_dbg", [128, 32 * 10], BF16, kind=KI) if DEBUG_TAPS else None

    with contextlib.ExitStack() as ctx:
        def sbt(name, shape, dt):
            return ctx.enter_context(nc.sbuf_tensor(name, shape, dt))

        def pst(name, shape, dt=F32):
            return ctx.enter_context(nc.psum_tensor(name, shape, dt))

        BIGA = sbt("BIGA", [128, 32, 1024], BF16)
        B0 = sbt("B0", [128, 32, 512], BF16)
        B1 = sbt("B1", [128, 32, 512], BF16)
        CC = sbt("CC", [128, 8192], F32)
        DD = sbt("DD", [128, 4096], BF16)
        cos_sb = sbt("cos_sb", [HD, 1024], BF16)
        ssin_sb = sbt("ssin_sb", [HD, 1024], BF16)
        p64_sb = sbt("p64_sb", [HD, HD], BF16)
        sc_sb = sbt("sc_sb", [128, 1024], F32)
        zr_sb = sbt("zr_sb", [128, 1024], F32)
        dm_sb = sbt("dm_sb", [128, 128], BF16)
        xm_sb = sbt("xm_sb", [128, 128], BF16)
        ak_sb = sbt("ak_sb", [128, 32, 10], BF16)
        av_sb = sbt("av_sb", [10, C], BF16)
        pf_sb = sbt("pf_sb", [128, 32, 10], BF16)
        g_sb = sbt("g_sb", [1, 1], F32)
        ones128 = sbt("ones128", [128, 1], BF16)
        ones10 = sbt("ones10", [10, 1], BF16)
        o110 = sbt("o110", [1, 10], BF16)
        o1128 = sbt("o1128", [1, 128], BF16)
        srow = sbt("srow", [1, 1024], F32)
        sarow = sbt("sarow", [1, 1024], BF16)
        stat = sbt("stat", [128, 16], F32)
        cst_sb = sbt("cst_sb", [128, 2], F32)
        k0_sb = sbt("k0_sb", [128, 1024], BF16)
        t1_sb = sbt("t1_sb", [128, 1024], BF16)
        k1_sb = sbt("k1_sb", [128, 1024], BF16)
        ape_sb = sbt("ape_sb", [128, 2, 512], F32)
        ve_sb = ape_sb.ap().bitcast(BF16)[:, :, 0:512]

        b1f = B1.ap().rearrange("p a b -> p (a b)")
        kf2 = [b1f[:, 0:2048], b1f[:, 2048:4096]]
        qf2 = [b1f[:, 4096:5120], b1f[:, 5120:6144]]
        v2 = [b1f[:, 6144:8192].rearrange("p (kc d) -> p kc d", d=128),
              b1f[:, 8192:10240].rearrange("p (kc d) -> p kc d", d=128)]
        exps = [b1f[:, 10240:11264], b1f[:, 11264:12288]]
        invb = b1f[:, 12288:13312]
        pa_sb = b1f[0:10, 13312:14336]
        expa = b1f[0:10, 14336:15360]
        invrow = b1f[0:1, 15360:16384]

        ccb = CC.ap().bitcast(BF16)
        xt_sl = [ccb[:, 0:4096], ccb[:, 4096:8192]]
        nrm_sl = [ccb[:, 8192:12288], ccb[:, 12288:16384]]
        qw_sl = [ccb[:, 0:4096], ccb[:, 4096:8192]]
        wn_sl = [ccb[:, 8192:12288], ccb[:, 12288:16384]]
        at_ap = CC.ap()[:, 0:4096]
        xr_ap = ccb[:, 8192:12288]
        p4dum = ccb[:, 12288:16384]
        ft_ap = CC.ap()[:, 0:4096]
        x2r_ap = CC.ap()[:, 4096:8192]

        psA = pst("psA", [128, 1024])
        psB = pst("psB", [128, 1024])
        psC = pst("psC", [128, 1024])
        psD = pst("psD", [128, 1024])
        psAB = [psA, psB]

        block = ctx.enter_context(nc.Block())
        p = Prog()
        sems = [ctx.enter_context(nc.semaphore(f"S{i}")) for i in range(8)]
        CsyL = Ctr(sems[0])
        CgpL = Ctr(sems[1])
        Cst = Ctr(sems[2])
        Cpe = Ctr(sems[3])
        Cact = Ctr(sems[4])
        Cdve = Ctr(sems[5])
        Cpool = Ctr(sems[6])
        Cout = Ctr(sems[7])

        p.add("vector", lambda e: e.memset(ones128[:, :], 1.0))
        p.add("vector", lambda e: e.memset(ones10[:, :], 1.0))
        p.add("vector", lambda e: e.memset(o110[:, :], 1.0))
        p.add("vector", lambda e: e.memset(o1128[:, :], 1.0))
        p.add("vector", lambda e: e.memset(cst_sb[:, 0:1], EPS))
        p.add("vector", lambda e: e.memset(cst_sb[:, 1:2], EXP_BIAS), incs=[(Cdve, 1)])
        ones_ready = Cdve.n
        for dst, src in ((p64_sb, p64), (dm_sb, dmask), (xm_sb, xmask),
                         (g_sb, gat)):
            nd = len(dst.shape)
            p.add("sync", lambda e, d=dst, s=src, nd=nd: e.dma_start(
                out=d[(slice(None),) * nd], in_=s[(slice(None),) * nd]),
                incs=[(CsyL, 16)])
        consts_ready = CsyL.n
        p.add("gpsimd", lambda e: e.dma_start(
            out=pf_sb[:, :, :], in_=awT[:, :].rearrange("(c p) j -> p c j", p=128)),
            incs=[(CgpL, 16)])
        p.add("gpsimd", lambda e: e.dma_start(out=sc_sb[:, :], in_=scs[:, :]),
              incs=[(CgpL, 16)])
        p.add("gpsimd", lambda e: e.dma_start(out=zr_sb[:, :], in_=zrs[:, :]),
              incs=[(CgpL, 16)])
        sczr_ready = CgpL.n
        pf_ready = CgpL.n

        def emit_ln_stats(src_ap, sc0, dummy_ap, pre_waits):
            p.add("vector", lambda e: e.reduce_sum(
                stat[:, sc0 + 5:sc0 + 6], src_ap, mybir.AxisListType.X),
                waits=pre_waits)
            p.add("vector", lambda e: e.scalar_tensor_tensor(
                out=dummy_ap, in0=src_ap, scalar=1.0, in1=src_ap,
                op0=ALU.mult, op1=ALU.mult, accum_out=stat[:, sc0 + 1:sc0 + 2]))
            p.add("vector", lambda e: e.tensor_scalar(
                stat[:, sc0:sc0 + 1], stat[:, sc0 + 5:sc0 + 6], -1.0 / C, None, ALU.mult))
            p.add("vector", lambda e: e.tensor_scalar(
                stat[:, sc0 + 3:sc0 + 4], stat[:, sc0 + 1:sc0 + 2], 1.0 / C, None,
                ALU.mult))
            p.add("vector", lambda e: e.tensor_tensor(
                stat[:, sc0 + 2:sc0 + 3], stat[:, sc0:sc0 + 1], stat[:, sc0:sc0 + 1],
                ALU.mult), incs=[(Cdve, 1)])
            m2m = Cdve.n
            p.add("vector", lambda e: e.tensor_tensor(
                stat[:, sc0 + 3:sc0 + 4], stat[:, sc0 + 3:sc0 + 4],
                stat[:, sc0 + 2:sc0 + 3], ALU.subtract),
                waits=[(Cdve, m2m)], incs=[(Cdve, 1)])
            dm = Cdve.n
            p.add("scalar", lambda e: e.activation(
                stat[:, sc0 + 4:sc0 + 5], stat[:, sc0 + 3:sc0 + 4], AF.Ln,
                bias=cst_sb[:, 0:1]), waits=[(Cdve, dm)], incs=[(Cact, 1)])
            lnm = Cact.n
            p.add("scalar", lambda e: e.activation(
                stat[:, sc0 + 4:sc0 + 5], stat[:, sc0 + 4:sc0 + 5], AF.Exp, scale=-0.5),
                waits=[(Cact, lnm)], incs=[(Cact, 1)])
            return stat[:, sc0:sc0 + 1], stat[:, sc0 + 4:sc0 + 5], Cact.n

        p0_norm_marks = {}
        p0_store_marks = {}
        for i in range(16):
            sl = i % 2
            w = []
            if i >= 2:
                w.append((Cdve, p0_norm_marks[i - 2]))
            p.add("gpsimd", lambda e, i=i, sl=sl: e.dma_start(
                out=xt_sl[sl], in_=x_in[128 * i:128 * (i + 1), :]),
                waits=w, incs=[(CgpL, 16)])
            ld = CgpL.n
            nm, rs, am = emit_ln_stats(xt_sl[sl], 8 * sl, DD[:, 0:C], [(CgpL, ld)])
            w = [(Cact, am)]
            if i >= 2:
                w.append((Cst, p0_store_marks[i - 2]))
            p.add("vector", lambda e, sl=sl, nm=nm, rs=rs: e.tensor_scalar(
                nrm_sl[sl], xt_sl[sl], nm, rs, ALU.add, ALU.mult),
                waits=w, incs=[(Cdve, 1)])
            p0_norm_marks[i] = Cdve.n
            p.add("sync", lambda e, i=i, sl=sl: e.dma_start(
                out=xn1_d[128 * i:128 * (i + 1), :], in_=nrm_sl[sl]),
                waits=[(Cdve, Cdve.n)], incs=[(Cst, 16)])
            p0_store_marks[i] = Cst.n
        xn1_stored = Cst.n
        p0_dve_done = Cdve.n

        kq_evac_marks = {}
        rope_prev = {"t1": 0, "pl": 0, "st": 0}
        ve_store_marks = {}

        def emit_rope_finish(kind, h, half, evac_m, tab_m):
            rot_m = Cpe.n
            prev_t1 = rope_prev["t1"]
            prev_pl = rope_prev["pl"]
            prev_st = rope_prev["st"]
            p.add("vector", lambda e: e.tensor_tensor(
                t1_sb[:, :], psC[:, :], ssin_sb[:, :], ALU.mult),
                waits=[(Cpe, rot_m), (Cpool, prev_pl), (CsyL, tab_m)], incs=[(Cdve, 1)])
            rope_prev["t1"] = Cdve.n
            p.add("gpsimd", lambda e: e.tensor_tensor(
                k1_sb[:, :], k0_sb[:, :], cos_sb[:, :], ALU.mult),
                waits=[(Cact, evac_m), (Cst, prev_st), (CsyL, tab_m)],
                incs=[(Cpool, 1)])
            t0_m = Cpool.n
            p.add("gpsimd", lambda e: e.tensor_tensor(
                k1_sb[:, :], k1_sb[:, :], t1_sb[:, :], ALU.add),
                waits=[(Cdve, rope_prev["t1"]), (Cpool, t0_m)], incs=[(Cpool, 1)])
            rope_prev["pl"] = Cpool.n
            if kind == "k":
                p.add("sync", lambda e, h=h, half=half: e.dma_start(
                    out=kf_d[128 * h:128 * (h + 1), 1024 * half:1024 * (half + 1)],
                    in_=k1_sb[:, :]),
                    waits=[(Cpool, Cpool.n)], incs=[(Cst, 16)])
            else:
                p.add("sync", lambda e, h=h: e.dma_start(
                    out=qf_d[128 * h:128 * (h + 1), 0:OWN], in_=k1_sb[:, :]),
                    waits=[(Cpool, Cpool.n)], incs=[(Cst, 16)])
            rope_prev["st"] = Cst.n

        av_evac_mark = {"m": 0}
        ak_evac_mark = {"m": 0}
        bslot_pe = {0: 0, 1: 0}
        for half in range(2):
            p.add("sync", lambda e, half=half: e.dma_start(
                out=cos_sb[:, :], in_=cosf[:, 1024 * half:1024 * (half + 1)]),
                waits=[(Cpool, rope_prev["pl"])], incs=[(CsyL, 16)])
            p.add("sync", lambda e, half=half: e.dma_start(
                out=ssin_sb[:, :], in_=ssinf[:, 1024 * half:1024 * (half + 1)]),
                waits=[(Cdve, rope_prev["t1"])], incs=[(CsyL, 16)])
            ropetab_ready = CsyL.n
            if half == 0:
                xw = [(Cst, xn1_stored), (CsyL, consts_ready)]
            else:
                xw = [(Cst, xn1_stored), (Cpe, Cpe.n)]
            for cch in range(32):
                p.add("sync", lambda e, cch=cch, half=half: e.dma_start(
                    out=BIGA[:, cch, :],
                    in_=xn1_d[1024 * half:1024 * (half + 1), 128 * cch:128 * (cch + 1)],
                    transpose=True),
                    waits=(xw if cch == 0 else []), incs=[(CsyL, 16)])
            xnf_ready = CsyL.n

            ev_marks = {}
            for e8 in range(8):
                wsl = B0 if e8 % 2 == 0 else B1
                wv_w = [(CgpL, pf_ready), (Cpe, bslot_pe[e8 % 2])]
                p.add("gpsimd", lambda e, e8=e8, wsl=wsl: e.dma_start(
                    out=wsl[:, :, :],
                    in_=wvT[:, 512 * e8:512 * (e8 + 1)].rearrange("(c p) n -> p c n", p=128)),
                    waits=wv_w, incs=[(CgpL, 16)])
                wv_ld = CgpL.n
                for tc in range(8):
                    ps = psAB[tc % 2]
                    ii = e8 * 8 + tc
                    for cch in range(32):
                        w = []
                        if cch == 0:
                            w = [(CgpL, wv_ld), (CsyL, xnf_ready)]
                            prev = ev_marks.get(ii - 2)
                            if prev is not None:
                                w.append((Cact, prev))
                        p.add("tensor", lambda e, ps=ps, cch=cch, tc=tc, wsl=wsl: e.matmul(
                            ps[:, 0:512], BIGA[:, cch, 128 * tc:128 * (tc + 1)],
                            wsl[:, cch, :], start=(cch == 0), stop=(cch == 31),
                            skip_group_check=True),
                            waits=w, incs=([(Cpe, 1)] if cch == 31 else []))
                        if half == 0 and tc == 0:
                            p.add("tensor", lambda e, cch=cch, wsl=wsl: e.matmul(
                                psC[0:10, 0:512], pf_sb[:, cch, :], wsl[:, cch, :],
                                start=(cch == 0), stop=(cch == 31), skip_group_check=True),
                                waits=([(Cact, av_evac_mark["m"]), (CgpL, pf_ready)]
                                       if cch == 0 else []),
                                incs=([(Cpe, 1)] if cch == 31 else []))
                    mm_m = Cpe.n
                    w = [(Cpe, mm_m)]
                    prev = ve_store_marks.get(ii - 2 + 64 * half)
                    if prev is not None:
                        w.append((Cst, prev))
                    p.add("scalar", lambda e, tc=tc, ps=ps: e.copy(
                        ve_sb[:, tc % 2, :], ps[:, 0:512]),
                        waits=w, incs=[(Cact, 1)])
                    p.add("sync", lambda e, tc=tc, e8=e8, half=half: e.dma_start(
                        out=v_d[1024 * half + 128 * tc:1024 * half + 128 * (tc + 1),
                                512 * e8:512 * (e8 + 1)],
                        in_=ve_sb[:, tc % 2, :]),
                        waits=[(Cact, Cact.n)], incs=[(Cst, 16)])
                    ev_marks[ii] = Cact.n
                    ve_store_marks[ii + 64 * half] = Cst.n
                bslot_pe[e8 % 2] = Cpe.n
                if half == 0:
                    p.add("scalar", lambda e, e8=e8: e.copy(
                        av_sb[0:10, 512 * e8:512 * (e8 + 1)], psC[0:10, 0:512]),
                        waits=[(Cpe, Cpe.n)], incs=[(Cact, 1)])
                    av_evac_mark["m"] = Cact.n

            for kind in (("k", "q") if half == 0 else ("k",)):
                wT_src = wkT if kind == "k" else wqT
                wld_marks = {}

                def emit_kq_load(h):
                    wsl = (B0 if h % 2 == 0 else B1)[:, :, 0:128]
                    p.add("gpsimd", lambda e, h=h, wsl=wsl, wT_src=wT_src: e.dma_start(
                        out=wsl, in_=wT_src[:, 128 * h:128 * (h + 1)].rearrange(
                            "(c p) n -> p c n", p=128)),
                        waits=[(Cpe, bslot_pe[h % 2])], incs=[(CgpL, 16)])
                    wld_marks[h] = CgpL.n

                emit_kq_load(0)
                for h in range(32):
                    wsl = (B0 if h % 2 == 0 else B1)[:, :, 0:128]
                    wld = wld_marks[h]
                    ps = psAB[h % 2]
                    for cch in range(32):
                        w = []
                        if cch == 0:
                            w = [(CgpL, wld)]
                            prev = kq_evac_marks.get((kind, half, h - 2))
                            if prev is not None:
                                w.append((Cact, prev))
                        for sg in range(2):
                            p.add("tensor", lambda e, ps=ps, cch=cch, sg=sg, wsl=wsl: e.matmul(
                                ps[:, 512 * sg:512 * (sg + 1)], wsl[:, cch, :],
                                BIGA[:, cch, 512 * sg:512 * (sg + 1)],
                                start=(cch == 0), stop=(cch == 31), skip_group_check=True),
                                waits=(w if sg == 0 else []),
                                incs=([(Cpe, 1)] if (cch == 31 and sg == 1) else []))
                        if kind == "k" and half == 0:
                            p.add("tensor", lambda e, cch=cch, wsl=wsl: e.matmul(
                                psD[:, 0:10], wsl[:, cch, :], pf_sb[:, cch, :],
                                start=(cch == 0), stop=(cch == 31), skip_group_check=True),
                                waits=([(Cdve, ak_evac_mark["m"])] if cch == 0 else []),
                                incs=([(Cpe, 1)] if cch == 31 else []))
                    mm_m = Cpe.n
                    bslot_pe[h % 2] = mm_m
                    if h + 1 < 32:
                        emit_kq_load(h + 1)
                    p.add("scalar", lambda e, ps=ps: e.copy(k0_sb[:, :], ps[:, :]),
                          waits=[(Cpe, mm_m), (Cpool, rope_prev["pl"])], incs=[(Cact, 1)])
                    kq_evac_marks[(kind, half, h)] = Cact.n
                    evac_m = Cact.n
                    if kind == "k" and half == 0:
                        p.add("vector", lambda e, h=h: e.tensor_copy(
                            ak_sb[:, h, :], psD[:, 0:10]),
                            waits=[(Cpe, mm_m)], incs=[(Cdve, 1)])
                        ak_evac_mark["m"] = Cdve.n
                    p.add("tensor", lambda e: e.matmul(
                        psC[:, 0:512], p64_sb[:, :], k0_sb[:, 0:512],
                        start=True, stop=True, skip_group_check=True),
                        waits=[(Cact, evac_m), (Cdve, rope_prev["t1"])])
                    p.add("tensor", lambda e: e.matmul(
                        psC[:, 512:1024], p64_sb[:, :], k0_sb[:, 512:1024],
                        start=True, stop=True, skip_group_check=True),
                        incs=[(Cpe, 1)])
                    emit_rope_finish(kind, h, half, evac_m, ropetab_ready)
        if DEBUG_TAPS:
            p.add("sync", lambda e: e.dma_start(
                out=av_dbg[:, :], in_=av_sb[0:10, :]),
                waits=[(Cact, Cact.n)], incs=[(Cst, 16)])
            p.add("sync", lambda e: e.dma_start(
                out=ak_dbg[:, :], in_=ak_sb.ap().rearrange("p a b -> p (a b)")),
                waits=[(Cdve, Cdve.n)], incs=[(Cst, 16)])
        p1_stores = Cst.n
        p1_pe_done = Cpe.n
        p1_dve_done = Cdve.n

        head_load_marks = {}
        head_end_pe = {}
        head_free = {}
        dq_wn_store = {}

        def emit_head_loads(h):
            sl = h % 2
            w = [(Cst, p1_stores)] if h < 2 else [(Cpe, head_end_pe[h - 2])]
            p.add("sync", lambda e, h=h, sl=sl: e.dma_start(
                out=kf2[sl], in_=kf_d[128 * h:128 * (h + 1), :]),
                waits=w, incs=[(CsyL, 16)])
            p.add("sync", lambda e, h=h, sl=sl: e.dma_start(
                out=qf2[sl], in_=qf_d[128 * h:128 * (h + 1), :]), incs=[(CsyL, 16)])
            p.add("sync", lambda e, h=h, sl=sl: e.dma_start(
                out=v2[sl], in_=v_d[:, 128 * h:128 * (h + 1)].rearrange(
                    "(kc tp) d -> tp kc d", tp=128)), incs=[(CsyL, 16)])
            head_load_marks[h] = CsyL.n

        emit_head_loads(0)
        emit_head_loads(1)
        prev_srow_m = {"m": 0}
        prev_pa_m = {"m": 0}

        for h in range(H):
            sl = h % 2
            if h == 0:
                p.add("gpsimd", lambda e: e.dma_start(
                    out=B0[:, :, :],
                    in_=wpT[:, 0:512].rearrange("(c p) n -> p c n", p=128)),
                    waits=[(Cpe, p1_pe_done)], incs=[(CgpL, 16)])
                wp0_ld = CgpL.n

            exp_marks = {}
            pv_marks = {}

            def bank_segs(qlo):
                return [(qlo, 512), (512, 1024)] if qlo < 512 else [(qlo, 1024)]

            def emit_sum_pv(cp):
                cqp = cp % 8
                qlop = 128 * cqp
                slp = cp % 2
                sgl = bank_segs(qlop)
                for sg, (s0, s1) in enumerate(sgl):
                    w = []
                    if sg == 0:
                        w = [(Cact, exp_marks[cp])]
                        if cp == 0:
                            w.append((Cact, sarow_evac_m))
                        if cp == 0 and h == 0:
                            w.append((Cdve, p1_dve_done))
                    p.add("tensor", lambda e, s0=s0, s1=s1, slp=slp, cp=cp: e.matmul(
                        psD[0:1, s0:s1], ones128[:, :], exps[slp][:, s0:s1],
                        start=(cp == 0), stop=(cp == 15), skip_group_check=True),
                        waits=w,
                        incs=([(Cpe, 1)] if (cp == 15 and sg == len(sgl) - 1) else []))
                for sg, (s0, s1) in enumerate(sgl):
                    w = []
                    if cp == 0 and sg == 0 and h >= 1:
                        w = [(Cdve, head_free[h - 1][2])]
                    p.add("tensor", lambda e, s0=s0, s1=s1, slp=slp, cp=cp, sl=sl: e.matmul(
                        psC[:, s0:s1], v2[sl][:, cp, :], exps[slp][:, s0:s1],
                        start=(cp == 0), stop=(cp == 15), skip_group_check=True),
                        waits=w,
                        incs=([(Cpe, 1)] if sg == len(sgl) - 1 else []))
                pv_marks[cp] = Cpe.n

            w = [(Cact, prev_srow_m["m"]), (Cdve, prev_pa_m["m"]),
                 (CsyL, head_load_marks[h])]
            if h == 0:
                w.append((Cdve, p1_dve_done))
            for sg in range(2):
                p.add("tensor", lambda e, sg=sg, sl=sl, h=h: e.matmul(
                    psD[0:10, 512 * sg:512 * (sg + 1)], ak_sb[:, h, :],
                    qf2[sl][:, 512 * sg:512 * (sg + 1)], start=True, stop=True,
                    skip_group_check=True),
                    waits=(w if sg == 0 else []),
                    incs=([(Cpe, 1)] if sg == 1 else []))
            asc_m = Cpe.n
            p.add("scalar", lambda e: e.activation(
                expa[:, :], psD[0:10, :], AF.Exp, scale=1.0 / 64.0),
                waits=[(Cpe, asc_m)], incs=[(Cact, 1)])
            expa_m = Cact.n
            for sg in range(2):
                p.add("tensor", lambda e, sg=sg: e.matmul(
                    psD[0:1, 512 * sg:512 * (sg + 1)], ones10[:, :],
                    expa[:, 512 * sg:512 * (sg + 1)], start=True, stop=True,
                    skip_group_check=True),
                    waits=([(Cact, expa_m)] if sg == 0 else []),
                    incs=([(Cpe, 1)] if sg == 1 else []))
            asum_m = Cpe.n
            p.add("scalar", lambda e: e.copy(sarow[:, :], psD[0:1, :]),
                  waits=[(Cpe, asum_m)], incs=[(Cact, 1)])
            sae_m = Cact.n
            sarow_evac_m = Cact.n

            for c in range(16):
                cq = c % 8
                qlo = 128 * cq
                width = 1024 - qlo
                slot = c % 2
                pss = psAB[slot]
                w = []
                if c == 0:
                    w = [(CsyL, head_load_marks[h])]
                    if h == 0:
                        w.append((Cdve, p1_dve_done))
                    else:
                        w.append((Cdve, head_free[h - 1][0]))
                elif c == 1 and h >= 1:
                    w = [(Cact, head_free[h - 1][1])]
                elif c >= 2:
                    w = [(Cact, exp_marks[c - 2])]
                sgl = bank_segs(qlo)
                for sg, (s0, s1) in enumerate(sgl):
                    p.add("tensor", lambda e, pss=pss, c=c, s0=s0, s1=s1, sl=sl: e.matmul(
                        pss[:, s0:s1], kf2[sl][:, 128 * c:128 * (c + 1)],
                        qf2[sl][:, s0:s1], start=True, stop=True, skip_group_check=True),
                        waits=(w if sg == 0 else []),
                        incs=([(Cpe, 1)] if sg == len(sgl) - 1 else []))
                sc_m = Cpe.n
                msk = dm_sb if c < 8 else xm_sb
                p.add("vector", lambda e, pss=pss, qlo=qlo, msk=msk: e.tensor_tensor(
                    pss[:, qlo:qlo + 128], pss[:, qlo:qlo + 128], msk[:, :], ALU.add),
                    waits=[(Cpe, sc_m)], incs=[(Cdve, 1)])
                mk_m = Cdve.n
                ew = [(Cdve, mk_m)]
                if c >= 2:
                    ew.append((Cpe, pv_marks[c - 2]))
                p.add("scalar", lambda e, pss=pss, qlo=qlo, width=width, slot=slot: e.activation(
                    exps[slot][:, qlo:qlo + width], pss[:, qlo:qlo + width], AF.Exp,
                    scale=RS128, bias=cst_sb[:, 1:2]),
                    waits=ew, incs=[(Cact, 1)])
                exp_marks[c] = Cact.n
                if c == 2:
                    p.add("scalar", lambda e: e.activation(
                        sarow[:, :], sarow[:, :], AF.Ln),
                        waits=[(Cact, sae_m)], incs=[(Cact, 1)])
                    saln_m = Cact.n
                    p.add("scalar", lambda e: e.activation(
                        sarow[:, :], sarow[:, :], AF.Exp, scale=-1.0),
                        waits=[(Cact, saln_m)], incs=[(Cact, 1)])
                    sa_m = Cact.n
                    p.add("gpsimd", lambda e: e.tensor_scalar(
                        sarow[:, :], sarow[:, :], g_sb[:, :], None, ALU.mult),
                        waits=[(Cact, sa_m)], incs=[(Cpool, 1)])
                    f_m = Cpool.n
                if c >= 1:
                    emit_sum_pv(c - 1)
            emit_sum_pv(15)
            pv_done = Cpe.n
            sum15_m = pv_marks[15] - 1

            p.add("scalar", lambda e: e.copy(srow[:, :], psD[0:1, :]),
                  waits=[(Cpe, pv_done)], incs=[(Cact, 1)])
            srow_m = Cact.n
            prev_srow_m["m"] = srow_m
            for sg in range(2):
                p.add("tensor", lambda e, sg=sg: e.matmul(
                    psB[0:10, 512 * sg:512 * (sg + 1)], o110[:, :],
                    sarow[:, 512 * sg:512 * (sg + 1)], start=True, stop=True,
                    skip_group_check=True),
                    waits=([(Cact, exp_marks[15]), (Cpool, f_m)] if sg == 0 else []),
                    incs=([(Cpe, 1)] if sg == 1 else []))
            fb_m = Cpe.n
            p.add("vector", lambda e: e.tensor_tensor(
                pa_sb[:, :], expa[:, :], psB[0:10, :], ALU.mult),
                waits=[(Cpe, fb_m)], incs=[(Cdve, 1)])
            pa_m = Cdve.n
            prev_pa_m["m"] = pa_m
            for sg in range(2):
                p.add("tensor", lambda e, sg=sg, h=h: e.matmul(
                    psA[:, 512 * sg:512 * (sg + 1)], av_sb[0:10, 128 * h:128 * (h + 1)],
                    pa_sb[:, 512 * sg:512 * (sg + 1)], start=True, stop=True,
                    skip_group_check=True),
                    waits=([(Cdve, pa_m), (Cact, exp_marks[14])] if sg == 0 else []),
                    incs=([(Cpe, 1)] if sg == 1 else []))
            av_m = Cpe.n
            p.add("scalar", lambda e: e.activation(srow[:, :], srow[:, :], AF.Ln),
                  incs=[(Cact, 1)])
            srln_m = Cact.n
            p.add("scalar", lambda e: e.activation(invrow[:, :], srow[:, :], AF.Exp,
                                                   scale=-1.0),
                  waits=[(Cact, srln_m)], incs=[(Cact, 1)])
            inv_m = Cact.n
            for sg in range(2):
                p.add("tensor", lambda e, sg=sg: e.matmul(
                    psB[:, 512 * sg:512 * (sg + 1)], o1128[:, :],
                    invrow[:, 512 * sg:512 * (sg + 1)], start=True, stop=True,
                    skip_group_check=True),
                    waits=([(Cact, inv_m), (Cdve, pa_m)] if sg == 0 else []),
                    incs=([(Cpe, 1)] if sg == 1 else []))
            invb_pe = Cpe.n
            p.add("scalar", lambda e: e.copy(invb[:, :], psB[:, :]),
                  waits=[(Cpe, invb_pe)], incs=[(Cact, 1)])
            invb_m = Cact.n
            p.add("vector", lambda e, h=h: e.tensor_tensor(
                BIGA[:, h, :], psC[:, :], invb[:, :], ALU.mult),
                waits=[(Cact, invb_m), (Cpe, pv_done)], incs=[(Cdve, 1)])
            y1_m = Cdve.n
            p.add("vector", lambda e, h=h: e.tensor_tensor(
                BIGA[:, h, :], BIGA[:, h, :], psA[:, :], ALU.add),
                waits=[(Cpe, av_m), (Cdve, y1_m)], incs=[(Cdve, 1)])
            y2_m = Cdve.n
            if DEBUG_TAPS:
                p.add("sync", lambda e, h=h: e.dma_start(
                    out=y_dbg[128 * h:128 * (h + 1), :], in_=BIGA[:, h, :]),
                    waits=[(Cdve, y2_m)], incs=[(Cst, 16)])
                p.add("sync", lambda e, h=h: e.dma_start(
                    out=sums_dbg[h:h + 1, 0:1024], in_=srow[:, :]),
                    waits=[(Cact, inv_m)], incs=[(Cst, 16)])
                p.add("gpsimd", lambda e, h=h: e.dma_start(
                    out=sums_dbg[h:h + 1, 1024:2048], in_=sarow[:, :]),
                    waits=[(Cdve, f_m)], incs=[(CgpL, 16)])
            head_free[h] = (y2_m, invb_m, y1_m)
            head_end_pe[h] = Cpe.n
            if h + 2 < H:
                emit_head_loads(h + 2)

            oc = h
            dsl = oc % 2
            w = [(CgpL, sczr_ready)]
            if oc < 2:
                w.append((Cst, xn1_stored))
                w.append((Cdve, p0_dve_done))
            prev = dq_wn_store.get(oc - 2)
            if prev is not None:
                w.append((Cst, prev))
            p.add("gpsimd", lambda e, oc=oc, dsl=dsl: e.dma_start(
                out=qw_sl[dsl], in_=qw[128 * oc:128 * (oc + 1), :]),
                waits=w, incs=[(CgpL, 16)])
            qw_ld = CgpL.n
            for g in range(32):
                eng = "vector" if g % 2 == 0 else "gpsimd"
                p.add(eng, lambda e, g=g, oc=oc, dsl=dsl: e.tensor_scalar(
                    wn_sl[dsl][:, 128 * g:128 * (g + 1)],
                    qw_sl[dsl][:, 128 * g:128 * (g + 1)],
                    sc_sb[:, 32 * oc + g:32 * oc + g + 1],
                    zr_sb[:, 32 * oc + g:32 * oc + g + 1],
                    ALU.mult, ALU.add),
                    waits=([(CgpL, qw_ld), (CgpL, sczr_ready)] if g in (0, 1) else []),
                    incs=([(Cdve, 1)] if g == 30 else ([(Cpool, 1)] if g == 31 else [])))
            dq_d, dq_p = Cdve.n, Cpool.n
            p.add("sync", lambda e, oc=oc, dsl=dsl: e.dma_start(
                out=wnat_d[128 * oc:128 * (oc + 1), :], in_=wn_sl[dsl]),
                waits=[(Cdve, dq_d), (Cpool, dq_p)], incs=[(Cst, 16)])
            dq_wn_store[oc] = Cst.n
        wnat_all = Cst.n
        p2_done_pe = Cpe.n
        p2_done_act = Cact.n
        p2_done_dve = Cdve.n

        proj_mm_marks = {}
        ape_store_marks = {}
        ev_prev = {}
        for e8 in range(8):
            wsl = B0 if e8 % 2 == 0 else B1
            if e8 > 0:
                w = ([(Cpe, p2_done_pe), (Cact, p2_done_act), (Cdve, p2_done_dve)]
                     if e8 == 1 else [(Cpe, proj_mm_marks[e8 - 2])])
                p.add("gpsimd", lambda e, e8=e8, wsl=wsl: e.dma_start(
                    out=wsl[:, :, :],
                    in_=wpT[:, 512 * e8:512 * (e8 + 1)].rearrange("(c p) n -> p c n", p=128)),
                    waits=w, incs=[(CgpL, 16)])
                wld = CgpL.n
            else:
                wld = wp0_ld
            for tc in range(8):
                ps = psAB[tc % 2]
                ii = e8 * 8 + tc
                for cch in range(32):
                    w = []
                    if cch == 0:
                        w = [(CgpL, wld)]
                        if e8 == 0 and tc == 0:
                            w.append((Cdve, p2_done_dve))
                        prev = ev_prev.get(ii - 2)
                        if prev is not None:
                            w.append((Cact, prev))
                    p.add("tensor", lambda e, ps=ps, cch=cch, tc=tc, wsl=wsl: e.matmul(
                        ps[:, 0:512], BIGA[:, cch, 128 * tc:128 * (tc + 1)],
                        wsl[:, cch, :], start=(cch == 0), stop=(cch == 31),
                        skip_group_check=True),
                        waits=w, incs=([(Cpe, 1)] if cch == 31 else []))
                mm_m = Cpe.n
                w = [(Cpe, mm_m)]
                prev = ape_store_marks.get(ii - 2)
                if prev is not None:
                    w.append((Cst, prev))
                p.add("scalar", lambda e, tc=tc, ps=ps: e.copy(
                    ape_sb[:, tc % 2, :], ps[:, 0:512]),
                    waits=w, incs=[(Cact, 1)])
                ev_prev[ii] = Cact.n
                p.add("sync", lambda e, tc=tc, e8=e8: e.dma_start(
                    out=attn_d[128 * tc:128 * (tc + 1), 512 * e8:512 * (e8 + 1)],
                    in_=ape_sb[:, tc % 2, :]),
                    waits=[(Cact, Cact.n)], incs=[(Cst, 16)])
                ape_store_marks[ii] = Cst.n
            proj_mm_marks[e8] = Cpe.n
        attn_stored = Cst.n

        p4_marks = []
        p4_x2m = {}
        for tc in range(8):
            w = [(Cst, attn_stored), (Cst, wnat_all)] if tc == 0 else [(Cst, p4_marks[tc - 1])]
            p.add("sync", lambda e, tc=tc: e.dma_start(
                out=at_ap, in_=attn_d[128 * tc:128 * (tc + 1), :]),
                waits=w, incs=[(CsyL, 16)])
            wg = [(Cst, wnat_all)] if tc == 0 else [(Cdve, p4_x2m[tc - 1])]
            p.add("gpsimd", lambda e, tc=tc: e.dma_start(
                out=xr_ap, in_=x_in[128 * tc:128 * (tc + 1), :]),
                waits=wg, incs=[(CgpL, 16)])
            ld_s, ld_g = CsyL.n, CgpL.n
            nm, rs, am = emit_ln_stats(at_ap, 0, p4dum, [(CsyL, ld_s), (CgpL, ld_g)])
            p.add("vector", lambda e, nm=nm, rs=rs: e.tensor_scalar(
                at_ap, at_ap, nm, rs, ALU.add, ALU.mult),
                waits=[(Cact, am)], incs=[(Cdve, 1)])
            u_m = Cdve.n
            p.add("vector", lambda e: e.tensor_tensor(at_ap, at_ap, xr_ap, ALU.add),
                  waits=[(Cdve, u_m)], incs=[(Cdve, 1)])
            x2_m = Cdve.n
            p4_x2m[tc] = x2_m
            p.add("sync", lambda e, tc=tc: e.dma_start(
                out=x2_d[128 * tc:128 * (tc + 1), :], in_=at_ap),
                waits=[(Cdve, x2_m)], incs=[(Cst, 16)])
            nm2, rs2, am2 = emit_ln_stats(at_ap, 0, p4dum, [(Cdve, x2_m)])
            p.add("vector", lambda e, nm2=nm2, rs2=rs2: e.tensor_scalar(
                DD[:, 0:C], at_ap, nm2, rs2, ALU.add, ALU.mult),
                waits=[(Cact, am2)], incs=[(Cdve, 1)])
            p.add("sync", lambda e, tc=tc: e.dma_start(
                out=xn2_d[128 * tc:128 * (tc + 1), :], in_=DD[:, 0:C]),
                waits=[(Cdve, Cdve.n)], incs=[(Cst, 16)])
            p4_marks.append(Cst.n)
        p4_stored = Cst.n

        for cch in range(32):
            w = [(Cst, p4_stored), (Cpe, proj_mm_marks[7])] if cch == 0 else []
            p.add("sync", lambda e, cch=cch: e.dma_start(
                out=BIGA[:, cch, 0:OWN],
                in_=xn2_d[:, 128 * cch:128 * (cch + 1)], transpose=True),
                waits=w, incs=[(CsyL, 16)])
        xn2f_ready = CsyL.n
        p.add("gpsimd", lambda e: e.dma_start(out=DD[0:1, 0:C], in_=brow[:, :]),
              waits=[(Cst, p4_stored)], incs=[(CgpL, 16)])
        brow_ld = CgpL.n
        ffn_mm_marks = {}
        ffe_store_marks = {}
        ev_prev = {}
        for e8 in range(8):
            wsl = B0 if e8 % 2 == 0 else B1
            w = ([(Cpe, proj_mm_marks[7]), (Cst, wnat_all)] if e8 < 2
                 else [(Cpe, ffn_mm_marks[e8 - 2]), (Cst, wnat_all)])
            for ic in range(32):
                p.add("sync", lambda e, e8=e8, ic=ic, wsl=wsl: e.dma_start(
                    out=wsl[:, ic, :],
                    in_=wnat_d[512 * e8:512 * (e8 + 1), 128 * ic:128 * (ic + 1)],
                    transpose=True),
                    waits=(w if ic == 0 else []), incs=[(CsyL, 16)])
            wld = CsyL.n
            for tc in range(8):
                ps = psAB[tc % 2]
                ii = e8 * 8 + tc
                w = [(CsyL, wld), (CsyL, xn2f_ready), (CgpL, brow_ld)]
                prev = ev_prev.get(ii - 2)
                if prev is not None:
                    w.append((Cdve, prev))
                p.add("tensor", lambda e, ps=ps, e8=e8: e.matmul(
                    ps[:, 0:512], o1128[:, :], DD[0:1, 512 * e8:512 * (e8 + 1)],
                    start=True, stop=False, skip_group_check=True), waits=w)
                for ic in range(32):
                    p.add("tensor", lambda e, ps=ps, ic=ic, tc=tc, wsl=wsl: e.matmul(
                        ps[:, 0:512], BIGA[:, ic, 128 * tc:128 * (tc + 1)],
                        wsl[:, ic, :], start=False, stop=(ic == 31),
                        skip_group_check=True),
                        incs=([(Cpe, 1)] if ic == 31 else []))
                mm_m = Cpe.n
                w = [(Cpe, mm_m)]
                prev = ffe_store_marks.get(ii - 2)
                if prev is not None:
                    w.append((Cst, prev))
                p.add("vector", lambda e, tc=tc, ps=ps: e.tensor_copy(
                    ape_sb[:, tc % 2, :], ps[:, 0:512]),
                    waits=w, incs=[(Cdve, 1)])
                ev_prev[ii] = Cdve.n
                p.add("scalar", lambda e, tc=tc, e8=e8: e.dma_start(
                    out=ffn_d[128 * tc:128 * (tc + 1), 512 * e8:512 * (e8 + 1)],
                    in_=ape_sb[:, tc % 2, :]),
                    waits=[(Cdve, Cdve.n)], incs=[(Cst, 16)])
                ffe_store_marks[ii] = Cst.n
            ffn_mm_marks[e8] = Cpe.n
        ffn_stored = Cst.n

        p6_marks = []
        for tc in range(8):
            w = [(Cst, ffn_stored)] if tc == 0 else [(Cout, p6_marks[tc - 1])]
            p.add("sync", lambda e, tc=tc: e.dma_start(
                out=ft_ap, in_=ffn_d[128 * tc:128 * (tc + 1), :]),
                waits=w, incs=[(CsyL, 16)])
            p.add("sync", lambda e, tc=tc: e.dma_start(
                out=x2r_ap, in_=x2_d[128 * tc:128 * (tc + 1), :]), incs=[(CsyL, 16)])
            ld = CsyL.n
            nm, rs, am = emit_ln_stats(ft_ap, 0, DD[:, 0:C], [(CsyL, ld)])
            p.add("vector", lambda e, nm=nm, rs=rs: e.tensor_scalar(
                ft_ap, ft_ap, nm, rs, ALU.add, ALU.mult),
                waits=[(Cact, am)], incs=[(Cdve, 1)])
            u6_m = Cdve.n
            p.add("vector", lambda e: e.tensor_tensor(ft_ap, ft_ap, x2r_ap, ALU.add),
                  waits=[(Cdve, u6_m)], incs=[(Cdve, 1)])
            p.add("sync", lambda e, tc=tc: e.dma_start(
                out=out[128 * tc:128 * (tc + 1), :], in_=ft_ap),
                waits=[(Cdve, Cdve.n)], incs=[(Cout, 16)])
            p6_marks.append(Cout.n)
        p.add("sync", lambda e: e.nop(), waits=[(Cout, Cout.n)])

        p.emit(block)
    return nc


def _host_prep(L0_attn_x, L0_attn_w_attn, L0_attn_w_proj, L0_attn_adapter_wte,
               L0_attn_gating, L0_ffn_qweight, L0_ffn_scales, L0_ffn_scaled_zeros,
               L0_ffn_bias):
    x = np.asarray(L0_attn_x, dtype=np.float32)
    wa = np.asarray(L0_attn_w_attn, dtype=np.float32)
    wp = np.asarray(L0_attn_w_proj, dtype=np.float32)
    aw = np.asarray(L0_attn_adapter_wte, dtype=np.float32)
    g = np.asarray(L0_attn_gating, dtype=np.float32)
    qwv = np.ascontiguousarray(np.asarray(L0_ffn_qweight, dtype=np.int32).reshape(C, C))
    sc = np.asarray(L0_ffn_scales, dtype=np.float32).reshape(C, 32)
    zr = np.asarray(L0_ffn_scaled_zeros, dtype=np.float32).reshape(C, 32)
    bi = np.ascontiguousarray(np.asarray(L0_ffn_bias, dtype=np.float32).reshape(1, C))

    wqT = np.ascontiguousarray(wa[0:C].T)
    wkT = np.ascontiguousarray(wa[C:2 * C].T)
    wvT = np.ascontiguousarray(wa[2 * C:3 * C].T)
    wpT = np.ascontiguousarray(wp.T)
    awT = np.ascontiguousarray(aw.T)
    scs = np.ascontiguousarray(sc.reshape(32, 128, 32).transpose(1, 0, 2).reshape(128, 1024))
    zrs = np.ascontiguousarray(zr.reshape(32, 128, 32).transpose(1, 0, 2).reshape(128, 1024))

    theta = 1.0 / (10000.0 ** (np.arange(0, HD, 2, dtype=np.float64) / HD))
    idx = np.outer(np.arange(T, dtype=np.float64), theta)
    idx2 = np.concatenate([idx, idx], axis=-1)
    cos_t = np.cos(idx2).astype(np.float32)
    sin_t = np.sin(idx2).astype(np.float32)
    sgn = np.where(np.arange(HD) < HD // 2, -1.0, 1.0).astype(np.float32)
    ssin_t = sin_t * sgn[None, :]
    p64m = np.zeros((HD, HD), dtype=np.float32)
    for d in range(HD):
        p64m[(d + 64) % 128, d] = 1.0
    dmask = np.where(np.arange(128)[:, None] <= np.arange(128)[None, :], 0.0,
                     NEG).astype(np.float32)

    perms = []
    for s in range(2):
        order = list(range(s, NB, 2)) + list(range(1 - s, NB, 2))
        perm = np.concatenate([np.arange(128 * gi, 128 * gi + 128) for gi in order])
        perms.append(perm)

    in_maps = []
    for c in range(8):
        b, s = c // 2, c % 2
        perm = perms[s]
        in_maps.append({
            "x": np.ascontiguousarray(x[b][perm]),
            "wqT": wqT, "wkT": wkT, "wvT": wvT, "wpT": wpT, "awT": awT,
            "gat": np.ascontiguousarray(g.reshape(1, 1)),
            "qw": qwv, "scs": scs, "zrs": zrs, "brow": bi,
            "cosf": np.ascontiguousarray(cos_t[perm].T).astype(BF),
            "ssinf": np.ascontiguousarray(ssin_t[perm].T).astype(BF),
            "p64": p64m.astype(BF),
            "dmask": dmask.astype(BF),
            "xmask": np.full((128, 128), NEG if s == 0 else 0.0, dtype=np.float32).astype(BF),
        })
    return in_maps, perms


def kernel(**inputs):
    in_maps, perms = _host_prep(**inputs)
    nc = build()
    res = run_bass_kernel_spmd(nc, in_maps, core_ids=list(range(8)))
    outf = np.empty((B, T, C), dtype=np.float32)
    for c in range(8):
        b, s = c // 2, c % 2
        outf[b][perms[s][0:OWN]] = np.asarray(res.results[c]["out"], dtype=np.float32)
    return outf


# version 08 (final)
# speedup vs baseline: 1.4299x; 1.0267x over previous
import sys

import numpy as np
import ml_dtypes

sys.path.insert(0, "/opt/trn_rl_repo")
import concourse.bass as bass
import concourse.mybir as mybir
from concourse.bass_utils import run_bass_kernel_spmd

F32 = mybir.dt.float32
BF16 = mybir.dt.bfloat16
I32 = mybir.dt.int32
AF = mybir.ActivationFunctionType
ALU = mybir.AluOpType
BF = ml_dtypes.bfloat16

B, T, C = 4, 2048, 4096
DEBUG_TAPS = False
H, HD = 32, 128
OWN = T // 2
NB = T // 128
EPS = 1e-5
EXP_BIAS = -10.0
NEG = -1.0e9
RS128 = 1.0 / float(np.sqrt(128.0))


class Ctr:

    def __init__(self, sem):
        self.sem = sem
        self.n = 0


class Prog:

    def __init__(self):
        self.ops = {k: [] for k in ("sync", "gpsimd", "vector", "scalar", "tensor")}

    def add(self, eng, fn, waits=(), incs=()):
        for c, v in waits:
            assert v <= c.n, f"wait on future value {v} > {c.n} ({eng})"
        post = []
        for c, v in incs:
            c.n += v
            post.append((c.sem, v))
        self.ops[eng].append((list(waits), fn, post))

    def emit(self, block):
        def mk(name):
            def inner(eng):
                for waits, fn, incs in self.ops[name]:
                    for c, v in waits:
                        eng.wait_ge(c.sem, v)
                    ins = fn(eng)
                    for sem, v in incs:
                        ins.then_inc(sem, v)
            return inner

        block.sync(mk("sync"))
        block.gpsimd(mk("gpsimd"))
        block.vector(mk("vector"))
        block.scalar(mk("scalar"))
        block.tensor(mk("tensor"))


def build():
    import contextlib

    nc = bass.Bass("TRN2", target_bir_lowering=False)

    x_in = nc.dram_tensor("x", [T, C], F32, kind="ExternalInput")
    wqT = nc.dram_tensor("wqT", [C, C], F32, kind="ExternalInput")
    wkT = nc.dram_tensor("wkT", [C, C], F32, kind="ExternalInput")
    wvT = nc.dram_tensor("wvT", [C, C], F32, kind="ExternalInput")
    wpT = nc.dram_tensor("wpT", [C, C], F32, kind="ExternalInput")
    awT = nc.dram_tensor("awT", [C, 10], F32, kind="ExternalInput")
    gat = nc.dram_tensor("gat", [1, 1], F32, kind="ExternalInput")
    qw = nc.dram_tensor("qw", [C, C], I32, kind="ExternalInput")
    scs = nc.dram_tensor("scs", [128, 1024], F32, kind="ExternalInput")
    zrs = nc.dram_tensor("zrs", [128, 1024], F32, kind="ExternalInput")
    brow = nc.dram_tensor("brow", [1, C], F32, kind="ExternalInput")
    cosf = nc.dram_tensor("cosf", [HD, T], BF16, kind="ExternalInput")
    ssinf = nc.dram_tensor("ssinf", [HD, T], BF16, kind="ExternalInput")
    p64 = nc.dram_tensor("p64", [HD, HD], BF16, kind="ExternalInput")
    dmask = nc.dram_tensor("dmask", [128, 128], BF16, kind="ExternalInput")
    xmask = nc.dram_tensor("xmask", [128, 128], BF16, kind="ExternalInput")

    out = nc.dram_tensor("out", [OWN, C], F32, kind="ExternalOutput")

    KI = "ExternalOutput" if DEBUG_TAPS else "Internal"
    xn1_d = nc.dram_tensor("xn1_d", [T, C], BF16, kind=KI)
    kf_d = nc.dram_tensor("kf_d", [C, T], BF16, kind=KI)
    qf_d = nc.dram_tensor("qf_d", [C, OWN], BF16, kind=KI)
    v_d = nc.dram_tensor("v_d", [T, C], BF16, kind=KI)
    wnat_d = nc.dram_tensor("wnat_d", [C, C], BF16, kind=KI)
    attn_d = nc.dram_tensor("attn_d", [OWN, C], F32, kind=KI)
    x2_d = nc.dram_tensor("x2_d", [OWN, C], F32, kind=KI)
    xn2_d = nc.dram_tensor("xn2_d", [OWN, C], BF16, kind=KI)
    ffn_d = nc.dram_tensor("ffn_d", [OWN, C], F32, kind=KI)
    y_dbg = nc.dram_tensor("y_dbg", [C, OWN], BF16, kind=KI) if DEBUG_TAPS else None
    sums_dbg = nc.dram_tensor("sums_dbg", [H, 2048], F32, kind=KI) if DEBUG_TAPS else None
    av_dbg = nc.dram_tensor("av_dbg", [10, C], BF16, kind=KI) if DEBUG_TAPS else None
    ak_dbg = nc.dram_tensor("ak_dbg", [128, 32 * 10], BF16, kind=KI) if DEBUG_TAPS else None

    with contextlib.ExitStack() as ctx:
        def sbt(name, shape, dt):
            return ctx.enter_context(nc.sbuf_tensor(name, shape, dt))

        def pst(name, shape, dt=F32):
            return ctx.enter_context(nc.psum_tensor(name, shape, dt))

        BIGA = sbt("BIGA", [128, 32, 1024], BF16)
        B0 = sbt("B0", [128, 32, 512], BF16)
        B1 = sbt("B1", [128, 32, 512], BF16)
        CC = sbt("CC", [128, 8192], F32)
        DD = sbt("DD", [128, 4096], BF16)
        cos_sb = sbt("cos_sb", [HD, 1024], BF16)
        ssin_sb = sbt("ssin_sb", [HD, 1024], BF16)
        p64_sb = sbt("p64_sb", [HD, HD], BF16)
        sc_sb = sbt("sc_sb", [128, 1024], F32)
        zr_sb = sbt("zr_sb", [128, 1024], F32)
        dm_sb = sbt("dm_sb", [128, 128], BF16)
        xm_sb = sbt("xm_sb", [128, 128], BF16)
        ak_sb = sbt("ak_sb", [128, 32, 10], BF16)
        av_sb = sbt("av_sb", [10, C], BF16)
        pf_sb = sbt("pf_sb", [128, 32, 10], BF16)
        g_sb = sbt("g_sb", [1, 1], F32)
        ones128 = sbt("ones128", [128, 1], BF16)
        ones10 = sbt("ones10", [10, 1], BF16)
        o110 = sbt("o110", [1, 10], BF16)
        o1128 = sbt("o1128", [1, 128], BF16)
        srow = sbt("srow", [1, 1024], F32)
        sarow = sbt("sarow", [1, 1024], BF16)
        stat = sbt("stat", [128, 16], F32)
        cst_sb = sbt("cst_sb", [128, 2], F32)
        k0_sb = sbt("k0_sb", [128, 1024], BF16)
        t1_sb = sbt("t1_sb", [128, 1024], BF16)
        k1_sb = sbt("k1_sb", [128, 1024], BF16)
        ape_sb = sbt("ape_sb", [128, 2, 512], F32)
        ve_sb = ape_sb.ap().bitcast(BF16)[:, :, 0:512]

        b1f = B1.ap().rearrange("p a b -> p (a b)")
        kf2 = [b1f[:, 0:2048], b1f[:, 2048:4096]]
        qf2 = [b1f[:, 4096:5120], b1f[:, 5120:6144]]
        v2 = [b1f[:, 6144:8192].rearrange("p (kc d) -> p kc d", d=128),
              b1f[:, 8192:10240].rearrange("p (kc d) -> p kc d", d=128)]
        exps = [b1f[:, 10240:11264], b1f[:, 11264:12288]]
        invb = b1f[:, 12288:13312]
        pa_sb = b1f[0:10, 13312:14336]
        expa = b1f[0:10, 14336:15360]
        invrow = b1f[0:1, 15360:16384]

        ccb = CC.ap().bitcast(BF16)
        xt_sl = [ccb[:, 0:4096], ccb[:, 4096:8192]]
        nrm_sl = [ccb[:, 8192:12288], ccb[:, 12288:16384]]
        qw_sl = [ccb[:, 0:4096], ccb[:, 4096:8192]]
        wn_sl = [ccb[:, 8192:12288], ccb[:, 12288:16384]]
        at_ap = CC.ap()[:, 0:4096]
        xr_ap = ccb[:, 8192:12288]
        p4dum = ccb[:, 12288:16384]
        ft_ap = CC.ap()[:, 0:4096]
        x2r_ap = CC.ap()[:, 4096:8192]

        psA = pst("psA", [128, 1024])
        psB = pst("psB", [128, 1024])
        psC = pst("psC", [128, 1024])
        psD = pst("psD", [128, 1024])
        psAB = [psA, psB]

        block = ctx.enter_context(nc.Block())
        p = Prog()
        sems = [ctx.enter_context(nc.semaphore(f"S{i}")) for i in range(8)]
        CsyL = Ctr(sems[0])
        CgpL = Ctr(sems[1])
        Cst = Ctr(sems[2])
        Cpe = Ctr(sems[3])
        Cact = Ctr(sems[4])
        Cdve = Ctr(sems[5])
        Cpool = Ctr(sems[6])
        Cout = Ctr(sems[7])

        p.add("vector", lambda e: e.memset(ones128[:, :], 1.0))
        p.add("vector", lambda e: e.memset(ones10[:, :], 1.0))
        p.add("vector", lambda e: e.memset(o110[:, :], 1.0))
        p.add("vector", lambda e: e.memset(o1128[:, :], 1.0))
        p.add("vector", lambda e: e.memset(cst_sb[:, 0:1], EPS))
        p.add("vector", lambda e: e.memset(cst_sb[:, 1:2], EXP_BIAS), incs=[(Cdve, 1)])
        ones_ready = Cdve.n
        for dst, src in ((p64_sb, p64), (dm_sb, dmask), (xm_sb, xmask),
                         (g_sb, gat)):
            nd = len(dst.shape)
            p.add("sync", lambda e, d=dst, s=src, nd=nd: e.dma_start(
                out=d[(slice(None),) * nd], in_=s[(slice(None),) * nd]),
                incs=[(CsyL, 16)])
        consts_ready = CsyL.n
        p.add("gpsimd", lambda e: e.dma_start(
            out=pf_sb[:, :, :], in_=awT[:, :].rearrange("(c p) j -> p c j", p=128)),
            incs=[(CgpL, 16)])
        p.add("gpsimd", lambda e: e.dma_start(out=sc_sb[:, :], in_=scs[:, :]),
              incs=[(CgpL, 16)])
        p.add("gpsimd", lambda e: e.dma_start(out=zr_sb[:, :], in_=zrs[:, :]),
              incs=[(CgpL, 16)])
        sczr_ready = CgpL.n
        pf_ready = CgpL.n

        def emit_ln_stats(src_ap, sc0, dummy_ap, pre_waits):
            p.add("vector", lambda e: e.reduce_sum(
                stat[:, sc0 + 5:sc0 + 6], src_ap, mybir.AxisListType.X),
                waits=pre_waits)
            p.add("vector", lambda e: e.scalar_tensor_tensor(
                out=dummy_ap, in0=src_ap, scalar=1.0, in1=src_ap,
                op0=ALU.mult, op1=ALU.mult, accum_out=stat[:, sc0 + 1:sc0 + 2]))
            p.add("vector", lambda e: e.tensor_scalar(
                stat[:, sc0:sc0 + 1], stat[:, sc0 + 5:sc0 + 6], -1.0 / C, None, ALU.mult))
            p.add("vector", lambda e: e.tensor_scalar(
                stat[:, sc0 + 3:sc0 + 4], stat[:, sc0 + 1:sc0 + 2], 1.0 / C, None,
                ALU.mult))
            p.add("vector", lambda e: e.tensor_tensor(
                stat[:, sc0 + 2:sc0 + 3], stat[:, sc0:sc0 + 1], stat[:, sc0:sc0 + 1],
                ALU.mult), incs=[(Cdve, 1)])
            m2m = Cdve.n
            p.add("vector", lambda e: e.tensor_tensor(
                stat[:, sc0 + 3:sc0 + 4], stat[:, sc0 + 3:sc0 + 4],
                stat[:, sc0 + 2:sc0 + 3], ALU.subtract),
                waits=[(Cdve, m2m)], incs=[(Cdve, 1)])
            dm = Cdve.n
            p.add("scalar", lambda e: e.activation(
                stat[:, sc0 + 4:sc0 + 5], stat[:, sc0 + 3:sc0 + 4], AF.Ln,
                bias=cst_sb[:, 0:1]), waits=[(Cdve, dm)], incs=[(Cact, 1)])
            lnm = Cact.n
            p.add("scalar", lambda e: e.activation(
                stat[:, sc0 + 4:sc0 + 5], stat[:, sc0 + 4:sc0 + 5], AF.Exp, scale=-0.5),
                waits=[(Cact, lnm)], incs=[(Cact, 1)])
            return stat[:, sc0:sc0 + 1], stat[:, sc0 + 4:sc0 + 5], Cact.n

        p0_norm_marks = {}
        p0_store_marks = {}
        for i in range(16):
            sl = i % 2
            w = []
            if i >= 2:
                w.append((Cdve, p0_norm_marks[i - 2]))
            p.add("gpsimd", lambda e, i=i, sl=sl: e.dma_start(
                out=xt_sl[sl], in_=x_in[128 * i:128 * (i + 1), :]),
                waits=w, incs=[(CgpL, 16)])
            ld = CgpL.n
            nm, rs, am = emit_ln_stats(xt_sl[sl], 8 * sl, DD[:, 0:C], [(CgpL, ld)])
            w = [(Cact, am)]
            if i >= 2:
                w.append((Cst, p0_store_marks[i - 2]))
            p.add("vector", lambda e, sl=sl, nm=nm, rs=rs: e.tensor_scalar(
                nrm_sl[sl], xt_sl[sl], nm, rs, ALU.add, ALU.mult),
                waits=w, incs=[(Cdve, 1)])
            p0_norm_marks[i] = Cdve.n
            p.add("sync", lambda e, i=i, sl=sl: e.dma_start(
                out=xn1_d[128 * i:128 * (i + 1), :], in_=nrm_sl[sl]),
                waits=[(Cdve, Cdve.n)], incs=[(Cst, 16)])
            p0_store_marks[i] = Cst.n
        xn1_stored = Cst.n
        p0_dve_done = Cdve.n

        kq_evac_marks = {}
        rope_prev = {"t1": 0, "pl": 0, "st": 0}
        ve_store_marks = {}

        def emit_rope_finish(kind, h, half, evac_m, tab_m):
            rot_m = Cpe.n
            prev_t1 = rope_prev["t1"]
            prev_pl = rope_prev["pl"]
            prev_st = rope_prev["st"]
            p.add("vector", lambda e: e.tensor_tensor(
                t1_sb[:, :], psC[:, :], ssin_sb[:, :], ALU.mult),
                waits=[(Cpe, rot_m), (Cpool, prev_pl), (CsyL, tab_m)], incs=[(Cdve, 1)])
            rope_prev["t1"] = Cdve.n
            p.add("gpsimd", lambda e: e.tensor_tensor(
                k1_sb[:, :], k0_sb[:, :], cos_sb[:, :], ALU.mult),
                waits=[(Cact, evac_m), (Cst, prev_st), (CsyL, tab_m)],
                incs=[(Cpool, 1)])
            t0_m = Cpool.n
            p.add("gpsimd", lambda e: e.tensor_tensor(
                k1_sb[:, :], k1_sb[:, :], t1_sb[:, :], ALU.add),
                waits=[(Cdve, rope_prev["t1"]), (Cpool, t0_m)], incs=[(Cpool, 1)])
            rope_prev["pl"] = Cpool.n
            if kind == "k":
                p.add("sync", lambda e, h=h, half=half: e.dma_start(
                    out=kf_d[128 * h:128 * (h + 1), 1024 * half:1024 * (half + 1)],
                    in_=k1_sb[:, :]),
                    waits=[(Cpool, Cpool.n)], incs=[(Cst, 16)])
            else:
                p.add("sync", lambda e, h=h: e.dma_start(
                    out=qf_d[128 * h:128 * (h + 1), 0:OWN], in_=k1_sb[:, :]),
                    waits=[(Cpool, Cpool.n)], incs=[(Cst, 16)])
            rope_prev["st"] = Cst.n

        av_evac_mark = {"m": 0}
        ak_evac_mark = {"m": 0}
        bslot_pe = {0: 0, 1: 0}
        for half in range(2):
            p.add("sync", lambda e, half=half: e.dma_start(
                out=cos_sb[:, :], in_=cosf[:, 1024 * half:1024 * (half + 1)]),
                waits=[(Cpool, rope_prev["pl"])], incs=[(CsyL, 16)])
            p.add("sync", lambda e, half=half: e.dma_start(
                out=ssin_sb[:, :], in_=ssinf[:, 1024 * half:1024 * (half + 1)]),
                waits=[(Cdve, rope_prev["t1"])], incs=[(CsyL, 16)])
            ropetab_ready = CsyL.n
            if half == 0:
                xw = [(Cst, xn1_stored), (CsyL, consts_ready)]
            else:
                xw = [(Cst, xn1_stored), (Cpe, Cpe.n)]
            for cch in range(32):
                p.add("sync", lambda e, cch=cch, half=half: e.dma_start(
                    out=BIGA[:, cch, :],
                    in_=xn1_d[1024 * half:1024 * (half + 1), 128 * cch:128 * (cch + 1)],
                    transpose=True),
                    waits=(xw if cch == 0 else []), incs=[(CsyL, 16)])
            xnf_ready = CsyL.n

            ev_marks = {}
            for e8 in range(8):
                wsl = B0 if e8 % 2 == 0 else B1
                wv_w = [(CgpL, pf_ready), (Cpe, bslot_pe[e8 % 2])]
                p.add("gpsimd", lambda e, e8=e8, wsl=wsl: e.dma_start(
                    out=wsl[:, :, :],
                    in_=wvT[:, 512 * e8:512 * (e8 + 1)].rearrange("(c p) n -> p c n", p=128)),
                    waits=wv_w, incs=[(CgpL, 16)])
                wv_ld = CgpL.n
                for tc in range(8):
                    ps = psAB[tc % 2]
                    ii = e8 * 8 + tc
                    for cch in range(32):
                        w = []
                        if cch == 0:
                            w = [(CgpL, wv_ld), (CsyL, xnf_ready)]
                            prev = ev_marks.get(ii - 2)
                            if prev is not None:
                                w.append((Cact, prev))
                        p.add("tensor", lambda e, ps=ps, cch=cch, tc=tc, wsl=wsl: e.matmul(
                            ps[:, 0:512], BIGA[:, cch, 128 * tc:128 * (tc + 1)],
                            wsl[:, cch, :], start=(cch == 0), stop=(cch == 31),
                            skip_group_check=True),
                            waits=w, incs=([(Cpe, 1)] if cch == 31 else []))
                        if half == 0 and tc == 0:
                            p.add("tensor", lambda e, cch=cch, wsl=wsl: e.matmul(
                                psC[0:10, 0:512], pf_sb[:, cch, :], wsl[:, cch, :],
                                start=(cch == 0), stop=(cch == 31), skip_group_check=True),
                                waits=([(Cact, av_evac_mark["m"]), (CgpL, pf_ready)]
                                       if cch == 0 else []),
                                incs=([(Cpe, 1)] if cch == 31 else []))
                    mm_m = Cpe.n
                    w = [(Cpe, mm_m)]
                    prev = ve_store_marks.get(ii - 2 + 64 * half)
                    if prev is not None:
                        w.append((Cst, prev))
                    p.add("scalar", lambda e, tc=tc, ps=ps: e.copy(
                        ve_sb[:, tc % 2, :], ps[:, 0:512]),
                        waits=w, incs=[(Cact, 1)])
                    p.add("sync", lambda e, tc=tc, e8=e8, half=half: e.dma_start(
                        out=v_d[1024 * half + 128 * tc:1024 * half + 128 * (tc + 1),
                                512 * e8:512 * (e8 + 1)],
                        in_=ve_sb[:, tc % 2, :]),
                        waits=[(Cact, Cact.n)], incs=[(Cst, 16)])
                    ev_marks[ii] = Cact.n
                    ve_store_marks[ii + 64 * half] = Cst.n
                bslot_pe[e8 % 2] = Cpe.n
                if half == 0:
                    p.add("scalar", lambda e, e8=e8: e.copy(
                        av_sb[0:10, 512 * e8:512 * (e8 + 1)], psC[0:10, 0:512]),
                        waits=[(Cpe, Cpe.n)], incs=[(Cact, 1)])
                    av_evac_mark["m"] = Cact.n

            for kind in (("k", "q") if half == 0 else ("k",)):
                wT_src = wkT if kind == "k" else wqT
                wld_marks = {}

                def emit_kq_load(h):
                    wsl = (B0 if h % 2 == 0 else B1)[:, :, 0:128]
                    p.add("gpsimd", lambda e, h=h, wsl=wsl, wT_src=wT_src: e.dma_start(
                        out=wsl, in_=wT_src[:, 128 * h:128 * (h + 1)].rearrange(
                            "(c p) n -> p c n", p=128)),
                        waits=[(Cpe, bslot_pe[h % 2])], incs=[(CgpL, 16)])
                    wld_marks[h] = CgpL.n

                emit_kq_load(0)
                for h in range(32):
                    wsl = (B0 if h % 2 == 0 else B1)[:, :, 0:128]
                    wld = wld_marks[h]
                    ps = psAB[h % 2]
                    for cch in range(32):
                        w = []
                        if cch == 0:
                            w = [(CgpL, wld)]
                            prev = kq_evac_marks.get((kind, half, h - 2))
                            if prev is not None:
                                w.append((Cact, prev))
                        for sg in range(2):
                            p.add("tensor", lambda e, ps=ps, cch=cch, sg=sg, wsl=wsl: e.matmul(
                                ps[:, 512 * sg:512 * (sg + 1)], wsl[:, cch, :],
                                BIGA[:, cch, 512 * sg:512 * (sg + 1)],
                                start=(cch == 0), stop=(cch == 31), skip_group_check=True),
                                waits=(w if sg == 0 else []),
                                incs=([(Cpe, 1)] if (cch == 31 and sg == 1) else []))
                        if kind == "k" and half == 0:
                            p.add("tensor", lambda e, cch=cch, wsl=wsl: e.matmul(
                                psD[:, 0:10], wsl[:, cch, :], pf_sb[:, cch, :],
                                start=(cch == 0), stop=(cch == 31), skip_group_check=True),
                                waits=([(Cdve, ak_evac_mark["m"])] if cch == 0 else []),
                                incs=([(Cpe, 1)] if cch == 31 else []))
                    mm_m = Cpe.n
                    bslot_pe[h % 2] = mm_m
                    if h + 1 < 32:
                        emit_kq_load(h + 1)
                    p.add("scalar", lambda e, ps=ps: e.copy(k0_sb[:, :], ps[:, :]),
                          waits=[(Cpe, mm_m), (Cpool, rope_prev["pl"])], incs=[(Cact, 1)])
                    kq_evac_marks[(kind, half, h)] = Cact.n
                    evac_m = Cact.n
                    if kind == "k" and half == 0:
                        p.add("vector", lambda e, h=h: e.tensor_copy(
                            ak_sb[:, h, :], psD[:, 0:10]),
                            waits=[(Cpe, mm_m)], incs=[(Cdve, 1)])
                        ak_evac_mark["m"] = Cdve.n
                    p.add("tensor", lambda e: e.matmul(
                        psC[:, 0:512], p64_sb[:, :], k0_sb[:, 0:512],
                        start=True, stop=True, skip_group_check=True),
                        waits=[(Cact, evac_m), (Cdve, rope_prev["t1"])])
                    p.add("tensor", lambda e: e.matmul(
                        psC[:, 512:1024], p64_sb[:, :], k0_sb[:, 512:1024],
                        start=True, stop=True, skip_group_check=True),
                        incs=[(Cpe, 1)])
                    emit_rope_finish(kind, h, half, evac_m, ropetab_ready)
        if DEBUG_TAPS:
            p.add("sync", lambda e: e.dma_start(
                out=av_dbg[:, :], in_=av_sb[0:10, :]),
                waits=[(Cact, Cact.n)], incs=[(Cst, 16)])
            p.add("sync", lambda e: e.dma_start(
                out=ak_dbg[:, :], in_=ak_sb.ap().rearrange("p a b -> p (a b)")),
                waits=[(Cdve, Cdve.n)], incs=[(Cst, 16)])
        p1_stores = Cst.n
        p1_pe_done = Cpe.n
        p1_dve_done = Cdve.n

        head_load_marks = {}
        head_end_pe = {}
        head_free = {}
        dq_wn_store = {}

        def emit_head_loads(h):
            sl = h % 2
            w = [(Cst, p1_stores)] if h < 2 else [(Cpe, head_end_pe[h - 2])]
            p.add("sync", lambda e, h=h, sl=sl: e.dma_start(
                out=kf2[sl], in_=kf_d[128 * h:128 * (h + 1), :]),
                waits=w, incs=[(CsyL, 16)])
            p.add("sync", lambda e, h=h, sl=sl: e.dma_start(
                out=qf2[sl], in_=qf_d[128 * h:128 * (h + 1), :]), incs=[(CsyL, 16)])
            p.add("sync", lambda e, h=h, sl=sl: e.dma_start(
                out=v2[sl], in_=v_d[:, 128 * h:128 * (h + 1)].rearrange(
                    "(kc tp) d -> tp kc d", tp=128)), incs=[(CsyL, 16)])
            head_load_marks[h] = CsyL.n

        emit_head_loads(0)
        emit_head_loads(1)
        prev_srow_m = {"m": 0}
        prev_pa_m = {"m": 0}

        for h in range(H):
            sl = h % 2
            if h == 0:
                p.add("gpsimd", lambda e: e.dma_start(
                    out=B0[:, :, :],
                    in_=wpT[:, 0:512].rearrange("(c p) n -> p c n", p=128)),
                    waits=[(Cpe, p1_pe_done)], incs=[(CgpL, 16)])
                wp0_ld = CgpL.n

            exp_marks = {}
            pv_marks = {}

            def bank_segs(qlo):
                return [(qlo, 512), (512, 1024)] if qlo < 512 else [(qlo, 1024)]

            def emit_sum_pv(cp):
                cqp = cp % 8
                qlop = 128 * cqp
                slp = cp % 2
                sgl = bank_segs(qlop)
                for sg, (s0, s1) in enumerate(sgl):
                    w = []
                    if sg == 0:
                        w = [(Cact, exp_marks[cp])]
                        if cp == 0:
                            w.append((Cact, sarow_evac_m))
                        if cp == 0 and h == 0:
                            w.append((Cdve, p1_dve_done))
                    p.add("tensor", lambda e, s0=s0, s1=s1, slp=slp, cp=cp: e.matmul(
                        psD[0:1, s0:s1], ones128[:, :], exps[slp][:, s0:s1],
                        start=(cp == 0), stop=(cp == 15), skip_group_check=True),
                        waits=w,
                        incs=([(Cpe, 1)] if (cp == 15 and sg == len(sgl) - 1) else []))
                for sg, (s0, s1) in enumerate(sgl):
                    w = []
                    if cp == 0 and sg == 0 and h >= 1:
                        w = [(Cdve, head_free[h - 1][2])]
                    p.add("tensor", lambda e, s0=s0, s1=s1, slp=slp, cp=cp, sl=sl: e.matmul(
                        psC[:, s0:s1], v2[sl][:, cp, :], exps[slp][:, s0:s1],
                        start=(cp == 0), stop=(cp == 15), skip_group_check=True),
                        waits=w,
                        incs=([(Cpe, 1)] if sg == len(sgl) - 1 else []))
                pv_marks[cp] = Cpe.n

            w = [(Cact, prev_srow_m["m"]), (Cdve, prev_pa_m["m"]),
                 (CsyL, head_load_marks[h])]
            if h == 0:
                w.append((Cdve, p1_dve_done))
            for sg in range(2):
                p.add("tensor", lambda e, sg=sg, sl=sl, h=h: e.matmul(
                    psD[0:10, 512 * sg:512 * (sg + 1)], ak_sb[:, h, :],
                    qf2[sl][:, 512 * sg:512 * (sg + 1)], start=True, stop=True,
                    skip_group_check=True),
                    waits=(w if sg == 0 else []),
                    incs=([(Cpe, 1)] if sg == 1 else []))
            asc_m = Cpe.n
            p.add("scalar", lambda e: e.activation(
                expa[:, :], psD[0:10, :], AF.Exp, scale=1.0 / 64.0),
                waits=[(Cpe, asc_m)], incs=[(Cact, 1)])
            expa_m = Cact.n
            for sg in range(2):
                p.add("tensor", lambda e, sg=sg: e.matmul(
                    psD[0:1, 512 * sg:512 * (sg + 1)], ones10[:, :],
                    expa[:, 512 * sg:512 * (sg + 1)], start=True, stop=True,
                    skip_group_check=True),
                    waits=([(Cact, expa_m)] if sg == 0 else []),
                    incs=([(Cpe, 1)] if sg == 1 else []))
            asum_m = Cpe.n
            p.add("scalar", lambda e: e.copy(sarow[:, :], psD[0:1, :]),
                  waits=[(Cpe, asum_m)], incs=[(Cact, 1)])
            sae_m = Cact.n
            sarow_evac_m = Cact.n

            for c in range(16):
                cq = c % 8
                qlo = 128 * cq
                width = 1024 - qlo
                slot = c % 2
                pss = psAB[slot]
                w = []
                if c == 0:
                    w = [(CsyL, head_load_marks[h])]
                    if h == 0:
                        w.append((Cdve, p1_dve_done))
                    else:
                        w.append((Cdve, head_free[h - 1][0]))
                elif c == 1 and h >= 1:
                    w = [(Cact, head_free[h - 1][1])]
                elif c >= 2:
                    w = [(Cact, exp_marks[c - 2])]
                sgl = bank_segs(qlo)
                for sg, (s0, s1) in enumerate(sgl):
                    p.add("tensor", lambda e, pss=pss, c=c, s0=s0, s1=s1, sl=sl: e.matmul(
                        pss[:, s0:s1], kf2[sl][:, 128 * c:128 * (c + 1)],
                        qf2[sl][:, s0:s1], start=True, stop=True, skip_group_check=True),
                        waits=(w if sg == 0 else []),
                        incs=([(Cpe, 1)] if sg == len(sgl) - 1 else []))
                sc_m = Cpe.n
                msk = dm_sb if c < 8 else xm_sb
                p.add("vector", lambda e, pss=pss, qlo=qlo, msk=msk: e.tensor_tensor(
                    pss[:, qlo:qlo + 128], pss[:, qlo:qlo + 128], msk[:, :], ALU.add),
                    waits=[(Cpe, sc_m)], incs=[(Cdve, 1)])
                mk_m = Cdve.n
                ew = [(Cdve, mk_m)]
                if c >= 2:
                    ew.append((Cpe, pv_marks[c - 2]))
                p.add("scalar", lambda e, pss=pss, qlo=qlo, width=width, slot=slot: e.activation(
                    exps[slot][:, qlo:qlo + width], pss[:, qlo:qlo + width], AF.Exp,
                    scale=RS128, bias=cst_sb[:, 1:2]),
                    waits=ew, incs=[(Cact, 1)])
                exp_marks[c] = Cact.n
                if c == 2:
                    p.add("scalar", lambda e: e.activation(
                        sarow[:, :], sarow[:, :], AF.Ln),
                        waits=[(Cact, sae_m)], incs=[(Cact, 1)])
                    saln_m = Cact.n
                    p.add("scalar", lambda e: e.activation(
                        sarow[:, :], sarow[:, :], AF.Exp, scale=-1.0),
                        waits=[(Cact, saln_m)], incs=[(Cact, 1)])
                    sa_m = Cact.n
                    p.add("gpsimd", lambda e: e.tensor_scalar(
                        sarow[:, :], sarow[:, :], g_sb[:, :], None, ALU.mult),
                        waits=[(Cact, sa_m)], incs=[(Cpool, 1)])
                    f_m = Cpool.n
                if c >= 1:
                    emit_sum_pv(c - 1)
            emit_sum_pv(15)
            pv_done = Cpe.n
            sum15_m = pv_marks[15] - 1

            p.add("scalar", lambda e: e.copy(srow[:, :], psD[0:1, :]),
                  waits=[(Cpe, pv_done)], incs=[(Cact, 1)])
            srow_m = Cact.n
            prev_srow_m["m"] = srow_m
            for sg in range(2):
                p.add("tensor", lambda e, sg=sg: e.matmul(
                    psB[0:10, 512 * sg:512 * (sg + 1)], o110[:, :],
                    sarow[:, 512 * sg:512 * (sg + 1)], start=True, stop=True,
                    skip_group_check=True),
                    waits=([(Cact, exp_marks[15]), (Cpool, f_m)] if sg == 0 else []),
                    incs=([(Cpe, 1)] if sg == 1 else []))
            fb_m = Cpe.n
            p.add("vector", lambda e: e.tensor_tensor(
                pa_sb[:, :], expa[:, :], psB[0:10, :], ALU.mult),
                waits=[(Cpe, fb_m)], incs=[(Cdve, 1)])
            pa_m = Cdve.n
            prev_pa_m["m"] = pa_m
            for sg in range(2):
                p.add("tensor", lambda e, sg=sg, h=h: e.matmul(
                    psA[:, 512 * sg:512 * (sg + 1)], av_sb[0:10, 128 * h:128 * (h + 1)],
                    pa_sb[:, 512 * sg:512 * (sg + 1)], start=True, stop=True,
                    skip_group_check=True),
                    waits=([(Cdve, pa_m), (Cact, exp_marks[14])] if sg == 0 else []),
                    incs=([(Cpe, 1)] if sg == 1 else []))
            av_m = Cpe.n
            p.add("scalar", lambda e: e.activation(srow[:, :], srow[:, :], AF.Ln),
                  incs=[(Cact, 1)])
            srln_m = Cact.n
            p.add("scalar", lambda e: e.activation(invrow[:, :], srow[:, :], AF.Exp,
                                                   scale=-1.0),
                  waits=[(Cact, srln_m)], incs=[(Cact, 1)])
            inv_m = Cact.n
            for sg in range(2):
                p.add("tensor", lambda e, sg=sg: e.matmul(
                    psB[:, 512 * sg:512 * (sg + 1)], o1128[:, :],
                    invrow[:, 512 * sg:512 * (sg + 1)], start=True, stop=True,
                    skip_group_check=True),
                    waits=([(Cact, inv_m), (Cdve, pa_m)] if sg == 0 else []),
                    incs=([(Cpe, 1)] if sg == 1 else []))
            invb_pe = Cpe.n
            p.add("scalar", lambda e: e.copy(invb[:, :], psB[:, :]),
                  waits=[(Cpe, invb_pe)], incs=[(Cact, 1)])
            invb_m = Cact.n
            p.add("vector", lambda e, h=h: e.tensor_tensor(
                BIGA[:, h, :], psC[:, :], invb[:, :], ALU.mult),
                waits=[(Cact, invb_m), (Cpe, pv_done)], incs=[(Cdve, 1)])
            y1_m = Cdve.n
            p.add("vector", lambda e, h=h: e.tensor_tensor(
                BIGA[:, h, :], BIGA[:, h, :], psA[:, :], ALU.add),
                waits=[(Cpe, av_m), (Cdve, y1_m)], incs=[(Cdve, 1)])
            y2_m = Cdve.n
            if DEBUG_TAPS:
                p.add("sync", lambda e, h=h: e.dma_start(
                    out=y_dbg[128 * h:128 * (h + 1), :], in_=BIGA[:, h, :]),
                    waits=[(Cdve, y2_m)], incs=[(Cst, 16)])
                p.add("sync", lambda e, h=h: e.dma_start(
                    out=sums_dbg[h:h + 1, 0:1024], in_=srow[:, :]),
                    waits=[(Cact, inv_m)], incs=[(Cst, 16)])
                p.add("gpsimd", lambda e, h=h: e.dma_start(
                    out=sums_dbg[h:h + 1, 1024:2048], in_=sarow[:, :]),
                    waits=[(Cdve, f_m)], incs=[(CgpL, 16)])
            head_free[h] = (y2_m, invb_m, y1_m)
            head_end_pe[h] = Cpe.n
            if h + 2 < H:
                emit_head_loads(h + 2)

            oc = h
            dsl = oc % 2
            w = [(CgpL, sczr_ready)]
            if oc < 2:
                w.append((Cst, xn1_stored))
                w.append((Cdve, p0_dve_done))
            prev = dq_wn_store.get(oc - 2)
            if prev is not None:
                w.append((Cst, prev))
            p.add("gpsimd", lambda e, oc=oc, dsl=dsl: e.dma_start(
                out=qw_sl[dsl], in_=qw[128 * oc:128 * (oc + 1), :]),
                waits=w, incs=[(CgpL, 16)])
            qw_ld = CgpL.n
            for g in range(32):
                eng = "vector" if g % 2 == 0 else "gpsimd"
                p.add(eng, lambda e, g=g, oc=oc, dsl=dsl: e.tensor_scalar(
                    wn_sl[dsl][:, 128 * g:128 * (g + 1)],
                    qw_sl[dsl][:, 128 * g:128 * (g + 1)],
                    sc_sb[:, 32 * oc + g:32 * oc + g + 1],
                    zr_sb[:, 32 * oc + g:32 * oc + g + 1],
                    ALU.mult, ALU.add),
                    waits=([(CgpL, qw_ld), (CgpL, sczr_ready)] if g in (0, 1) else []),
                    incs=([(Cdve, 1)] if g == 30 else ([(Cpool, 1)] if g == 31 else [])))
            dq_d, dq_p = Cdve.n, Cpool.n
            p.add("sync", lambda e, oc=oc, dsl=dsl: e.dma_start(
                out=wnat_d[128 * oc:128 * (oc + 1), :], in_=wn_sl[dsl]),
                waits=[(Cdve, dq_d), (Cpool, dq_p)], incs=[(Cst, 16)])
            dq_wn_store[oc] = Cst.n
        wnat_all = Cst.n
        p2_done_pe = Cpe.n
        p2_done_act = Cact.n
        p2_done_dve = Cdve.n

        proj_mm_marks = {}
        ape_store_marks = {}
        ev_prev = {}
        for e8 in range(8):
            wsl = B0 if e8 % 2 == 0 else B1
            if e8 > 0:
                w = ([(Cpe, p2_done_pe), (Cact, p2_done_act), (Cdve, p2_done_dve)]
                     if e8 == 1 else [(Cpe, proj_mm_marks[e8 - 2])])
                p.add("gpsimd", lambda e, e8=e8, wsl=wsl: e.dma_start(
                    out=wsl[:, :, :],
                    in_=wpT[:, 512 * e8:512 * (e8 + 1)].rearrange("(c p) n -> p c n", p=128)),
                    waits=w, incs=[(CgpL, 16)])
                wld = CgpL.n
            else:
                wld = wp0_ld
            for tc in range(8):
                ps = psAB[tc % 2]
                ii = e8 * 8 + tc
                for cch in range(32):
                    w = []
                    if cch == 0:
                        w = [(CgpL, wld)]
                        if e8 == 0 and tc == 0:
                            w.append((Cdve, p2_done_dve))
                        prev = ev_prev.get(ii - 2)
                        if prev is not None:
                            w.append((Cact, prev))
                    p.add("tensor", lambda e, ps=ps, cch=cch, tc=tc, wsl=wsl: e.matmul(
                        ps[:, 0:512], BIGA[:, cch, 128 * tc:128 * (tc + 1)],
                        wsl[:, cch, :], start=(cch == 0), stop=(cch == 31),
                        skip_group_check=True),
                        waits=w, incs=([(Cpe, 1)] if cch == 31 else []))
                mm_m = Cpe.n
                w = [(Cpe, mm_m)]
                prev = ape_store_marks.get(ii - 2)
                if prev is not None:
                    w.append((Cst, prev))
                p.add("scalar", lambda e, tc=tc, ps=ps: e.copy(
                    ape_sb[:, tc % 2, :], ps[:, 0:512]),
                    waits=w, incs=[(Cact, 1)])
                ev_prev[ii] = Cact.n
                p.add("sync", lambda e, tc=tc, e8=e8: e.dma_start(
                    out=attn_d[128 * tc:128 * (tc + 1), 512 * e8:512 * (e8 + 1)],
                    in_=ape_sb[:, tc % 2, :]),
                    waits=[(Cact, Cact.n)], incs=[(Cst, 16)])
                ape_store_marks[ii] = Cst.n
            proj_mm_marks[e8] = Cpe.n
        attn_stored = Cst.n

        p4_marks = []
        p4_x2m = {}
        for tc in range(8):
            w = ([(Cst, ape_store_marks[56]), (Cst, wnat_all)] if tc == 0
                 else [(Cst, p4_marks[tc - 1]), (Cst, ape_store_marks[56 + tc])])
            p.add("sync", lambda e, tc=tc: e.dma_start(
                out=at_ap, in_=attn_d[128 * tc:128 * (tc + 1), :]),
                waits=w, incs=[(CsyL, 16)])
            wg = [(Cst, wnat_all)] if tc == 0 else [(Cdve, p4_x2m[tc - 1])]
            p.add("gpsimd", lambda e, tc=tc: e.dma_start(
                out=xr_ap, in_=x_in[128 * tc:128 * (tc + 1), :]),
                waits=wg, incs=[(CgpL, 16)])
            ld_s, ld_g = CsyL.n, CgpL.n
            nm, rs, am = emit_ln_stats(at_ap, 0, p4dum, [(CsyL, ld_s), (CgpL, ld_g)])
            p.add("vector", lambda e, nm=nm, rs=rs: e.tensor_scalar(
                at_ap, at_ap, nm, rs, ALU.add, ALU.mult),
                waits=[(Cact, am)], incs=[(Cdve, 1)])
            u_m = Cdve.n
            p.add("vector", lambda e: e.tensor_tensor(at_ap, at_ap, xr_ap, ALU.add),
                  waits=[(Cdve, u_m)], incs=[(Cdve, 1)])
            x2_m = Cdve.n
            p4_x2m[tc] = x2_m
            p.add("sync", lambda e, tc=tc: e.dma_start(
                out=x2_d[128 * tc:128 * (tc + 1), :], in_=at_ap),
                waits=[(Cdve, x2_m)], incs=[(Cst, 16)])
            nm2, rs2, am2 = emit_ln_stats(at_ap, 0, p4dum, [(Cdve, x2_m)])
            p.add("vector", lambda e, nm2=nm2, rs2=rs2: e.tensor_scalar(
                DD[:, 0:C], at_ap, nm2, rs2, ALU.add, ALU.mult),
                waits=[(Cact, am2)], incs=[(Cdve, 1)])
            p.add("sync", lambda e, tc=tc: e.dma_start(
                out=xn2_d[128 * tc:128 * (tc + 1), :], in_=DD[:, 0:C]),
                waits=[(Cdve, Cdve.n)], incs=[(Cst, 16)])
            p4_marks.append(Cst.n)
        p4_stored = Cst.n

        for cch in range(32):
            w = [(Cst, p4_stored), (Cpe, proj_mm_marks[7])] if cch == 0 else []
            p.add("sync", lambda e, cch=cch: e.dma_start(
                out=BIGA[:, cch, 0:OWN],
                in_=xn2_d[:, 128 * cch:128 * (cch + 1)], transpose=True),
                waits=w, incs=[(CsyL, 16)])
        xn2f_ready = CsyL.n
        p.add("gpsimd", lambda e: e.dma_start(out=DD[0:1, 0:C], in_=brow[:, :]),
              waits=[(Cst, p4_stored)], incs=[(CgpL, 16)])
        brow_ld = CgpL.n
        ffn_mm_marks = {}
        ffe_store_marks = {}
        ev_prev = {}
        for e8 in range(8):
            wsl = B0 if e8 % 2 == 0 else B1
            w = ([(Cpe, proj_mm_marks[7]), (Cst, wnat_all)] if e8 < 2
                 else [(Cpe, ffn_mm_marks[e8 - 2]), (Cst, wnat_all)])
            for ic in range(32):
                p.add("sync", lambda e, e8=e8, ic=ic, wsl=wsl: e.dma_start(
                    out=wsl[:, ic, :],
                    in_=wnat_d[512 * e8:512 * (e8 + 1), 128 * ic:128 * (ic + 1)],
                    transpose=True),
                    waits=(w if ic == 0 else []), incs=[(CsyL, 16)])
            wld = CsyL.n
            for tc in range(8):
                ps = psAB[tc % 2]
                ii = e8 * 8 + tc
                w = [(CsyL, wld), (CsyL, xn2f_ready), (CgpL, brow_ld)]
                prev = ev_prev.get(ii - 2)
                if prev is not None:
                    w.append((Cdve, prev))
                p.add("tensor", lambda e, ps=ps, e8=e8: e.matmul(
                    ps[:, 0:512], o1128[:, :], DD[0:1, 512 * e8:512 * (e8 + 1)],
                    start=True, stop=False, skip_group_check=True), waits=w)
                for ic in range(32):
                    p.add("tensor", lambda e, ps=ps, ic=ic, tc=tc, wsl=wsl: e.matmul(
                        ps[:, 0:512], BIGA[:, ic, 128 * tc:128 * (tc + 1)],
                        wsl[:, ic, :], start=False, stop=(ic == 31),
                        skip_group_check=True),
                        incs=([(Cpe, 1)] if ic == 31 else []))
                mm_m = Cpe.n
                w = [(Cpe, mm_m)]
                prev = ffe_store_marks.get(ii - 2)
                if prev is not None:
                    w.append((Cst, prev))
                p.add("vector", lambda e, tc=tc, ps=ps: e.tensor_copy(
                    ape_sb[:, tc % 2, :], ps[:, 0:512]),
                    waits=w, incs=[(Cdve, 1)])
                ev_prev[ii] = Cdve.n
                p.add("scalar", lambda e, tc=tc, e8=e8: e.dma_start(
                    out=ffn_d[128 * tc:128 * (tc + 1), 512 * e8:512 * (e8 + 1)],
                    in_=ape_sb[:, tc % 2, :]),
                    waits=[(Cdve, Cdve.n)], incs=[(Cst, 16)])
                ffe_store_marks[ii] = Cst.n
            ffn_mm_marks[e8] = Cpe.n
        ffn_stored = Cst.n

        p6_marks = []
        for tc in range(8):
            w = ([(Cst, ffe_store_marks[56])] if tc == 0
                 else [(Cout, p6_marks[tc - 1]), (Cst, ffe_store_marks[56 + tc])])
            p.add("sync", lambda e, tc=tc: e.dma_start(
                out=ft_ap, in_=ffn_d[128 * tc:128 * (tc + 1), :]),
                waits=w, incs=[(CsyL, 16)])
            p.add("sync", lambda e, tc=tc: e.dma_start(
                out=x2r_ap, in_=x2_d[128 * tc:128 * (tc + 1), :]), incs=[(CsyL, 16)])
            ld = CsyL.n
            nm, rs, am = emit_ln_stats(ft_ap, 0, DD[:, 0:C], [(CsyL, ld)])
            p.add("vector", lambda e, nm=nm, rs=rs: e.tensor_scalar(
                ft_ap, ft_ap, nm, rs, ALU.add, ALU.mult),
                waits=[(Cact, am)], incs=[(Cdve, 1)])
            u6_m = Cdve.n
            p.add("vector", lambda e: e.tensor_tensor(ft_ap, ft_ap, x2r_ap, ALU.add),
                  waits=[(Cdve, u6_m)], incs=[(Cdve, 1)])
            p.add("sync", lambda e, tc=tc: e.dma_start(
                out=out[128 * tc:128 * (tc + 1), :], in_=ft_ap),
                waits=[(Cdve, Cdve.n)], incs=[(Cout, 16)])
            p6_marks.append(Cout.n)
        p.add("sync", lambda e: e.nop(), waits=[(Cout, Cout.n)])

        p.emit(block)
    return nc


def _host_prep(L0_attn_x, L0_attn_w_attn, L0_attn_w_proj, L0_attn_adapter_wte,
               L0_attn_gating, L0_ffn_qweight, L0_ffn_scales, L0_ffn_scaled_zeros,
               L0_ffn_bias):
    x = np.asarray(L0_attn_x, dtype=np.float32)
    wa = np.asarray(L0_attn_w_attn, dtype=np.float32)
    wp = np.asarray(L0_attn_w_proj, dtype=np.float32)
    aw = np.asarray(L0_attn_adapter_wte, dtype=np.float32)
    g = np.asarray(L0_attn_gating, dtype=np.float32)
    qwv = np.ascontiguousarray(np.asarray(L0_ffn_qweight, dtype=np.int32).reshape(C, C))
    sc = np.asarray(L0_ffn_scales, dtype=np.float32).reshape(C, 32)
    zr = np.asarray(L0_ffn_scaled_zeros, dtype=np.float32).reshape(C, 32)
    bi = np.ascontiguousarray(np.asarray(L0_ffn_bias, dtype=np.float32).reshape(1, C))

    wqT = np.ascontiguousarray(wa[0:C].T)
    wkT = np.ascontiguousarray(wa[C:2 * C].T)
    wvT = np.ascontiguousarray(wa[2 * C:3 * C].T)
    wpT = np.ascontiguousarray(wp.T)
    awT = np.ascontiguousarray(aw.T)
    scs = np.ascontiguousarray(sc.reshape(32, 128, 32).transpose(1, 0, 2).reshape(128, 1024))
    zrs = np.ascontiguousarray(zr.reshape(32, 128, 32).transpose(1, 0, 2).reshape(128, 1024))

    theta = 1.0 / (10000.0 ** (np.arange(0, HD, 2, dtype=np.float64) / HD))
    idx = np.outer(np.arange(T, dtype=np.float64), theta)
    idx2 = np.concatenate([idx, idx], axis=-1)
    cos_t = np.cos(idx2).astype(np.float32)
    sin_t = np.sin(idx2).astype(np.float32)
    sgn = np.where(np.arange(HD) < HD // 2, -1.0, 1.0).astype(np.float32)
    ssin_t = sin_t * sgn[None, :]
    p64m = np.zeros((HD, HD), dtype=np.float32)
    for d in range(HD):
        p64m[(d + 64) % 128, d] = 1.0
    dmask = np.where(np.arange(128)[:, None] <= np.arange(128)[None, :], 0.0,
                     NEG).astype(np.float32)

    perms = []
    for s in range(2):
        order = list(range(s, NB, 2)) + list(range(1 - s, NB, 2))
        perm = np.concatenate([np.arange(128 * gi, 128 * gi + 128) for gi in order])
        perms.append(perm)

    in_maps = []
    for c in range(8):
        b, s = c // 2, c % 2
        perm = perms[s]
        in_maps.append({
            "x": np.ascontiguousarray(x[b][perm]),
            "wqT": wqT, "wkT": wkT, "wvT": wvT, "wpT": wpT, "awT": awT,
            "gat": np.ascontiguousarray(g.reshape(1, 1)),
            "qw": qwv, "scs": scs, "zrs": zrs, "brow": bi,
            "cosf": np.ascontiguousarray(cos_t[perm].T).astype(BF),
            "ssinf": np.ascontiguousarray(ssin_t[perm].T).astype(BF),
            "p64": p64m.astype(BF),
            "dmask": dmask.astype(BF),
            "xmask": np.full((128, 128), NEG if s == 0 else 0.0, dtype=np.float32).astype(BF),
        })
    return in_maps, perms


def kernel(**inputs):
    in_maps, perms = _host_prep(**inputs)
    nc = build()
    res = run_bass_kernel_spmd(nc, in_maps, core_ids=list(range(8)))
    outf = np.empty((B, T, C), dtype=np.float32)
    for c in range(8):
        b, s = c // 2, c % 2
        outf[b][perms[s][0:OWN]] = np.asarray(res.results[c]["out"], dtype=np.float32)
    return outf
